# version 32
# baseline (speedup 1.0000x reference)
from contextlib import ExitStack
import numpy as np
import concourse.bass as bass
import concourse.mybir as mybir
from concourse.bass_utils import run_bass_kernel_spmd

F32 = mybir.dt.float32
BF16 = mybir.dt.bfloat16
I16 = mybir.dt.int16
U16 = mybir.dt.uint16
ALU = mybir.AluOpType
AF = mybir.ActivationFunctionType
AX = mybir.AxisListType

MAGIC = float(np.float32(1.5 * 2.0 ** 23))
D = 1024
DC = 8
NP = 8
I3 = 3072
IC = 24
PACK = 176
EPS = 1e-5
CPAD = 2
CW = 2048 + 2 * CPAD
COLW = 336

ENGINES = ("sync", "gpsimd", "vector", "scalar", "tensor")


class Sched:
    def __init__(self, nc):
        self.nc = nc
        self.ops = {e: [] for e in ENGINES}
        self.counts = {}
        self.events = {}
        self.sems = {}
        self.auto_counts = {e: 0 for e in ENGINES}

    def edge(self, engname, e, inst):
        self.auto_counts[engname] += 1
        n = self.auto_counts[engname]
        sem = self.sems["a_" + engname]
        inst.then_inc(sem, 1)
        e.wait_ge(sem, n)

    def op(self, eng, fn, waits=(), sets=None, sem=None, amt=1):
        wl = [w for w in waits if w is not None]
        inc = None
        if sets is not None:
            assert sets not in self.events, f"duplicate event {sets}"
            if sem is None:
                sem = "c_" + eng
                amt = 1
            v = self.counts.get(sem, 0) + amt
            self.counts[sem] = v
            self.events[sets] = (sem, v)
            inc = (sem, amt)
        self.ops[eng].append((wl, fn, inc))
        return sets

    def dma(self, eng, fn, waits=(), sets=None, sem=None):
        assert sem is not None
        return self.op(eng, fn, waits, sets, sem=sem, amt=16)

    def emit(self, stack):
        names = set(self.counts.keys())
        for e in ENGINES:
            names.add("a_" + e)
        for (sn, _v) in self.events.values():
            names.add(sn)
        for n in sorted(names):
            self.sems[n] = stack.enter_context(self.nc.semaphore(n))
        block = stack.enter_context(self.nc.Block())
        sched = self

        def mk(engname):
            own = "c_" + engname
            engops = sched.ops[engname]

            def body(eng):
                for wl, fn, inc in engops:
                    for w in wl:
                        s, v = sched.events[w]
                        eng.wait_ge(sched.sems[s], v)
                    inst = fn(eng)
                    if inc is not None:
                        if inst is None:
                            inst = eng.nop()
                        inst.then_inc(sched.sems[inc[0]], inc[1])
            return body

        block.sync(mk("sync"))
        block.gpsimd(mk("gpsimd"))
        block.vector(mk("vector"))
        block.scalar(mk("scalar"))
        block.tensor(mk("tensor"))


def build_nc(S):
    NT1 = S // 128
    T512 = S // 512
    CW = S + 2 * CPAD
    nc = bass.Bass("TRN2", target_bir_lowering=False)

    x_d = nc.dram_tensor("x", [S, D], F32, kind="ExternalInput")
    wq_d = nc.dram_tensor("wq", [D, D], F32, kind="ExternalInput")
    wk_d = nc.dram_tensor("wk", [D, D], F32, kind="ExternalInput")
    wv_d = nc.dram_tensor("wv", [D, D], F32, kind="ExternalInput")
    wp_d = nc.dram_tensor("wp", [D, D], F32, kind="ExternalInput")
    bp_d = nc.dram_tensor("bp", [D], F32, kind="ExternalInput")
    cw_d = nc.dram_tensor("cw", [D, 3], F32, kind="ExternalInput")
    cb_d = nc.dram_tensor("cb", [D], F32, kind="ExternalInput")
    ws_d = nc.dram_tensor("ws", [9 * 128, D], F32, kind="ExternalInput")
    out_d = nc.dram_tensor("out", [S, D], F32, kind="ExternalOutput")

    u_sp = nc.dram_tensor("u_sp", [IC * 128, S], F32)
    g_sp = nc.dram_tensor("g_sp", [IC * 128, S], F32)
    y_sp = nc.dram_tensor("y_sp", [S, D], F32)
    iup_sp = nc.dram_tensor("iup_sp", [IC * 128, S], I16)
    ig2_sp = nc.dram_tensor("ig2_sp", [IC * 128, S], I16)
    sq_sp = nc.dram_tensor("sq_sp", [IC * 128, S], I16)
    ks_d = nc.dram_tensor("ks_d", [D], F32)
    mx_d = nc.dram_tensor("mx_d", [128], F32)
    agin = nc.dram_tensor("agin", [9 * 128, D], BF16)
    agout = nc.dram_tensor("agout", [72 * 128, D], BF16)
    cc_in = nc.dram_tensor("cc_in", [1, 8], F32)
    cc_out = nc.dram_tensor("cc_out", [1, 8], F32)

    stack = ExitStack()
    E = stack.enter_context

    BW = max(NT1 * 1024, IC * 512)
    B1 = E(nc.sbuf_tensor("B1", [128, BW], BF16))
    B2 = E(nc.sbuf_tensor("B2", [128, BW], BF16))
    B3 = E(nc.sbuf_tensor("B3", [128, max(DC * CW, IC * 512)], BF16))
    B4 = E(nc.sbuf_tensor("B4", [128, 16384], BF16))
    WN = E(nc.sbuf_tensor("WN", [128, 2048], BF16))
    WQB = E(nc.sbuf_tensor("WQB", [128, 2048], BF16))
    F1 = E(nc.sbuf_tensor("F1", [128, 10240], F32))
    I16R = E(nc.sbuf_tensor("I16R", [128, 8192], I16))
    WSTG = I16R.bitcast(BF16)[:, 4096:6144]
    COLS = E(nc.sbuf_tensor("COLS", [128, COLW], F32))
    ROWB = E(nc.sbuf_tensor("ROWB", [1, 128], F32))
    CCR = E(nc.sbuf_tensor("CCR", [1, 8], F32))
    BPR = E(nc.sbuf_tensor("BPR", [1, 1024], F32))
    BPH = E(nc.sbuf_tensor("BPH", [1, 1024], BF16))
    BPL = E(nc.sbuf_tensor("BPL", [1, 1024], BF16))
    ONE = E(nc.sbuf_tensor("ONE", [1, 128], BF16))
    ONC = E(nc.sbuf_tensor("ONC", [128, 1], BF16))
    IDB = E(nc.sbuf_tensor("IDB", [128, 128], BF16))
    CWT = E(nc.sbuf_tensor("CWT", [128, 24], F32))
    CBT = E(nc.sbuf_tensor("CBT", [128, 8], F32))
    KSF = E(nc.sbuf_tensor("KSF", [128, 8], F32))

    KVM = E(nc.sbuf_tensor("KVM", [128, NP * 130], BF16))
    PS_A = E(nc.psum_tensor("PS_A", [128, 2048], F32))
    PS_B = E(nc.psum_tensor("PS_B", [128, 2048], F32))

    sc = Sched(nc)

    CL = {}
    _next = [0]

    def col(name, n=1):
        if name not in CL:
            CL[name] = (_next[0], n)
            _next[0] += n
        o, n2 = CL[name]
        assert n2 >= n
        return COLS[:, o:o + n]

    def F(a, b):
        return F1[:, a:b]

    for j in range(4):
        for nm in ("sr", "sq", "m", "e2", "mm", "v", "ln", "r0", "t1", "rstd", "nm"):
            col(f"{nm}_{j}")
    col("rd", 16)
    col("blk", 32)
    col("sblk", 32)
    col("rblk", 32)
    col("ymxs", NT1)
    col("umx", IC)
    col("gmx", IC)
    col("tmx", IC)
    col("omx", IC)
    for nm in ("ymxa", "umxa", "gmxa", "tmxa", "omxa", "g0_0", "gA_0", "gG_0",
               "gB_0", "gC_0", "sx", "rsx", "su", "fu", "sg", "fg", "sg16",
               "st2", "ft", "so", "fo", "pso", "tmpc", "cjunk"):
        col(nm)

    def _bv(ps, c):
        return ps[:, c:c + 128].bitcast(BF16)[:, 0:128]
    TRP8 = [_bv(PS_A, 0), _bv(PS_A, 512), _bv(PS_A, 1024), _bv(PS_A, 1536),
            _bv(PS_B, 0), _bv(PS_B, 512), _bv(PS_B, 1024), _bv(PS_B, 1536)]
    TRP6 = [_bv(PS_B, 1536), _bv(PS_A, 0), _bv(PS_A, 512),
            _bv(PS_B, 0), _bv(PS_B, 512), _bv(PS_B, 1024)]

    last_ccr = [None]

    def p0():
        sc.dma("sync", lambda e: e.dma_start(out=BPR[:, :], in_=bp_d[None, :]),
               sets="ld_bp", sem="d_cst0")
        cw_src = bass.AP(cw_d, 0, [[3, 128], [384, 8], [1, 3]])
        sc.dma("sync", lambda e: e.dma_start(out=CWT[:, :], in_=cw_src),
               sets="ld_cw", sem="d_cst1")
        cb_src = bass.AP(cb_d, 0, [[1, 128], [128, 8]])
        sc.dma("sync", lambda e: e.dma_start(out=CBT[:, :], in_=cb_src, allow_slow_non_contiguous=True),
               sets="ld_cb", sem="d_cst2")
        for t in range(NT1):
            sc.dma("gpsimd",
                   lambda e, t=t: e.dma_start(out=B1[:, t * 1024:(t + 1) * 1024],
                                              in_=x_d[t * 128:(t + 1) * 128, :]),
                   sets=f"xbtm_{t}", sem="d_gx")

        sc.op("vector", lambda e: e.memset(IDB[:, :], 1.0), sets="idb0")
        sc.op("gpsimd",
              lambda e: e.affine_select(IDB[:, :], IDB[:, :], [[1, 128]], ALU.is_equal,
                                        0.0, base=0, channel_multiplier=-1),
              waits=["idb0"], sets="idb")
        sc.op("gpsimd", lambda e: e.memset(ONE[:, :], 1.0), sets="one")
        sc.op("gpsimd", lambda e: e.memset(ONC[:, :], 1.0), sets="onc")
        sc.op("vector", lambda e: e.tensor_copy(BPH[:, :], BPR[:, :]),
              waits=["ld_bp"], sets="bph")
        sc.op("vector", lambda e: e.tensor_copy(F1[0:1, 0:1024], BPH[:, :]),
              waits=["bph"], sets="bps")
        sc.op("vector", lambda e: e.tensor_tensor(BPL[:, :], BPR[:, :], F1[0:1, 0:1024],
                                                  ALU.subtract),
              waits=["bps"], sets="bpl")

        def padz(e):
            last = None
            for c in range(DC):
                e.memset(B3[:, c * CW:c * CW + CPAD], 0.0)
                last = e.memset(B3[:, c * CW + CPAD + S:(c + 1) * CW], 0.0)
            return last
        sc.op("gpsimd", padz, sets="padz")

    def p0_weights(ms):
        AW = [wk_d, wv_d, wq_d, wp_d]
        for m in ms:
            wd = AW[m]
            slot = m % 2
            teng = "scalar" if m == 2 else "sync"
            tsem = "d_awts" if m == 2 else "d_awtp"
            for oc in range(DC):
                g = m * DC + oc
                wslot = g % 2
                wl = []
                if g >= 2:
                    wl.append(f"awt_{g-2}_{DC-1}")
                sc.dma("gpsimd",
                       lambda e, wd=wd, oc=oc, wslot=wslot:
                       e.dma_start(out=WN[:, wslot * 1024:(wslot + 1) * 1024],
                                   in_=wd[oc * 128:(oc + 1) * 128, :]),
                       waits=wl, sets=f"awc_{g}", sem=f"d_awc{wslot}")
                for dc in range(DC):
                    w2 = [f"awc_{g}"]
                    if oc == 0 and dc == 0 and m >= 2:
                        w2.append(f"aw_used_{m-2}")
                    sc.dma(teng,
                           lambda e, slot=slot, dc=dc, oc=oc, wslot=wslot:
                           e.dma_start_transpose(
                               B4[:, slot * 8192 + dc * 1024 + oc * 128:
                                  slot * 8192 + dc * 1024 + oc * 128 + 128],
                               WN[:, wslot * 1024 + dc * 128:
                                  wslot * 1024 + dc * 128 + 128]),
                           waits=w2, sets=f"awt_{g}_{dc}", sem=f"{tsem}{wslot}")
            sc.op(teng, lambda e: e.nop(),
                  waits=[f"awt_{m*DC+oc}_{DC-1}" for oc in range(6, 8)],
                  sets=f"awrdy_{m}")


    NU = 72
    NUC = NU // 8

    natv_w = I16R.bitcast(F32)

    def phase_w_chunk(lo, hi):
        natv = natv_w
        NSCR = natv[:, 3072:4096]
        for pos in range(lo, hi):
            sl = pos % 2
            wl = []
            if pos >= 2:
                wl.append(f"Wdq1_{pos-2}")
            sc.dma("scalar",
                   lambda e, pos=pos, sl=sl: e.dma_start(
                       out=natv[:, sl * 1024:(sl + 1) * 1024],
                       in_=ws_d[pos * 128:(pos + 1) * 128, :]),
                   waits=wl, sets=f"Wld_{pos}", sem=f"d_ws{sl}")

            def dq0(e, sl=sl):
                w3 = bass.AP(natv, sl * 1024, [[4096, 128], [32, 32], [1, 32]])
                sc.edge("vector", e, e.tensor_reduce(col("blk", 32), w3, AX.X, ALU.max,
                                                     apply_absolute_value=True))
                sc.edge("vector", e, e.tensor_scalar(col("sblk", 32), col("blk", 32),
                                                     1.0 / 7.0, 1e-12, ALU.mult, ALU.max))
                return e.reciprocal(col("rblk", 32), col("sblk", 32))
            sc.op("vector", dq0, waits=[f"Wld_{pos}"], sets=f"Wdq0_{pos}")

            def dq1(e, sl=sl):
                co, _ = CL["rblk"]
                rs3 = bass.AP(COLS.ap().tensor, co, [[COLW, 128], [1, 32], [0, 32]])
                return e.tensor_tensor(NSCR, natv[:, sl * 1024:(sl + 1) * 1024],
                                       rs3, ALU.mult)
            sc.op("vector", dq1, waits=[f"Wdq0_{pos}"], sets=f"Wdq1_{pos}")
            sc.op("vector", lambda e: e.tensor_scalar(NSCR, NSCR,
                                                      MAGIC, -MAGIC, ALU.add, ALU.add),
                  waits=[f"Wdq1_{pos}"], sets=f"Wdq2_{pos}")

            def dq3(e, sl=sl):
                co, _ = CL["sblk"]
                s3 = bass.AP(COLS.ap().tensor, co, [[COLW, 128], [1, 32], [0, 32]])
                return e.tensor_tensor(WQB[:, sl * 1024:(sl + 1) * 1024],
                                       NSCR, s3, ALU.mult)
            w3l = [f"Wdq2_{pos}"]
            if pos >= 2:
                w3l.append(f"Wtr_{pos-2}_{DC-1}")
            sc.op("vector", dq3, waits=w3l, sets=f"Wdq3_{pos}")
            for dc in range(DC):
                w4 = [f"Wdq3_{pos}"]
                if dc == 0 and pos >= 2:
                    w4.append(f"Wsp_{pos-2}")
                sc.dma("sync",
                       lambda e, sl=sl, dc=dc: e.dma_start_transpose(
                           WSTG[:, sl * 1024 + dc * 128:sl * 1024 + dc * 128 + 128],
                           WQB[:, sl * 1024 + dc * 128:sl * 1024 + dc * 128 + 128]),
                       waits=w4, sets=f"Wtr_{pos}_{dc}", sem=f"d_wtr{sl}")
            sc.dma("gpsimd",
                   lambda e, pos=pos, sl=sl: e.dma_start(
                       out=agin[pos * 128:(pos + 1) * 128, :],
                       in_=WSTG[:, sl * 1024:(sl + 1) * 1024]),
                   waits=[f"Wtr_{pos}_{DC-1}"], sets=f"Wsp_{pos}", sem=f"d_agin{sl}")
    def phase_w_tail():
        sc.op("gpsimd",
              lambda e: e.collective_compute("AllGather", ALU.bypass,
                                             replica_groups=[list(range(8))],
                                             ins=[agin.ap()], outs=[agout.ap()]),
              waits=[f"Wsp_{NUC-1}", f"Wsp_{NUC-2}"], sets="ag2", sem="ccs2",
              amt=1)

    def p1_xbfm():
        for t in range(NT1):
            for dc in range(DC):
                k = t * DC + dc
                r = k % 8

                def tr(e, t=t, dc=dc, r=r):
                    return e.transpose(TRP8[r], B1[:, t * 1024 + dc * 128:
                                                   t * 1024 + dc * 128 + 128], IDB[:, :])
                w = [f"xbtm_{NT1-1}", "idb"]
                if k >= 8:
                    w.append(f"xbfm_{k-8}")
                sc.op("tensor", tr, waits=w, sets=f"xbtr_{k}")

                def ev(e, t=t, dc=dc, r=r):
                    return e.tensor_copy(
                        B3[:, dc * CW + CPAD + t * 128:dc * CW + CPAD + t * 128 + 128],
                        TRP8[r])
                sc.op("vector", ev, waits=[f"xbtr_{k}", "padz"], sets=f"xbfm_{k}")
        sc.events["xbfm_done"] = sc.events[f"xbfm_{NT1*DC-1}"]

    def proj_tm(dst, m, elu, tag):
        for nt in range(NT1):
            r = nt % 2
            w = ["xbfm_done", f"awrdy_{m}"]
            if nt >= 2:
                w.append(f"{tag}_ev_{nt-2}")

            def mms(e, nt=nt, r=r, m=m):
                slot = m % 2
                inst = None
                for dc in range(DC):
                    lhsT = B3[:, dc * CW + CPAD + nt * 128:dc * CW + CPAD + nt * 128 + 128]
                    for h in range(2):
                        inst = e.matmul(
                            PS_B[:, r * 1024 + h * 512:r * 1024 + h * 512 + 512],
                            lhsT,
                            B4[:, slot * 8192 + dc * 1024 + h * 512:
                               slot * 8192 + dc * 1024 + h * 512 + 512],
                            start=(dc == 0), stop=(dc == DC - 1))
                return inst
            sc.op("tensor", mms, waits=w, sets=f"{tag}_mm_{nt}")
            if nt == NT1 - 1:
                sc.events[f"aw_used_{m}"] = sc.events[f"{tag}_mm_{nt}"]
            ps = PS_B[:, r * 1024:(r + 1) * 1024]
            if elu:
                w0 = [f"{tag}_mm_{nt}"]
                if nt >= 1:
                    w0.append(f"{tag}_ev_{nt-1}")
                sc.op("scalar", lambda e, ps=ps: e.activation(F(4096, 5120), ps,
                                                              AF.Relu, scale=-1.0),
                      waits=w0, sets=f"{tag}_s0_{nt}")
                sc.op("scalar", lambda e: e.activation(F(5120, 6144), F(4096, 5120),
                                                       AF.Exp, scale=-1.0),
                      waits=[f"{tag}_s0_{nt}"], sets=f"{tag}_s1_{nt}")
                sc.op("vector", lambda e, ps=ps: e.tensor_scalar(F(6144, 7168), ps, 0.0,
                                                                 None, ALU.max),
                      waits=[f"{tag}_s1_{nt}"], sets=f"{tag}_s2_{nt}")

                def fuse(e, nt=nt):
                    return e.tensor_tensor(dst[:, nt * 1024:(nt + 1) * 1024],
                                           F(5120, 6144), F(6144, 7168), ALU.add)
                sc.op("vector", fuse, waits=[f"{tag}_s2_{nt}"], sets=f"{tag}_ev_{nt}")

                def mk_ksmm(nt):
                    def ksmm(e, nt=nt):
                        inst = None
                        for h in range(2):
                            inst = e.matmul(PS_A[0:1, h * 512:(h + 1) * 512],
                                            ONC[:, :],
                                            dst[:, nt * 1024 + h * 512:
                                                nt * 1024 + h * 512 + 512],
                                            start=(nt == 0), stop=(nt == NT1 - 1))
                        return inst
                    return ksmm
                if nt >= 1:
                    sc.op("tensor", mk_ksmm(nt - 1), waits=[f"{tag}_ev_{nt-1}", "onc"],
                          sets=f"ks_mm_{nt-1}")
                if nt == NT1 - 1:
                    sc.op("tensor", mk_ksmm(nt), waits=[f"{tag}_ev_{nt}", "onc"],
                          sets=f"ks_mm_{nt}")
            else:
                def ev(e, nt=nt, ps=ps):
                    return e.copy(dst[:, nt * 1024:(nt + 1) * 1024], ps)
                sc.op("scalar", ev, waits=[f"{tag}_mm_{nt}"], sets=f"{tag}_ev_{nt}")
        sc.events[f"{tag}_done"] = sc.events[f"{tag}_ev_{NT1-1}"]

    def p1_kstate_finish():
        sc.op("vector", lambda e: e.tensor_copy(BPR[0:1, :], PS_A[0:1, 0:1024]),
              waits=[f"ks_mm_{NT1-1}", "bpl"], sets="ks_row")
        sc.dma("gpsimd", lambda e: e.dma_start(out=ks_d[None, :], in_=BPR[0:1, :]),
               waits=["ks_row"], sets="ks_st", sem="d_ks")
        ks_src = bass.AP(ks_d, 0, [[1, 128], [128, 8]])
        sc.dma("gpsimd", lambda e: e.dma_start(out=KSF[:, :], in_=ks_src, allow_slow_non_contiguous=True),
               waits=["ks_st"], sets="ks_ld", sem="d_ks")

    def p1_kv():
        sc.op("gpsimd", lambda e: e.memset(KVM[:, :], 0.0), sets="kvm0")
        for p in range(NP):
            def mm(e, p=p):
                inst = None
                for nt in range(NT1):
                    inst = e.matmul(PS_A[:, 1024 + p * 128:1152 + p * 128],
                                    B1[:, nt * 1024 + p * 128:nt * 1024 + p * 128 + 128],
                                    B2[:, nt * 1024 + p * 128:nt * 1024 + p * 128 + 128],
                                    start=(nt == 0), stop=(nt == NT1 - 1))
                return inst
            sc.op("tensor", mm, waits=["kb_done", "v_done"], sets=f"kv_mm_{p}")
        for p in range(NP):
            def ev(e, p=p):
                e.tensor_copy(KVM[0:64, p * 130:p * 130 + 64],
                              PS_A[0:64, 1024 + p * 128:1024 + p * 128 + 64])
                e.tensor_copy(KVM[64:128, p * 130 + 64:p * 130 + 128],
                              PS_A[64:128, 1024 + p * 128 + 64:1024 + p * 128 + 128])
                e.tensor_copy(KVM[0:64, p * 130 + 128:p * 130 + 129], KSF[0:64, p:p + 1])
                return e.tensor_copy(KVM[64:128, p * 130 + 129:p * 130 + 130],
                                     KSF[64:128, p:p + 1])
            sc.op("vector", ev, waits=[f"kv_mm_{NP-1}", "ks_ld", "kvm0"], sets=f"kv_ev_{p}")
        sc.events["kv_done"] = sc.events[f"kv_ev_{NP-1}"]

    def p1_conv():
        for c in range(DC):
            base = c * CW + CPAD
            w = ["kv_done", "ld_cw", "ld_cb", "xbfm_done"]
            if c >= 1:
                w = [f"cv_{c-1}", "ld_cw", "ld_cb"]
            sc.op("vector",
                  lambda e, c=c, base=base: e.tensor_scalar(
                      F(0, 0 + S), B3[:, base:base + S],
                      CWT[:, 3 * c + 1:3 * c + 2], CBT[:, c:c + 1],
                      ALU.mult, ALU.add),
                  waits=w, sets=f"cv1_{c}")
            sc.op("vector",
                  lambda e, c=c, base=base: e.scalar_tensor_tensor(
                      F(2048, 2048 + S), B3[:, base - 1:base + S - 1],
                      CWT[:, 3 * c:3 * c + 1], F(0, 0 + S), ALU.mult, ALU.add),
                  waits=[f"cv1_{c}"], sets=f"cv2_{c}")
            sc.op("vector",
                  lambda e, c=c, base=base: e.scalar_tensor_tensor(
                      B1[:, c * S:(c + 1) * S], B3[:, base + 1:base + S + 1],
                      CWT[:, 3 * c + 2:3 * c + 3], F(2048, 2048 + S), ALU.mult, ALU.add),
                  waits=[f"cv2_{c}"], sets=f"cv_{c}")
        sc.events["conv_done"] = sc.events[f"cv_{DC-1}"]

    def p1_qattn(mid=None):
        for T in range(T512):
            for oc in range(DC):
                r = oc % 2
                w = ["xbfm_done", "awrdy_2"]
                if T == 0 and oc == 0:
                    w.append("ks_row")
                if oc >= 2:
                    w.append(f"q_ev_{T}_{oc-2}")
                if T > 0 and oc == 0:
                    w.append(f"a2_{T-1}_done")

                def mms(e, T=T, oc=oc, r=r):
                    inst = None
                    for dc in range(DC):
                        inst = e.matmul(
                            PS_A[:, r * 512:(r + 1) * 512],
                            B4[:, dc * 1024 + oc * 128:dc * 1024 + oc * 128 + 128],
                            B3[:, dc * CW + CPAD + T * 512:dc * CW + CPAD + (T + 1) * 512],
                            start=(dc == 0), stop=(dc == DC - 1))
                    return inst
                sc.op("tensor", mms, waits=w, sets=f"q_mm_{T}_{oc}")
                ps = PS_A[:, r * 512:(r + 1) * 512]
                w0 = [f"q_mm_{T}_{oc}"]
                if oc >= 1:
                    w0.append(f"q_ev_{T}_{oc-1}")
                elif T > 0:
                    w0.append(f"q_ev_{T-1}_{DC-1}")
                sc.op("scalar", lambda e, ps=ps: e.activation(F(4096, 4608), ps,
                                                              AF.Relu, scale=-1.0),
                      waits=w0, sets=f"q_s0_{T}_{oc}")
                sc.op("scalar", lambda e: e.activation(F(5120, 5632), F(4096, 4608),
                                                       AF.Exp, scale=-1.0),
                      waits=[f"q_s0_{T}_{oc}"], sets=f"q_s1_{T}_{oc}")
                sc.op("scalar", lambda e, ps=ps: e.activation(F(6144, 6656), ps,
                                                              AF.Relu),
                      waits=[f"q_s1_{T}_{oc}"], sets=f"q_s2_{T}_{oc}")
                wf = [f"q_s2_{T}_{oc}"]
                if T == 0 and oc == 0:
                    wf.append("kv_done")
                if T > 0 and oc == 0:
                    wf.append(f"amm_{T-1}_3")

                def qev(e, oc=oc):
                    return e.tensor_tensor(B2[:, oc * 512:(oc + 1) * 512],
                                           F(5120, 5632), F(6144, 6656), ALU.add)
                sc.op("vector", qev, waits=wf, sets=f"q_ev_{T}_{oc}")
            for sub in range(4):
                w = [f"q_ev_{T}_{DC-1}", "kv_done"]
                if sub > 0:
                    w.append(f"attev_{T}_{sub-1}")
                elif T > 0:
                    w.append(f"r1s_{T*4-1}")

                def amm(e, T=T, sub=sub):
                    inst = None
                    for p in range(NP):
                        inst = e.matmul(
                            PS_B[:, p * PACK:p * PACK + 130],
                            B2[:, p * 512 + sub * 128:p * 512 + sub * 128 + 128],
                            KVM[:, p * 130:(p + 1) * 130],
                            start=True, stop=True)
                    return inst
                sc.op("tensor", amm, waits=w, sets=f"amm_{T}_{sub}")

                def rdg(e):
                    src = bass.AP(PS_B.ap().tensor, 128, [[2048, 128], [PACK, 8], [1, 2]])
                    sc.edge("vector", e, e.tensor_scalar(col("rd", 16), src, 1e-6, None,
                                                         ALU.max))
                    return e.reciprocal(col("rd", 16), col("rd", 16))
                sc.op("vector", rdg, waits=[f"amm_{T}_{sub}"], sets=f"rdr_{T}_{sub}")
                wv = [f"rdr_{T}_{sub}"]
                if T > 0 and sub == 0:
                    wv.append(f"a2_{T-1}_done")

                def adiv(e, sub=sub):
                    src = bass.AP(PS_B.ap().tensor, 0,
                                  [[2048, 128], [PACK, 8], [64, 2], [1, 64]])
                    co, _ = CL["rd"]
                    rdx = bass.AP(COLS.ap().tensor, co,
                                  [[COLW, 128], [2, 8], [1, 2], [0, 64]])
                    return e.tensor_tensor(B2[:, 4096 + sub * 1024:4096 + (sub + 1) * 1024],
                                           src, rdx, ALU.mult)
                sc.op("vector", adiv, waits=wv, sets=f"attev_{T}_{sub}")
            for k in range(DC * 4):
                ec, sub = divmod(k, 4)
                r = k % 6

                def tr(e, ec=ec, sub=sub, r=r):
                    return e.transpose(TRP6[r],
                                       B2[:, 4096 + sub * 1024 + ec * 128:
                                          4096 + sub * 1024 + ec * 128 + 128],
                                       IDB[:, :])
                w = [f"attev_{T}_{sub}", f"attev_{T}_3"]
                if k >= 6:
                    w.append(f"a2_{T}_{k-6}")
                sc.op("tensor", tr, waits=w, sets=f"a2tr_{T}_{k}")
                wf = [f"a2tr_{T}_{k}", "conv_done", "ld_cb"]
                if T > 0 and k == 0:
                    wf.append(f"pmm_{T-1}_3")

                def fuse(e, ec=ec, sub=sub, r=r, T=T):
                    return e.tensor_tensor(
                        B2[:, 8192 + ec * 512 + sub * 128:
                           8192 + ec * 512 + sub * 128 + 128],
                        B1[:, ec * S + T * 512 + sub * 128:
                           ec * S + T * 512 + sub * 128 + 128],
                        TRP6[r], ALU.add)
                sc.op("vector", fuse, waits=wf, sets=f"a2_{T}_{k}")
            sc.events[f"a2_{T}_done"] = sc.events[f"a2_{T}_{DC*4-1}"]
            for sub in range(4):
                t128 = T * 4 + sub
                xsl = t128 % 2
                wld = []
                if t128 >= 2:
                    wld.append(f"r1s_{t128-2}")
                sc.dma("sync",
                       lambda e, t128=t128, xsl=xsl: e.dma_start(
                           out=F(8192 + xsl * 1024, 9216 + xsl * 1024),
                           in_=x_d[t128 * 128:(t128 + 1) * 128, :]),
                       waits=wld, sets=f"xld_{t128}", sem=f"d_x{xsl}")
                r = sub % 2
                ps = PS_A[:, 1024:2048] if r == 0 else PS_B[:, 0:1024]
                w = [f"a2_{T}_done", "awrdy_3", "bpl", "one"]
                if sub >= 2:
                    w.append(f"r1s_{t128-2}")
                if T > 0 and sub < 2:
                    w.append(f"r1s_{(T-1)*4 + sub + 2}")
                if T == 0 and sub == 0:
                    w.append(f"kv_ev_{NP-1}")

                def pmm(e, sub=sub, ps=ps):
                    for ec in range(DC):
                        e.matmul(ps[:, 0:512],
                                 B2[:, 8192 + ec * 512 + sub * 128:
                                    8192 + ec * 512 + sub * 128 + 128],
                                 B4[:, 8192 + ec * 1024:8192 + ec * 1024 + 512],
                                 start=(ec == 0), stop=False)
                        e.matmul(ps[:, 512:1024],
                                 B2[:, 8192 + ec * 512 + sub * 128:
                                    8192 + ec * 512 + sub * 128 + 128],
                                 B4[:, 8192 + ec * 1024 + 512:8192 + ec * 1024 + 1024],
                                 start=(ec == 0), stop=False)
                    e.matmul(ps[:, 0:512], ONE[:, :], BPH[0:1, 0:512],
                             start=False, stop=False)
                    e.matmul(ps[:, 512:1024], ONE[:, :], BPH[0:1, 512:1024],
                             start=False, stop=False)
                    e.matmul(ps[:, 0:512], ONE[:, :], BPL[0:1, 0:512],
                             start=False, stop=True)
                    return e.matmul(ps[:, 512:1024], ONE[:, :], BPL[0:1, 512:1024],
                                    start=False, stop=True)
                sc.op("tensor", pmm, waits=w, sets=f"pmm_{T}_{sub}")
                if sub == 3:
                    sc.events[f"pmm_{T}_3"] = sc.events[f"pmm_{T}_{sub}"]

                def r1f(e, ps=ps, t128=t128):
                    xsl = t128 % 2
                    return e.scalar_tensor_tensor(F(7168, 8192), ps, 1.0,
                                                  F(8192 + xsl * 1024,
                                                    9216 + xsl * 1024),
                                                  ALU.mult, ALU.add,
                                                  accum_out=col(f"sr_{t128 % 4}"))
                sc.op("vector", r1f, waits=[f"pmm_{T}_{sub}", f"xld_{t128}"],
                      sets=f"r1s_{t128}")
                sc.op("scalar",
                      lambda e, t128=t128: e.activation(F(4096, 5120), F(7168, 8192),
                                                        AF.Square,
                                                        accum_out=col(f"sq_{t128 % 4}")),
                      waits=[f"r1s_{t128}"], sets=f"r1q_{t128}")

                def stats(e, t128=t128):
                    j = t128 % 4
                    sc.edge("vector", e, e.tensor_scalar(col(f"m_{j}"), col(f"sr_{j}"), 1.0 / D, None, ALU.mult))
                    sc.edge("vector", e, e.tensor_scalar(col(f"e2_{j}"), col(f"sq_{j}"), 1.0 / D, None, ALU.mult))
                    sc.edge("vector", e, e.tensor_tensor(col(f"mm_{j}"), col(f"m_{j}"), col(f"m_{j}"), ALU.mult))
                    sc.edge("vector", e, e.tensor_tensor(col(f"v_{j}"), col(f"e2_{j}"), col(f"mm_{j}"), ALU.subtract))
                    return e.tensor_scalar(col(f"v_{j}"), col(f"v_{j}"), EPS, None, ALU.add)
                sc.op("vector", stats, waits=[f"r1q_{t128}"], sets=f"st_{t128}")

                def rstd1(e, t128=t128):
                    j = t128 % 4
                    sc.edge("scalar", e, e.activation(col(f"ln_{j}"), col(f"v_{j}"), AF.Ln))
                    return e.activation(col(f"r0_{j}"), col(f"ln_{j}"), AF.Exp, scale=-0.5)
                sc.op("scalar", rstd1, waits=[f"st_{t128}"], sets=f"rs0_{t128}")

                def rstd2(e, t128=t128):
                    j = t128 % 4
                    sc.edge("vector", e, e.tensor_tensor(col(f"t1_{j}"), col(f"r0_{j}"), col(f"r0_{j}"), ALU.mult))
                    sc.edge("vector", e, e.tensor_tensor(col(f"t1_{j}"), col(f"t1_{j}"), col(f"v_{j}"), ALU.mult))
                    sc.edge("vector", e, e.tensor_scalar(col(f"t1_{j}"), col(f"t1_{j}"), -0.5, 1.5, ALU.mult, ALU.add))
                    sc.edge("vector", e, e.tensor_tensor(col(f"rstd_{j}"), col(f"r0_{j}"), col(f"t1_{j}"), ALU.mult))
                    sc.edge("vector", e, e.tensor_tensor(col(f"nm_{j}"), col(f"m_{j}"), col(f"rstd_{j}"), ALU.mult))
                    return e.tensor_scalar(col(f"nm_{j}"), col(f"nm_{j}"), -1.0, None,
                                           ALU.mult)
                sc.op("vector", rstd2, waits=[f"rs0_{t128}"], sets=f"rs_{t128}")

                def yslot(t):
                    return F((t % 2) * 1024, (t % 2) * 1024 + 1024)

                def yap(e, t128=t128):
                    j = t128 % 4
                    return e.activation(yslot(t128), F(7168, 8192), AF.Identity,
                                        bias=col(f"nm_{j}"), scale=col(f"rstd_{j}"))
                w2 = [f"rs_{t128}"]
                if t128 >= 2:
                    w2.append(f"ymx_{t128-2}")
                else:
                    w2.append("conv_done")
                sc.op("scalar", yap, waits=w2, sets=f"y_{t128}")
                sc.dma("gpsimd",
                       lambda e, t128=t128: e.dma_start(
                           out=y_sp[t128 * 128:(t128 + 1) * 128, :], in_=yslot(t128)),
                       waits=[f"y_{t128}"], sets=f"yst_{t128}", sem=f"d_y{t128 % 2}")
                sc.op("vector",
                      lambda e, t128=t128: e.tensor_reduce(
                          COLS[:, CL["ymxs"][0] + t128:CL["ymxs"][0] + t128 + 1],
                          yslot(t128), AX.X, ALU.max, apply_absolute_value=True),
                      waits=[f"y_{t128}", f"yst_{t128}"], sets=f"ymx_{t128}")
            if mid is not None:
                mid(T)

    def allreduce_cols(tag, srcs, waits, ncols):
        wz = list(waits)
        if last_ccr[0] is not None:
            wz.append(last_ccr[0])
        sc.op("gpsimd", lambda e: e.memset(CCR[:, :], 0.0), waits=wz, sets=f"ccz_{tag}")
        for i, cn in enumerate(srcs):
            wl = [f"ccz_{tag}"]
            if i > 0:
                wl.append(f"ccr_{tag}_{i-1}")
            sc.dma("gpsimd",
                   lambda e, cn=cn: e.dma_start(out=mx_d[:],
                                                in_=COLS[:, CL[cn][0]:CL[cn][0] + 1]),
                   waits=wl, sets=f"cca_{tag}_{i}", sem="d_cc")
            mx_src = bass.AP(mx_d, 0, [[0, 1], [1, 128]])
            sc.dma("gpsimd", lambda e: e.dma_start(out=ROWB[:, :], in_=mx_src),
                   waits=[f"cca_{tag}_{i}"], sets=f"ccb_{tag}_{i}", sem="d_cc")
            sc.op("vector",
                  lambda e, i=i: e.tensor_reduce(CCR[0:1, i:i + 1], ROWB[:, :], AX.X,
                                                 ALU.max),
                  waits=[f"ccb_{tag}_{i}"], sets=f"ccr_{tag}_{i}")
        last_ccr[0] = f"ccr_{tag}_{len(srcs)-1}"
        sc.dma("gpsimd", lambda e: e.dma_start(out=cc_in[:, :], in_=CCR[:, :]),
               waits=[f"ccr_{tag}_{len(srcs)-1}"], sets=f"cci_{tag}", sem="d_cc")
        sc.op("gpsimd",
              lambda e: e.collective_compute("AllReduce", ALU.max,
                                             replica_groups=[list(range(8))],
                                             ins=[cc_in.ap()], outs=[cc_out.ap()]),
              waits=[f"cci_{tag}"], sets=f"cc_{tag}", sem="ccs", amt=1)
        for i in range(ncols):
            src = bass.AP(cc_out, i, [[0, 128], [1, 1]])
            sc.dma("gpsimd",
                   lambda e, i=i, src=src, tag=tag: e.dma_start(
                       out=COLS[:, CL[f"g{tag}_{i}"][0]:CL[f"g{tag}_{i}"][0] + 1],
                       in_=src),
                   waits=[f"cc_{tag}"], sets=f"gbc_{tag}_{i}", sem="d_cc")

    def p2():
        sc.op("vector",
              lambda e: e.tensor_reduce(col("ymxa"),
                                        COLS[:, CL["ymxs"][0]:CL["ymxs"][0] + NT1],
                                        AX.X, ALU.max),
              waits=[f"ymx_{NT1-1}"], sets="ymxa")
        allreduce_cols("0", ["ymxa"], ["ymxa"], 1)

        def sxf(e):
            sc.edge("vector", e, e.tensor_scalar(col("sx"), col("g0_0"), 1.0 / 32767.0, 1e-12, ALU.mult, ALU.max))
            return e.reciprocal(col("rsx"), col("sx"))
        sc.op("vector", sxf, waits=["gbc_0_0"], sets="sx")
        for t in range(NT1):
            sl = t % 2
            wl = ["sx"]
            if t >= 2:
                wl.append(f"ixr_{t-2}")
            else:
                wl.append(f"yst_{NT1-1}")
                wl.append(f"yst_{NT1-2}")
            sc.dma("sync",
                   lambda e, t=t, sl=sl: e.dma_start(
                       out=F(sl * 1024, (sl + 1) * 1024),
                       in_=y_sp[t * 128:(t + 1) * 128, :]),
                   waits=wl, sets=f"yld_{t}", sem=f"d_yl{sl}")

            def rnd(e, sl=sl):
                sc.edge("vector", e, e.tensor_scalar(F(2048, 3072), F(sl * 1024, (sl + 1) * 1024),
                                col("rsx"), MAGIC, ALU.mult, ALU.add))
                return e.tensor_scalar(F(3072, 4096), F(2048, 3072), -MAGIC, None, ALU.add)
            wr = [f"yld_{t}"]
            if t >= 1:
                wr.append(f"ixb_{t-1}")
            sc.op("vector", rnd, waits=wr, sets=f"ixr_{t}")
            wb2 = [f"ixr_{t}"]
            if t >= 1:
                wb2.append(f"ixtr_{t-1}_{DC-1}")
            else:
                wb2.append(f"Wtr_{NUC-1}_{DC-1}")
            sc.op("vector", lambda e: e.tensor_copy(WQB[:, 0:1024], F(3072, 4096)),
                  waits=wb2, sets=f"ixb_{t}")
            for dc in range(DC):
                k = t * DC + dc
                r = k % 8

                def tr(e, dc=dc, r=r):
                    return e.transpose(TRP8[r], WQB[:, dc * 128:(dc + 1) * 128], IDB[:, :])
                w = [f"ixb_{t}"]
                if k >= 8:
                    w.append(f"ixev_{k-8}")
                sc.op("tensor", tr, waits=w, sets=f"ixtr_{t}_{dc}")

                def ev(e, t=t, dc=dc, r=r):
                    return e.tensor_copy(B2[:, dc * S + t * 128:dc * S + t * 128 + 128],
                                         TRP8[r])
                sc.op("vector", ev, waits=[f"ixtr_{t}_{dc}"], sets=f"ixev_{k}")
        sc.events["ix_done"] = sc.events[f"ixev_{NT1*DC-1}"]

    def p3_one(mat, sp, tag, mcols):
        for ic in range(IC):
            u = mat * IC + ic
            row = ((u % 8) * NUC + u // 8) * 128
            rsl = ic % 4
            wl = ["ag2"]
            if ic >= 4:
                wl.append(f"{tag}mm_{ic-4}")
            if mat == 0 and ic < 4:
                wl.append(f"q_mm_{T512-1}_{DC-1}")
            if mat == 1 and ic < 4:
                wl.append(f"u_mm_{IC-4+ic}")
            sc.dma("sync",
                   lambda e, row=row, rsl=rsl: e.dma_start(
                       out=B4[:, rsl * 1024:(rsl + 1) * 1024],
                       in_=agout[row:row + 128, :]),
                   waits=wl, sets=f"{tag}wt_{ic}", sem=f"d_wn{rsl}")
            pr = ic % 2
            PS = PS_A if pr == 0 else PS_B
            w5 = [f"{tag}wt_{ic}", "ix_done"]
            if ic >= 2:
                w5.append(f"{tag}ev_{ic-2}")
                w5.append(f"{tag}mx_{ic-2}")
            if mat == 0 and ic < 2:
                w5.append(f"r1s_{S//128-1}")
            if mat == 1 and ic < 2:
                w5.append(f"u_ev_{IC-2+ic}")
                w5.append(f"u_mx_{IC-2+ic}")

            def mms(e, rsl=rsl, PS=PS):
                inst = None
                for dc in range(DC):
                    for tt in range(S // 512):
                        inst = e.matmul(
                            PS[:, tt * 512:(tt + 1) * 512],
                            B4[:, rsl * 1024 + dc * 128:rsl * 1024 + dc * 128 + 128],
                            B2[:, dc * S + tt * 512:dc * S + (tt + 1) * 512],
                            start=(dc == 0), stop=(dc == DC - 1))
                return inst
            sc.op("tensor", mms, waits=w5, sets=f"{tag}mm_{ic}")
            esl = ic % 2
            w6 = [f"{tag}mm_{ic}"]
            if ic >= 2:
                w6.append(f"{tag}sp_{ic-2}")
            if mat == 1 and ic < 2:
                w6.append(f"u_sp_{IC-2+ic}")
            sc.op("scalar",
                  lambda e, esl=esl, PS=PS: e.copy(
                      F(6144 + esl * 2048, 6144 + esl * 2048 + S), PS[:, 0:S]),
                  waits=w6, sets=f"{tag}ev_{ic}")
            sc.op("vector",
                  lambda e, ic=ic, esl=esl: e.tensor_reduce(
                      COLS[:, CL[mcols][0] + ic:CL[mcols][0] + ic + 1],
                      F(6144 + esl * 2048, 6144 + esl * 2048 + S), AX.X, ALU.max,
                      apply_absolute_value=True),
                  waits=[f"{tag}ev_{ic}"], sets=f"{tag}mx_{ic}")
            sc.dma("gpsimd",
                   lambda e, ic=ic, esl=esl, sp=sp: e.dma_start(
                       out=sp[ic * 128:(ic + 1) * 128, :],
                       in_=F(6144 + esl * 2048, 6144 + esl * 2048 + S)),
                   waits=[f"{tag}ev_{ic}"], sets=f"{tag}sp_{ic}", sem=f"d_{tag}sp{esl}")

    def p3():
        p3_one(0, u_sp, "u_", "umx")
        sc.op("vector",
              lambda e: e.tensor_reduce(col("umxa"),
                                        COLS[:, CL["umx"][0]:CL["umx"][0] + IC],
                                        AX.X, ALU.max),
              waits=[f"u_mx_{IC-1}"], sets="umxa")
        allreduce_cols("A", ["umxa"], ["umxa"], 1)

        def algu(e):
            sc.edge("vector", e, e.tensor_tensor(col("tmpc"), col("gA_0"), col("sx"), ALU.mult))
            sc.edge("vector", e, e.tensor_scalar(col("su"), col("tmpc"), 1.0 / 32767.0, 1e-12, ALU.mult, ALU.max))
            sc.edge("vector", e, e.reciprocal(col("fu"), col("su")))
            return e.tensor_tensor(col("fu"), col("fu"), col("sx"), ALU.mult)
        sc.op("vector", algu, waits=["gbc_A_0", "sx"], sets="algU")
        p3_one(1, g_sp, "g_", "gmx")
        sc.op("vector",
              lambda e: e.tensor_reduce(col("gmxa"),
                                        COLS[:, CL["gmx"][0]:CL["gmx"][0] + IC],
                                        AX.X, ALU.max),
              waits=[f"g_mx_{IC-1}"], sets="gmxa")

    def barrier_a():
        allreduce_cols("G", ["gmxa"], ["gmxa"], 1)

        def alg(e):
            sc.edge("vector", e, e.tensor_tensor(col("tmpc"), col("gG_0"), col("sx"), ALU.mult))
            sc.edge("vector", e, e.tensor_scalar(col("sg"), col("tmpc"), 1.0 / 32767.0, 1e-12, ALU.mult, ALU.max))
            sc.edge("vector", e, e.reciprocal(col("fg"), col("sg")))
            return e.tensor_tensor(col("fg"), col("fg"), col("sx"), ALU.mult)
        sc.op("vector", alg, waits=["gbc_G_0", "sx"], sets="algA")

    def p2a():
        B3i = B3.bitcast(I16)
        B1f = B1.bitcast(F32)
        B4i2 = B4.bitcast(I16)

        def UR(q):
            return F(q * S, (q + 1) * S)

        def GR(q):
            return B1f[:, q * S:(q + 1) * S]

        def GI(q):
            return B3i[:, q * S:(q + 1) * S]

        def SQ(st):
            return B4i2[:, st * S:(st + 1) * S]

        def IUS(q):
            return I16R[:, q * S:(q + 1) * S]

        for ic in range(IC):
            q3 = ic % 3
            wl = [f"u_sp_{ic}"]
            if ic >= 3:
                wl.append(f"iu_{ic-3}")
            elif ic < 2:
                wl.append(f"ixb_{NT1-1}")
            sc.dma("sync",
                   lambda e, ic=ic, q3=q3: e.dma_start(
                       out=F(q3 * S, (q3 + 1) * S),
                       in_=u_sp[ic * 128:(ic + 1) * 128, :]),
                   waits=wl, sets=f"uld_{ic}", sem=f"d_ul{ic % 4}")
            w0 = [f"uld_{ic}", "algU"]
            if ic >= 4:
                w0.append(f"iusp_{ic-4}")
            sc.op("vector",
                  lambda e, ic=ic, q3=q3: e.tensor_scalar(IUS(ic % 4),
                                                          F(q3 * S, (q3 + 1) * S),
                                                          col("fu"), None, ALU.mult),
                  waits=w0, sets=f"iu_{ic}")
            sc.dma("gpsimd",
                   lambda e, ic=ic: e.dma_start(
                       out=iup_sp[ic * 128:(ic + 1) * 128, :], in_=IUS(ic % 4)),
                   waits=[f"iu_{ic}"], sets=f"iusp_{ic}", sem=f"d_iu{ic % 4}")

        def loads(ic):
            q = ic % 4
            wlg = [f"g_sp_{ic}"]
            if ic >= 4:
                wlg.append(f"t2sp_{ic-4}")
            sc.dma("sync",
                   lambda e, ic=ic, q=q: e.dma_start(
                       out=GR(q), in_=g_sp[ic * 128:(ic + 1) * 128, :]),
                   waits=wlg, sets=f"gld_{ic}", sem=f"d_gl{ic % 4}")
            wdl_ops(ic)

        def casts(ic):
            q = ic % 4
            w1 = [f"gld_{ic}", "algA"]
            if ic >= 8:
                w1.append(f"t2o_{ic-8}")
            sc.op("vector",
                  lambda e, ic=ic, q=q: e.tensor_scalar(GI(ic % 8), GR(q), col("fg"),
                                                        None, ALU.mult),
                  waits=w1, sets=f"ig_{ic}")
            sc.op("scalar",
                  lambda e, ic=ic, q=q: e.activation(GR(q), GI(ic % 8), AF.Sigmoid,
                                                     scale=col("sg")),
                  waits=[f"ig_{ic}"], sets=f"sg_{ic}")

        for ic0 in range(4):
            loads(ic0)
        casts(0)
        casts(1)
        for ic in range(IC):
            q = ic % 4
            st = ic % 2
            w2 = [f"sg_{ic}"]
            if ic >= 2:
                w2.append(f"t2g_{ic-2}")
            sc.op("vector",
                  lambda e, ic=ic, q=q, st=st: e.tensor_scalar(SQ(st), GR(q), 65536.0,
                                                               -32768.0, ALU.mult,
                                                               ALU.add),
                  waits=w2, sets=f"sq_{ic}")
            sc.op("gpsimd",
                  lambda e, ic=ic, q=q, st=st: e.tensor_tensor(GR(q), GI(ic % 8),
                                                               SQ(st), ALU.mult),
                  waits=[f"sq_{ic}"], sets=f"t2g_{ic}")
            if ic + 4 < IC:
                loads(ic + 4)
            if ic + 2 < IC:
                casts(ic + 2)
            sc.op("vector",
                  lambda e, ic=ic, q=q: e.scalar_tensor_tensor(
                      GR(q), GI(ic % 8), 32768.0, GR(q), ALU.mult, ALU.add),
                  waits=[f"t2g_{ic}"], sets=f"t2o_{ic}")
            sc.op("vector",
                  lambda e, ic=ic, q=q: e.tensor_reduce(
                      COLS[:, CL["tmx"][0] + ic:CL["tmx"][0] + ic + 1],
                      GR(q), AX.X, ALU.max, apply_absolute_value=True),
                  waits=[f"t2o_{ic}"], sets=f"t2_{ic}")
            sc.dma("gpsimd",
                   lambda e, ic=ic, q=q: e.dma_start(
                       out=g_sp[ic * 128:(ic + 1) * 128, :], in_=GR(q)),
                   waits=[f"t2_{ic}"], sets=f"t2sp_{ic}", sem=f"d_t2{ic % 4}")
        sc.op("vector",
              lambda e: e.tensor_reduce(col("tmxa"),
                                        COLS[:, CL["tmx"][0]:CL["tmx"][0] + IC],
                                        AX.X, ALU.max),
              waits=[f"t2_{IC-1}"], sets="tmxa")

    def wd_slot(ic):
        if ic < 12:
            return B4[:, 4096 + ic * 1024:4096 + (ic + 1) * 1024]
        return B2[:, (ic - 12) * 1024:(ic - 11) * 1024]

    def p_wdeq():
        pass

    def wdl_ops(j):
        oc, s3 = divmod(j, 3)
        u = 48 + j
        row = ((u % 8) * NUC + u // 8) * 128
        for dc in range(DC):
            ic = s3 * 8 + dc
            dst = wd_slot(ic)
            wl = []
            if j == 0 and dc == 0:
                wl = ["algA"]
            if j == 1 and dc == 4:
                wl = [f"g_mm_{IC-1}"]
            sc.dma("scalar",
                   lambda e, row=row, dc=dc, dst=dst, oc=oc: e.dma_start(
                       out=dst[:, oc * 128:(oc + 1) * 128],
                       in_=agout[row:row + 128, dc * 128:(dc + 1) * 128]),
                   waits=wl, sets=f"wdl_{j}_{dc}", sem=f"d_wdl{j % 4}")

    def wdl_fin():
        sc.op("scalar", lambda e: e.copy(col("cjunk"), CWT[:, 0:1]),
              waits=[f"wdl_{j}_{DC-1}" for j in range(20, 24)] + ["ld_cw"],
              sets="wd_ready")

    def barrier_b():
        allreduce_cols("B", ["tmxa"], ["tmxa"], 1)

        def alg(e):
            sc.edge("vector", e, e.tensor_scalar(col("sg16"), col("sg"), 1.0 / 65536.0, None, ALU.mult))
            sc.edge("vector", e, e.tensor_tensor(col("tmpc"), col("gB_0"), col("sg16"), ALU.mult))
            sc.edge("vector", e, e.tensor_scalar(col("st2"), col("tmpc"), 1.0 / 32767.0, 1e-12, ALU.mult, ALU.max))
            sc.edge("vector", e, e.reciprocal(col("ft"), col("st2")))
            return e.tensor_tensor(col("ft"), col("ft"), col("sg16"), ALU.mult)
        sc.op("vector", alg, waits=["gbc_B_0", "algA"], sets="algB")

    def p2c():
        B3i = B3.bitcast(I16)
        B1f = B1.bitcast(F32)

        def FR(q):
            return F(q * S, (q + 1) * S)

        def IUP(q):
            return B3i[:, q * S:(q + 1) * S]

        def PRD(q):
            return B1f[:, q * S:(q + 1) * S]

        def IG2(q):
            return I16R[:, q * S:(q + 1) * S]

        def loads2(ic):
            q = ic % 4
            wl = [f"t2sp_{ic}"]
            if ic >= 4:
                wl.append(f"c_cast_{ic-4}")
            elif ic < 3:
                wl.append(f"iu_{IC-3+ic}")
            else:
                wl.append("gmxa")
                wl.append("umxa")
                wl.append(f"g_sp_{IC-2}")
                wl.append(f"u_sp_{IC-2}")
            sc.dma("sync",
                   lambda e, ic=ic, q=q: e.dma_start(
                       out=FR(q), in_=g_sp[ic * 128:(ic + 1) * 128, :]),
                   waits=wl, sets=f"t2ld_{ic}", sem=f"d_cig{ic % 4}")
            wu_ = [f"iusp_{ic}"]
            if ic >= 4:
                wu_.append(f"otg_{ic-4}")
            else:
                wu_.append(f"t2o_{IC-8+ic}")
                wu_.append(f"t2o_{IC-4+ic}")
            sc.dma("sync",
                   lambda e, ic=ic, q=q: e.dma_start(out=IUP(q),
                                                     in_=iup_sp[ic * 128:(ic + 1) * 128, :]),
                   waits=wu_, sets=f"iuld_{ic}", sem=f"d_ciu{ic % 4}")

        def ccast(ic):
            q = ic % 4
            w0 = [f"t2ld_{ic}", "algB"]
            if ic >= 4:
                w0.append(f"ig2s_{ic-4}")
                w0.append(f"otg_{ic-4}")
            else:
                w0.append(f"iusp_{IC-4+ic}")
            sc.op("vector",
                  lambda e, ic=ic, q=q: e.tensor_scalar(IG2(q), FR(q), col("ft"),
                                                        None, ALU.mult),
                  waits=w0, sets=f"c_cast_{ic}")

        for ic0 in range(4):
            loads2(ic0)
        ccast(0)
        for ic in range(IC):
            q = ic % 4
            sc.dma("gpsimd",
                   lambda e, ic=ic, q=q: e.dma_start(
                       out=ig2_sp[ic * 128:(ic + 1) * 128, :], in_=IG2(q)),
                   waits=[f"c_cast_{ic}"], sets=f"ig2s_{ic}", sem=f"d_g2{ic % 4}")
            w1 = [f"c_cast_{ic}", f"iuld_{ic}"]
            if ic >= 4:
                w1.append(f"ot_{ic-4}")
            else:
                w1.append(f"t2sp_{IC-4+ic}")
            sc.op("gpsimd",
                  lambda e, ic=ic, q=q: e.tensor_tensor(PRD(q), IG2(q), IUP(q),
                                                        ALU.mult),
                  waits=w1, sets=f"otg_{ic}")
            if ic + 4 < IC:
                loads2(ic + 4)
            if ic + 1 < IC:
                ccast(ic + 1)
            sc.op("vector",
                  lambda e, ic=ic, q=q: e.tensor_reduce(
                      COLS[:, CL["omx"][0] + ic:CL["omx"][0] + ic + 1],
                      PRD(q), AX.X, ALU.max, apply_absolute_value=True),
                  waits=[f"otg_{ic}"], sets=f"ot_{ic}")
        sc.op("vector",
              lambda e: e.tensor_reduce(col("omxa"),
                                        COLS[:, CL["omx"][0]:CL["omx"][0] + IC],
                                        AX.X, ALU.max),
              waits=[f"ot_{IC-1}"], sets="omxa")

    def barrier_c():
        allreduce_cols("C", ["omxa"], ["omxa"], 1)

        def alg(e):
            sc.edge("vector", e, e.tensor_tensor(col("pso"), col("st2"), col("su"), ALU.mult))
            sc.edge("vector", e, e.tensor_tensor(col("tmpc"), col("gC_0"), col("pso"), ALU.mult))
            sc.edge("vector", e, e.tensor_scalar(col("so"), col("tmpc"), 1.0 / 32767.0, 1e-12, ALU.mult, ALU.max))
            sc.edge("vector", e, e.reciprocal(col("fo"), col("so")))
            return e.tensor_tensor(col("fo"), col("fo"), col("pso"), ALU.mult)
        sc.op("vector", alg, waits=["gbc_C_0", "algB"], sets="algC")

    def p2d():
        B4f = B4.bitcast(F32)

        def io_chain(sb):
            IOB = B3 if sb % 2 == 0 else B1
            for b in range(IC // 4):
                kb = sb * (IC // 4) + b
                bsl = b % 2

                def b_src(sp, sb=sb, b=b):
                    return bass.AP(sp, (b * 4) * 128 * S + sb * 512,
                                   [[S, 128], [128 * S, 4], [1, 512]])
                wl = []
                if kb >= 2:
                    wl.append(f"dot_{kb-2}")
                else:
                    for nm_ in (f"ig2s_{IC-4}", f"ig2s_{IC-3}", f"ig2s_{IC-2}",
                                f"ig2s_{IC-1}", f"otg_{IC-1}"):
                        wl.append(nm_)
                sc.dma("sync",
                       lambda e, sb=sb, b=b, bsl=bsl: e.dma_start(
                           out=I16R[:, bsl * 2048:bsl * 2048 + 2048],
                           in_=b_src(ig2_sp, sb, b)),
                       waits=wl, sets=f"g2ld_{kb}", sem=f"d_dg{kb % 4}")
                sc.dma("sync",
                       lambda e, sb=sb, b=b, bsl=bsl: e.dma_start(
                           out=I16R[:, 4096 + bsl * 2048:4096 + bsl * 2048 + 2048],
                           in_=b_src(iup_sp, sb, b)),
                       waits=wl, sets=f"duld_{kb}", sem=f"d_du{kb % 4}")
                w1 = [f"g2ld_{kb}", f"duld_{kb}"]
                if kb >= 2:
                    w1.append(f"iot_{kb-2}")
                else:
                    w1.append(f"c_cast_{IC-3+kb}")
                sc.op("gpsimd",
                      lambda e, bsl=bsl: e.tensor_tensor(
                          F(2048 + bsl * 2048, 4096 + bsl * 2048),
                          I16R[:, bsl * 2048:bsl * 2048 + 2048],
                          I16R[:, 4096 + bsl * 2048:4096 + bsl * 2048 + 2048],
                          ALU.mult),
                      waits=w1, sets=f"dot_{kb}")
                w2 = [f"dot_{kb}", "algC"]
                if kb >= 1:
                    w2.append(f"io_{kb-1}")
                sc.op("scalar",
                      lambda e, bsl=bsl: e.activation(
                          B4f[:, 0:2048], F(2048 + bsl * 2048, 4096 + bsl * 2048),
                          AF.Copy, bias=MAGIC, scale=col("fo")),
                      waits=w2, sets=f"iot_{kb}")
                w3 = [f"iot_{kb}"]
                if sb >= 2 and b == 0:
                    w3.append(f"dmm_{sb-2}_3")
                sc.op("vector",
                      lambda e, IOB=IOB, b=b: e.tensor_scalar(
                          IOB[:, b * 2048:(b + 1) * 2048], B4f[:, 0:2048], -MAGIC,
                          None, ALU.add),
                      waits=w3, sets=f"io_{kb}")

        def mm_ln(sb):
            IOB = B3 if sb % 2 == 0 else B1
            for sub in range(4):
                t128 = sb * 4 + sub
                r = sub % 2
                ps = PS_A[:, 0:1024] if r == 0 else PS_A[:, 1024:2048]
                w = [f"io_{sb*(IC//4)+IC//4-1}", "wd_ready"]
                if sub >= 2:
                    w.append(f"r2s_{t128-2}")
                if sb > 0 and sub < 2:
                    w.append(f"r2s_{(sb-1)*4 + sub + 2}")

                def dmm(e, IOB=IOB, sub=sub, ps=ps):
                    inst = None
                    for ic in range(IC):
                        wt = wd_slot(ic)
                        e.matmul(ps[:, 0:512],
                                 IOB[:, ic * 512 + sub * 128:ic * 512 + sub * 128 + 128],
                                 wt[:, 0:512], start=(ic == 0), stop=(ic == IC - 1))
                        inst = e.matmul(ps[:, 512:1024],
                                        IOB[:, ic * 512 + sub * 128:
                                            ic * 512 + sub * 128 + 128],
                                        wt[:, 512:1024], start=(ic == 0),
                                        stop=(ic == IC - 1))
                    return inst
                sc.op("tensor", dmm, waits=w, sets=f"dmm_{sb}_{sub}")
                if sub == 3:
                    sc.events[f"dmm_{sb}_3"] = sc.events[f"dmm_{sb}_{sub}"]
                ysl = t128 % 2
                wld = []
                if t128 >= 2:
                    wld.append(f"r2s_{t128-2}")
                else:
                    wld.append(f"c_cast_{IC-4}")
                sc.dma("sync",
                       lambda e, t128=t128, ysl=ysl: e.dma_start(
                           out=F(ysl * 1024, (ysl + 1) * 1024),
                           in_=y_sp[t128 * 128:(t128 + 1) * 128, :]),
                       waits=wld, sets=f"yld2_{t128}", sem=f"d_y2{ysl}")
                wr2 = [f"dmm_{sb}_{sub}", f"yld2_{t128}", "algC"]
                if t128 >= 1:
                    wr2.append(f"r2_{t128-1}")

                def r2f(e, ps=ps, ysl=ysl, t128=t128):
                    return e.scalar_tensor_tensor(F(6144, 7168), ps, col("so"),
                                                  F(ysl * 1024, (ysl + 1) * 1024),
                                                  ALU.mult, ALU.add,
                                                  accum_out=col(f"sr_{t128 % 4}"))
                sc.op("vector", r2f, waits=wr2, sets=f"r2s_{t128}")
                sc.op("scalar",
                      lambda e, t128=t128: e.activation(F(7168, 8192), F(6144, 7168),
                                                        AF.Square,
                                                        accum_out=col(f"sq_{t128 % 4}")),
                      waits=[f"r2s_{t128}"], sets=f"r2q_{t128}")

                def stats(e, t128=t128):
                    j = t128 % 4
                    sc.edge("vector", e, e.tensor_scalar(col(f"m_{j}"), col(f"sr_{j}"), 1.0 / D, None, ALU.mult))
                    sc.edge("vector", e, e.tensor_scalar(col(f"e2_{j}"), col(f"sq_{j}"), 1.0 / D, None, ALU.mult))
                    sc.edge("vector", e, e.tensor_tensor(col(f"mm_{j}"), col(f"m_{j}"), col(f"m_{j}"), ALU.mult))
                    sc.edge("vector", e, e.tensor_tensor(col(f"v_{j}"), col(f"e2_{j}"), col(f"mm_{j}"), ALU.subtract))
                    return e.tensor_scalar(col(f"v_{j}"), col(f"v_{j}"), EPS, None, ALU.add)
                sc.op("vector", stats, waits=[f"r2q_{t128}"], sets=f"st2_{t128}")

                def rstd1(e, t128=t128):
                    j = t128 % 4
                    sc.edge("scalar", e, e.activation(col(f"ln_{j}"), col(f"v_{j}"), AF.Ln))
                    return e.activation(col(f"r0_{j}"), col(f"ln_{j}"), AF.Exp, scale=-0.5)
                sc.op("scalar", rstd1, waits=[f"st2_{t128}"], sets=f"rs20_{t128}")

                def rstd2(e, t128=t128):
                    j = t128 % 4
                    sc.edge("vector", e, e.tensor_tensor(col(f"t1_{j}"), col(f"r0_{j}"), col(f"r0_{j}"), ALU.mult))
                    sc.edge("vector", e, e.tensor_tensor(col(f"t1_{j}"), col(f"t1_{j}"), col(f"v_{j}"), ALU.mult))
                    sc.edge("vector", e, e.tensor_scalar(col(f"t1_{j}"), col(f"t1_{j}"), -0.5, 1.5, ALU.mult, ALU.add))
                    sc.edge("vector", e, e.tensor_tensor(col(f"rstd_{j}"), col(f"r0_{j}"), col(f"t1_{j}"), ALU.mult))
                    sc.edge("vector", e, e.tensor_tensor(col(f"nm_{j}"), col(f"m_{j}"), col(f"rstd_{j}"), ALU.mult))
                    return e.tensor_scalar(col(f"nm_{j}"), col(f"nm_{j}"), -1.0, None,
                                           ALU.mult)
                sc.op("vector", rstd2, waits=[f"rs20_{t128}"], sets=f"rs2_{t128}")
                osl = t128 % 2

                def oap(e, t128=t128, osl=osl):
                    j = t128 % 4
                    return e.activation(F(8192 + osl * 1024, 9216 + osl * 1024),
                                        F(6144, 7168), AF.Identity,
                                        bias=col(f"nm_{j}"), scale=col(f"rstd_{j}"))
                w3 = [f"rs2_{t128}"]
                if t128 >= 2:
                    w3.append(f"ost_{t128-2}")
                sc.op("scalar", oap, waits=w3, sets=f"r2_{t128}")
                sc.dma("sync",
                       lambda e, t128=t128, osl=osl: e.dma_start(
                           out=out_d[t128 * 128:(t128 + 1) * 128, :],
                           in_=F(8192 + osl * 1024, 9216 + osl * 1024)),
                       waits=[f"r2_{t128}"], sets=f"ost_{t128}", sem=f"d_out{t128 % 2}")
        for sb in range(T512):
            io_chain(sb)
            if sb >= 1:
                mm_ln(sb - 1)
        mm_ln(T512 - 1)
        nst = S // 128

        def finw(e):
            e.wait_ge(sc.sems["d_out0"], 16 * ((nst + 1) // 2))
            return e.wait_ge(sc.sems["d_out1"], 16 * (nst // 2))
        sc.ops["sync"].append(([], finw, None))
        sc.counts.setdefault("d_out0", 0)
        sc.counts.setdefault("d_out1", 0)

    p0()
    p0_weights([0, 1])
    p1_xbfm()
    proj_tm(B1, 0, True, "kb")
    proj_tm(B2, 1, False, "v")
    p1_kstate_finish()
    p1_kv()
    p0_weights([2, 3])
    def _pw_mid(T):
        if T512 == 1:
            phase_w_chunk(0, NUC)
            phase_w_tail()
        elif T == 0:
            phase_w_chunk(0, 3)
        elif T == 1:
            phase_w_chunk(3, 6)
        elif T == 2:
            phase_w_chunk(6, NUC)
            phase_w_tail()

    p1_conv()
    p1_qattn(mid=_pw_mid)
    p2()
    p3()
    barrier_a()
    p2a()
    wdl_fin()
    barrier_b()
    p2c()
    barrier_c()
    p2d()

    assert _next[0] <= COLW, f"COLS overflow: {_next[0]}"
    sc.emit(stack)
    stack.close()
    return nc


_NC_CACHE = {}


def get_nc(S):
    if S not in _NC_CACHE:
        _NC_CACHE[S] = build_nc(S)
    return _NC_CACHE[S]


def kernel(L0_attn_x, L0_attn_Wq, L0_attn_Wk, L0_attn_Wv, L0_attn_Wp, L0_attn_bp,
           L0_attn_conv_w, L0_attn_conv_b, L0_ffn_w_gate, L0_ffn_w_up, L0_ffn_w_down):
    x = np.ascontiguousarray(np.asarray(L0_attn_x, dtype=np.float32))
    B, S, _ = x.shape
    nc = get_nc(S)
    w = {
        "wq": np.ascontiguousarray(L0_attn_Wq, dtype=np.float32),
        "wk": np.ascontiguousarray(L0_attn_Wk, dtype=np.float32),
        "wv": np.ascontiguousarray(L0_attn_Wv, dtype=np.float32),
        "wp": np.ascontiguousarray(L0_attn_Wp, dtype=np.float32),
        "bp": np.ascontiguousarray(L0_attn_bp, dtype=np.float32),
        "cw": np.ascontiguousarray(L0_attn_conv_w, dtype=np.float32),
        "cb": np.ascontiguousarray(L0_attn_conv_b, dtype=np.float32),
    }
    wu = np.asarray(L0_ffn_w_up, dtype=np.float32)
    wg = np.asarray(L0_ffn_w_gate, dtype=np.float32)
    wdn = np.asarray(L0_ffn_w_down, dtype=np.float32)

    def unit(u):
        if u < 24:
            return wu[u * 128:(u + 1) * 128, :]
        if u < 48:
            return wg[(u - 24) * 128:(u - 23) * 128, :]
        j = u - 48
        oc, s3 = divmod(j, 3)
        return wdn[oc * 128:(oc + 1) * 128, s3 * 1024:(s3 + 1) * 1024]

    in_maps = []
    for b in range(B):
        ws = np.ascontiguousarray(
            np.concatenate([unit(b + 8 * p) for p in range(9)], axis=0))
        in_maps.append(dict(w, x=np.ascontiguousarray(x[b]), ws=ws))
    res = run_bass_kernel_spmd(nc, in_maps, core_ids=list(range(B)))
    return np.stack([np.asarray(r["out"]) for r in res.results], axis=0)


# version 33
# speedup vs baseline: 1.0218x; 1.0218x over previous
from contextlib import ExitStack
import numpy as np
import concourse.bass as bass
import concourse.mybir as mybir
from concourse.bass_utils import run_bass_kernel_spmd

F32 = mybir.dt.float32
BF16 = mybir.dt.bfloat16
I16 = mybir.dt.int16
U16 = mybir.dt.uint16
ALU = mybir.AluOpType
AF = mybir.ActivationFunctionType
AX = mybir.AxisListType

MAGIC = float(np.float32(1.5 * 2.0 ** 23))
D = 1024
DC = 8
NP = 8
I3 = 3072
IC = 24
PACK = 176
EPS = 1e-5
CPAD = 2
CW = 2048 + 2 * CPAD
COLW = 336

ENGINES = ("sync", "gpsimd", "vector", "scalar", "tensor")


class Sched:
    def __init__(self, nc):
        self.nc = nc
        self.ops = {e: [] for e in ENGINES}
        self.counts = {}
        self.events = {}
        self.sems = {}
        self.auto_counts = {e: 0 for e in ENGINES}

    def edge(self, engname, e, inst):
        self.auto_counts[engname] += 1
        n = self.auto_counts[engname]
        sem = self.sems["a_" + engname]
        inst.then_inc(sem, 1)
        e.wait_ge(sem, n)

    def op(self, eng, fn, waits=(), sets=None, sem=None, amt=1):
        wl = [w for w in waits if w is not None]
        inc = None
        if sets is not None:
            assert sets not in self.events, f"duplicate event {sets}"
            if sem is None:
                sem = "c_" + eng
                amt = 1
            v = self.counts.get(sem, 0) + amt
            self.counts[sem] = v
            self.events[sets] = (sem, v)
            inc = (sem, amt)
        self.ops[eng].append((wl, fn, inc))
        return sets

    def dma(self, eng, fn, waits=(), sets=None, sem=None):
        assert sem is not None
        return self.op(eng, fn, waits, sets, sem=sem, amt=16)

    def emit(self, stack):
        names = set(self.counts.keys())
        for e in ENGINES:
            names.add("a_" + e)
        for (sn, _v) in self.events.values():
            names.add(sn)
        for n in sorted(names):
            self.sems[n] = stack.enter_context(self.nc.semaphore(n))
        block = stack.enter_context(self.nc.Block())
        sched = self

        def mk(engname):
            own = "c_" + engname
            engops = sched.ops[engname]

            def body(eng):
                for wl, fn, inc in engops:
                    for w in wl:
                        s, v = sched.events[w]
                        eng.wait_ge(sched.sems[s], v)
                    inst = fn(eng)
                    if inc is not None:
                        if inst is None:
                            inst = eng.nop()
                        inst.then_inc(sched.sems[inc[0]], inc[1])
            return body

        block.sync(mk("sync"))
        block.gpsimd(mk("gpsimd"))
        block.vector(mk("vector"))
        block.scalar(mk("scalar"))
        block.tensor(mk("tensor"))


def build_nc(S):
    NT1 = S // 128
    T512 = S // 512
    CW = S + 2 * CPAD
    nc = bass.Bass("TRN2", target_bir_lowering=False)

    x_d = nc.dram_tensor("x", [S, D], F32, kind="ExternalInput")
    wq_d = nc.dram_tensor("wq", [D, D], F32, kind="ExternalInput")
    wk_d = nc.dram_tensor("wk", [D, D], F32, kind="ExternalInput")
    wv_d = nc.dram_tensor("wv", [D, D], F32, kind="ExternalInput")
    wp_d = nc.dram_tensor("wp", [D, D], F32, kind="ExternalInput")
    bp_d = nc.dram_tensor("bp", [D], F32, kind="ExternalInput")
    cw_d = nc.dram_tensor("cw", [D, 3], F32, kind="ExternalInput")
    cb_d = nc.dram_tensor("cb", [D], F32, kind="ExternalInput")
    ws_d = nc.dram_tensor("ws", [9 * 128, D], F32, kind="ExternalInput")
    out_d = nc.dram_tensor("out", [S, D], F32, kind="ExternalOutput")

    u_sp = nc.dram_tensor("u_sp", [IC * 128, S], F32)
    g_sp = nc.dram_tensor("g_sp", [IC * 128, S], F32)
    y_sp = nc.dram_tensor("y_sp", [S, D], F32)
    iup_sp = nc.dram_tensor("iup_sp", [IC * 128, S], I16)
    ig2_sp = nc.dram_tensor("ig2_sp", [IC * 128, S], I16)
    sq_sp = nc.dram_tensor("sq_sp", [IC * 128, S], I16)
    ks_d = nc.dram_tensor("ks_d", [D], F32)
    mx_d = nc.dram_tensor("mx_d", [128], F32)
    agin = nc.dram_tensor("agin", [9 * 128, D], BF16)
    agout = nc.dram_tensor("agout", [72 * 128, D], BF16)
    cc_in = nc.dram_tensor("cc_in", [1, 8], F32)
    cc_out = nc.dram_tensor("cc_out", [1, 8], F32)

    stack = ExitStack()
    E = stack.enter_context

    BW = max(NT1 * 1024, IC * 512)
    B1 = E(nc.sbuf_tensor("B1", [128, BW], BF16))
    B2 = E(nc.sbuf_tensor("B2", [128, BW], BF16))
    B3 = E(nc.sbuf_tensor("B3", [128, max(DC * CW, IC * 512)], BF16))
    B4 = E(nc.sbuf_tensor("B4", [128, 16384], BF16))
    WN = E(nc.sbuf_tensor("WN", [128, 2048], BF16))
    WQB = E(nc.sbuf_tensor("WQB", [128, 2048], BF16))
    F1 = E(nc.sbuf_tensor("F1", [128, 10240], F32))
    I16R = E(nc.sbuf_tensor("I16R", [128, 8192], I16))
    WSTG = I16R.bitcast(BF16)[:, 4096:6144]
    COLS = E(nc.sbuf_tensor("COLS", [128, COLW], F32))
    ROWB = E(nc.sbuf_tensor("ROWB", [1, 128], F32))
    CCR = E(nc.sbuf_tensor("CCR", [1, 8], F32))
    BPR = E(nc.sbuf_tensor("BPR", [1, 1024], F32))
    BPH = E(nc.sbuf_tensor("BPH", [1, 1024], BF16))
    BPL = E(nc.sbuf_tensor("BPL", [1, 1024], BF16))
    ONE = E(nc.sbuf_tensor("ONE", [1, 128], BF16))
    ONC = E(nc.sbuf_tensor("ONC", [128, 1], BF16))
    IDB = E(nc.sbuf_tensor("IDB", [128, 128], BF16))
    CWT = E(nc.sbuf_tensor("CWT", [128, 24], F32))
    CBT = E(nc.sbuf_tensor("CBT", [128, 8], F32))
    KSF = E(nc.sbuf_tensor("KSF", [128, 8], F32))

    KVM = E(nc.sbuf_tensor("KVM", [128, NP * 130], BF16))
    PS_A = E(nc.psum_tensor("PS_A", [128, 2048], F32))
    PS_B = E(nc.psum_tensor("PS_B", [128, 2048], F32))

    sc = Sched(nc)

    CL = {}
    _next = [0]

    def col(name, n=1):
        if name not in CL:
            CL[name] = (_next[0], n)
            _next[0] += n
        o, n2 = CL[name]
        assert n2 >= n
        return COLS[:, o:o + n]

    def F(a, b):
        return F1[:, a:b]

    for j in range(4):
        for nm in ("sr", "sq", "m", "e2", "mm", "v", "ln", "r0", "t1", "rstd", "nm"):
            col(f"{nm}_{j}")
    col("rd", 16)
    col("blk", 32)
    col("sblk", 32)
    col("rblk", 32)
    col("ymxs", NT1)
    col("umx", IC)
    col("gmx", IC)
    col("tmx", IC)
    col("omx", IC)
    for nm in ("ymxa", "umxa", "gmxa", "tmxa", "omxa", "g0_0", "gA_0", "gA_1",
               "gB_0", "gC_0", "sx", "rsx", "su", "fu", "sg", "fg", "sg16",
               "st2", "ft", "so", "fo", "pso", "tmpc", "cjunk"):
        col(nm)

    def _bv(ps, c):
        return ps[:, c:c + 128].bitcast(BF16)[:, 0:128]
    TRP8 = [_bv(PS_A, 0), _bv(PS_A, 512), _bv(PS_A, 1024), _bv(PS_A, 1536),
            _bv(PS_B, 0), _bv(PS_B, 512), _bv(PS_B, 1024), _bv(PS_B, 1536)]
    TRP6 = [_bv(PS_B, 1536), _bv(PS_A, 0), _bv(PS_A, 512),
            _bv(PS_B, 0), _bv(PS_B, 512), _bv(PS_B, 1024)]

    last_ccr = [None]

    def p0():
        sc.dma("sync", lambda e: e.dma_start(out=BPR[:, :], in_=bp_d[None, :]),
               sets="ld_bp", sem="d_cst0")
        cw_src = bass.AP(cw_d, 0, [[3, 128], [384, 8], [1, 3]])
        sc.dma("sync", lambda e: e.dma_start(out=CWT[:, :], in_=cw_src),
               sets="ld_cw", sem="d_cst1")
        cb_src = bass.AP(cb_d, 0, [[1, 128], [128, 8]])
        sc.dma("sync", lambda e: e.dma_start(out=CBT[:, :], in_=cb_src, allow_slow_non_contiguous=True),
               sets="ld_cb", sem="d_cst2")
        for t in range(NT1):
            sc.dma("gpsimd",
                   lambda e, t=t: e.dma_start(out=B1[:, t * 1024:(t + 1) * 1024],
                                              in_=x_d[t * 128:(t + 1) * 128, :]),
                   sets=f"xbtm_{t}", sem="d_gx")

        sc.op("vector", lambda e: e.memset(IDB[:, :], 1.0), sets="idb0")
        sc.op("gpsimd",
              lambda e: e.affine_select(IDB[:, :], IDB[:, :], [[1, 128]], ALU.is_equal,
                                        0.0, base=0, channel_multiplier=-1),
              waits=["idb0"], sets="idb")
        sc.op("gpsimd", lambda e: e.memset(ONE[:, :], 1.0), sets="one")
        sc.op("gpsimd", lambda e: e.memset(ONC[:, :], 1.0), sets="onc")
        sc.op("vector", lambda e: e.tensor_copy(BPH[:, :], BPR[:, :]),
              waits=["ld_bp"], sets="bph")
        sc.op("vector", lambda e: e.tensor_copy(F1[0:1, 0:1024], BPH[:, :]),
              waits=["bph"], sets="bps")
        sc.op("vector", lambda e: e.tensor_tensor(BPL[:, :], BPR[:, :], F1[0:1, 0:1024],
                                                  ALU.subtract),
              waits=["bps"], sets="bpl")

        def padz(e):
            last = None
            for c in range(DC):
                e.memset(B3[:, c * CW:c * CW + CPAD], 0.0)
                last = e.memset(B3[:, c * CW + CPAD + S:(c + 1) * CW], 0.0)
            return last
        sc.op("gpsimd", padz, sets="padz")

    def p0_weights(ms):
        AW = [wk_d, wv_d, wq_d, wp_d]
        for m in ms:
            wd = AW[m]
            slot = m % 2
            teng = "scalar" if m == 2 else "sync"
            tsem = "d_awts" if m == 2 else "d_awtp"
            for oc in range(DC):
                g = m * DC + oc
                wslot = g % 2
                wl = []
                if g >= 2:
                    wl.append(f"awt_{g-2}_{DC-1}")
                sc.dma("gpsimd",
                       lambda e, wd=wd, oc=oc, wslot=wslot:
                       e.dma_start(out=WN[:, wslot * 1024:(wslot + 1) * 1024],
                                   in_=wd[oc * 128:(oc + 1) * 128, :]),
                       waits=wl, sets=f"awc_{g}", sem=f"d_awc{wslot}")
                for dc in range(DC):
                    w2 = [f"awc_{g}"]
                    if oc == 0 and dc == 0 and m >= 2:
                        w2.append(f"aw_used_{m-2}")
                    sc.dma(teng,
                           lambda e, slot=slot, dc=dc, oc=oc, wslot=wslot:
                           e.dma_start_transpose(
                               B4[:, slot * 8192 + dc * 1024 + oc * 128:
                                  slot * 8192 + dc * 1024 + oc * 128 + 128],
                               WN[:, wslot * 1024 + dc * 128:
                                  wslot * 1024 + dc * 128 + 128]),
                           waits=w2, sets=f"awt_{g}_{dc}", sem=f"{tsem}{wslot}")
            sc.op(teng, lambda e: e.nop(),
                  waits=[f"awt_{m*DC+oc}_{DC-1}" for oc in range(6, 8)],
                  sets=f"awrdy_{m}")


    NU = 72
    NUC = NU // 8

    natv_w = I16R.bitcast(F32)

    def phase_w_chunk(lo, hi):
        natv = natv_w
        NSCR = natv[:, 3072:4096]
        for pos in range(lo, hi):
            sl = pos % 2
            wl = []
            if pos >= 2:
                wl.append(f"Wdq1_{pos-2}")
            sc.dma("scalar",
                   lambda e, pos=pos, sl=sl: e.dma_start(
                       out=natv[:, sl * 1024:(sl + 1) * 1024],
                       in_=ws_d[pos * 128:(pos + 1) * 128, :]),
                   waits=wl, sets=f"Wld_{pos}", sem=f"d_ws{sl}")

            def dq0(e, sl=sl):
                w3 = bass.AP(natv, sl * 1024, [[4096, 128], [32, 32], [1, 32]])
                sc.edge("vector", e, e.tensor_reduce(col("blk", 32), w3, AX.X, ALU.max,
                                                     apply_absolute_value=True))
                sc.edge("vector", e, e.tensor_scalar(col("sblk", 32), col("blk", 32),
                                                     1.0 / 7.0, 1e-12, ALU.mult, ALU.max))
                return e.reciprocal(col("rblk", 32), col("sblk", 32))
            sc.op("vector", dq0, waits=[f"Wld_{pos}"], sets=f"Wdq0_{pos}")

            def dq1(e, sl=sl):
                co, _ = CL["rblk"]
                rs3 = bass.AP(COLS.ap().tensor, co, [[COLW, 128], [1, 32], [0, 32]])
                return e.tensor_tensor(NSCR, natv[:, sl * 1024:(sl + 1) * 1024],
                                       rs3, ALU.mult)
            sc.op("vector", dq1, waits=[f"Wdq0_{pos}"], sets=f"Wdq1_{pos}")
            sc.op("vector", lambda e: e.tensor_scalar(NSCR, NSCR,
                                                      MAGIC, -MAGIC, ALU.add, ALU.add),
                  waits=[f"Wdq1_{pos}"], sets=f"Wdq2_{pos}")

            def dq3(e, sl=sl):
                co, _ = CL["sblk"]
                s3 = bass.AP(COLS.ap().tensor, co, [[COLW, 128], [1, 32], [0, 32]])
                return e.tensor_tensor(WQB[:, sl * 1024:(sl + 1) * 1024],
                                       NSCR, s3, ALU.mult)
            w3l = [f"Wdq2_{pos}"]
            if pos >= 2:
                w3l.append(f"Wtr_{pos-2}_{DC-1}")
            sc.op("vector", dq3, waits=w3l, sets=f"Wdq3_{pos}")
            for dc in range(DC):
                w4 = [f"Wdq3_{pos}"]
                if dc == 0 and pos >= 2:
                    w4.append(f"Wsp_{pos-2}")
                sc.dma("sync",
                       lambda e, sl=sl, dc=dc: e.dma_start_transpose(
                           WSTG[:, sl * 1024 + dc * 128:sl * 1024 + dc * 128 + 128],
                           WQB[:, sl * 1024 + dc * 128:sl * 1024 + dc * 128 + 128]),
                       waits=w4, sets=f"Wtr_{pos}_{dc}", sem=f"d_wtr{sl}")
            sc.dma("gpsimd",
                   lambda e, pos=pos, sl=sl: e.dma_start(
                       out=agin[pos * 128:(pos + 1) * 128, :],
                       in_=WSTG[:, sl * 1024:(sl + 1) * 1024]),
                   waits=[f"Wtr_{pos}_{DC-1}"], sets=f"Wsp_{pos}", sem=f"d_agin{sl}")
    def phase_w_tail():
        sc.op("gpsimd",
              lambda e: e.collective_compute("AllGather", ALU.bypass,
                                             replica_groups=[list(range(8))],
                                             ins=[agin.ap()], outs=[agout.ap()]),
              waits=[f"Wsp_{NUC-1}", f"Wsp_{NUC-2}"], sets="ag2", sem="ccs2",
              amt=1)

    def p1_xbfm():
        for t in range(NT1):
            for dc in range(DC):
                k = t * DC + dc
                r = k % 8

                def tr(e, t=t, dc=dc, r=r):
                    return e.transpose(TRP8[r], B1[:, t * 1024 + dc * 128:
                                                   t * 1024 + dc * 128 + 128], IDB[:, :])
                w = [f"xbtm_{NT1-1}", "idb"]
                if k >= 8:
                    w.append(f"xbfm_{k-8}")
                sc.op("tensor", tr, waits=w, sets=f"xbtr_{k}")

                def ev(e, t=t, dc=dc, r=r):
                    return e.tensor_copy(
                        B3[:, dc * CW + CPAD + t * 128:dc * CW + CPAD + t * 128 + 128],
                        TRP8[r])
                sc.op("vector", ev, waits=[f"xbtr_{k}", "padz"], sets=f"xbfm_{k}")
        sc.events["xbfm_done"] = sc.events[f"xbfm_{NT1*DC-1}"]

    def proj_tm(dst, m, elu, tag):
        for nt in range(NT1):
            r = nt % 2
            w = ["xbfm_done", f"awrdy_{m}"]
            if nt >= 2:
                w.append(f"{tag}_ev_{nt-2}")

            def mms(e, nt=nt, r=r, m=m):
                slot = m % 2
                inst = None
                for dc in range(DC):
                    lhsT = B3[:, dc * CW + CPAD + nt * 128:dc * CW + CPAD + nt * 128 + 128]
                    for h in range(2):
                        inst = e.matmul(
                            PS_B[:, r * 1024 + h * 512:r * 1024 + h * 512 + 512],
                            lhsT,
                            B4[:, slot * 8192 + dc * 1024 + h * 512:
                               slot * 8192 + dc * 1024 + h * 512 + 512],
                            start=(dc == 0), stop=(dc == DC - 1))
                return inst
            sc.op("tensor", mms, waits=w, sets=f"{tag}_mm_{nt}")
            if nt == NT1 - 1:
                sc.events[f"aw_used_{m}"] = sc.events[f"{tag}_mm_{nt}"]
            ps = PS_B[:, r * 1024:(r + 1) * 1024]
            if elu:
                w0 = [f"{tag}_mm_{nt}"]
                if nt >= 1:
                    w0.append(f"{tag}_ev_{nt-1}")
                sc.op("scalar", lambda e, ps=ps: e.activation(F(4096, 5120), ps,
                                                              AF.Relu, scale=-1.0),
                      waits=w0, sets=f"{tag}_s0_{nt}")
                sc.op("scalar", lambda e: e.activation(F(5120, 6144), F(4096, 5120),
                                                       AF.Exp, scale=-1.0),
                      waits=[f"{tag}_s0_{nt}"], sets=f"{tag}_s1_{nt}")
                sc.op("vector", lambda e, ps=ps: e.tensor_scalar(F(6144, 7168), ps, 0.0,
                                                                 None, ALU.max),
                      waits=[f"{tag}_s1_{nt}"], sets=f"{tag}_s2_{nt}")

                def fuse(e, nt=nt):
                    return e.tensor_tensor(dst[:, nt * 1024:(nt + 1) * 1024],
                                           F(5120, 6144), F(6144, 7168), ALU.add)
                sc.op("vector", fuse, waits=[f"{tag}_s2_{nt}"], sets=f"{tag}_ev_{nt}")

                def mk_ksmm(nt):
                    def ksmm(e, nt=nt):
                        inst = None
                        for h in range(2):
                            inst = e.matmul(PS_A[0:1, h * 512:(h + 1) * 512],
                                            ONC[:, :],
                                            dst[:, nt * 1024 + h * 512:
                                                nt * 1024 + h * 512 + 512],
                                            start=(nt == 0), stop=(nt == NT1 - 1))
                        return inst
                    return ksmm
                if nt >= 1:
                    sc.op("tensor", mk_ksmm(nt - 1), waits=[f"{tag}_ev_{nt-1}", "onc"],
                          sets=f"ks_mm_{nt-1}")
                if nt == NT1 - 1:
                    sc.op("tensor", mk_ksmm(nt), waits=[f"{tag}_ev_{nt}", "onc"],
                          sets=f"ks_mm_{nt}")
            else:
                def ev(e, nt=nt, ps=ps):
                    return e.copy(dst[:, nt * 1024:(nt + 1) * 1024], ps)
                sc.op("scalar", ev, waits=[f"{tag}_mm_{nt}"], sets=f"{tag}_ev_{nt}")
        sc.events[f"{tag}_done"] = sc.events[f"{tag}_ev_{NT1-1}"]

    def p1_kstate_finish():
        sc.op("vector", lambda e: e.tensor_copy(BPR[0:1, :], PS_A[0:1, 0:1024]),
              waits=[f"ks_mm_{NT1-1}", "bpl"], sets="ks_row")
        sc.dma("gpsimd", lambda e: e.dma_start(out=ks_d[None, :], in_=BPR[0:1, :]),
               waits=["ks_row"], sets="ks_st", sem="d_ks")
        ks_src = bass.AP(ks_d, 0, [[1, 128], [128, 8]])
        sc.dma("gpsimd", lambda e: e.dma_start(out=KSF[:, :], in_=ks_src, allow_slow_non_contiguous=True),
               waits=["ks_st"], sets="ks_ld", sem="d_ks")

    def p1_kv():
        sc.op("gpsimd", lambda e: e.memset(KVM[:, :], 0.0), sets="kvm0")
        for p in range(NP):
            def mm(e, p=p):
                inst = None
                for nt in range(NT1):
                    inst = e.matmul(PS_A[:, 1024 + p * 128:1152 + p * 128],
                                    B1[:, nt * 1024 + p * 128:nt * 1024 + p * 128 + 128],
                                    B2[:, nt * 1024 + p * 128:nt * 1024 + p * 128 + 128],
                                    start=(nt == 0), stop=(nt == NT1 - 1))
                return inst
            sc.op("tensor", mm, waits=["kb_done", "v_done"], sets=f"kv_mm_{p}")
        for p in range(NP):
            def ev(e, p=p):
                e.tensor_copy(KVM[0:64, p * 130:p * 130 + 64],
                              PS_A[0:64, 1024 + p * 128:1024 + p * 128 + 64])
                e.tensor_copy(KVM[64:128, p * 130 + 64:p * 130 + 128],
                              PS_A[64:128, 1024 + p * 128 + 64:1024 + p * 128 + 128])
                e.tensor_copy(KVM[0:64, p * 130 + 128:p * 130 + 129], KSF[0:64, p:p + 1])
                return e.tensor_copy(KVM[64:128, p * 130 + 129:p * 130 + 130],
                                     KSF[64:128, p:p + 1])
            sc.op("vector", ev, waits=[f"kv_mm_{NP-1}", "ks_ld", "kvm0"], sets=f"kv_ev_{p}")
        sc.events["kv_done"] = sc.events[f"kv_ev_{NP-1}"]

    def p1_conv():
        for c in range(DC):
            base = c * CW + CPAD
            w = ["kv_done", "ld_cw", "ld_cb", "xbfm_done"]
            if c >= 1:
                w = [f"cv_{c-1}", "ld_cw", "ld_cb"]
            sc.op("vector",
                  lambda e, c=c, base=base: e.tensor_scalar(
                      F(0, 0 + S), B3[:, base:base + S],
                      CWT[:, 3 * c + 1:3 * c + 2], CBT[:, c:c + 1],
                      ALU.mult, ALU.add),
                  waits=w, sets=f"cv1_{c}")
            sc.op("vector",
                  lambda e, c=c, base=base: e.scalar_tensor_tensor(
                      F(2048, 2048 + S), B3[:, base - 1:base + S - 1],
                      CWT[:, 3 * c:3 * c + 1], F(0, 0 + S), ALU.mult, ALU.add),
                  waits=[f"cv1_{c}"], sets=f"cv2_{c}")
            sc.op("vector",
                  lambda e, c=c, base=base: e.scalar_tensor_tensor(
                      B1[:, c * S:(c + 1) * S], B3[:, base + 1:base + S + 1],
                      CWT[:, 3 * c + 2:3 * c + 3], F(2048, 2048 + S), ALU.mult, ALU.add),
                  waits=[f"cv2_{c}"], sets=f"cv_{c}")
        sc.events["conv_done"] = sc.events[f"cv_{DC-1}"]

    def p1_qattn(mid=None):
        for T in range(T512):
            for oc in range(DC):
                r = oc % 2
                w = ["xbfm_done", "awrdy_2"]
                if T == 0 and oc == 0:
                    w.append("ks_row")
                if oc >= 2:
                    w.append(f"q_ev_{T}_{oc-2}")
                if T > 0 and oc == 0:
                    w.append(f"a2_{T-1}_done")

                def mms(e, T=T, oc=oc, r=r):
                    inst = None
                    for dc in range(DC):
                        inst = e.matmul(
                            PS_A[:, r * 512:(r + 1) * 512],
                            B4[:, dc * 1024 + oc * 128:dc * 1024 + oc * 128 + 128],
                            B3[:, dc * CW + CPAD + T * 512:dc * CW + CPAD + (T + 1) * 512],
                            start=(dc == 0), stop=(dc == DC - 1))
                    return inst
                sc.op("tensor", mms, waits=w, sets=f"q_mm_{T}_{oc}")
                ps = PS_A[:, r * 512:(r + 1) * 512]
                w0 = [f"q_mm_{T}_{oc}"]
                if oc >= 1:
                    w0.append(f"q_ev_{T}_{oc-1}")
                elif T > 0:
                    w0.append(f"q_ev_{T-1}_{DC-1}")
                sc.op("scalar", lambda e, ps=ps: e.activation(F(4096, 4608), ps,
                                                              AF.Relu, scale=-1.0),
                      waits=w0, sets=f"q_s0_{T}_{oc}")
                sc.op("scalar", lambda e: e.activation(F(5120, 5632), F(4096, 4608),
                                                       AF.Exp, scale=-1.0),
                      waits=[f"q_s0_{T}_{oc}"], sets=f"q_s1_{T}_{oc}")
                sc.op("scalar", lambda e, ps=ps: e.activation(F(6144, 6656), ps,
                                                              AF.Relu),
                      waits=[f"q_s1_{T}_{oc}"], sets=f"q_s2_{T}_{oc}")
                wf = [f"q_s2_{T}_{oc}"]
                if T == 0 and oc == 0:
                    wf.append("kv_done")
                if T > 0 and oc == 0:
                    wf.append(f"amm_{T-1}_3")

                def qev(e, oc=oc):
                    return e.tensor_tensor(B2[:, oc * 512:(oc + 1) * 512],
                                           F(5120, 5632), F(6144, 6656), ALU.add)
                sc.op("vector", qev, waits=wf, sets=f"q_ev_{T}_{oc}")
            for sub in range(4):
                w = [f"q_ev_{T}_{DC-1}", "kv_done"]
                if sub > 0:
                    w.append(f"attev_{T}_{sub-1}")
                elif T > 0:
                    w.append(f"r1s_{T*4-1}")

                def amm(e, T=T, sub=sub):
                    inst = None
                    for p in range(NP):
                        inst = e.matmul(
                            PS_B[:, p * PACK:p * PACK + 130],
                            B2[:, p * 512 + sub * 128:p * 512 + sub * 128 + 128],
                            KVM[:, p * 130:(p + 1) * 130],
                            start=True, stop=True)
                    return inst
                sc.op("tensor", amm, waits=w, sets=f"amm_{T}_{sub}")

                def rdg(e):
                    src = bass.AP(PS_B.ap().tensor, 128, [[2048, 128], [PACK, 8], [1, 2]])
                    sc.edge("vector", e, e.tensor_scalar(col("rd", 16), src, 1e-6, None,
                                                         ALU.max))
                    return e.reciprocal(col("rd", 16), col("rd", 16))
                sc.op("vector", rdg, waits=[f"amm_{T}_{sub}"], sets=f"rdr_{T}_{sub}")
                wv = [f"rdr_{T}_{sub}"]
                if T > 0 and sub == 0:
                    wv.append(f"a2_{T-1}_done")

                def adiv(e, sub=sub):
                    src = bass.AP(PS_B.ap().tensor, 0,
                                  [[2048, 128], [PACK, 8], [64, 2], [1, 64]])
                    co, _ = CL["rd"]
                    rdx = bass.AP(COLS.ap().tensor, co,
                                  [[COLW, 128], [2, 8], [1, 2], [0, 64]])
                    return e.tensor_tensor(B2[:, 4096 + sub * 1024:4096 + (sub + 1) * 1024],
                                           src, rdx, ALU.mult)
                sc.op("vector", adiv, waits=wv, sets=f"attev_{T}_{sub}")
            for k in range(DC * 4):
                ec, sub = divmod(k, 4)
                r = k % 6

                def tr(e, ec=ec, sub=sub, r=r):
                    return e.transpose(TRP6[r],
                                       B2[:, 4096 + sub * 1024 + ec * 128:
                                          4096 + sub * 1024 + ec * 128 + 128],
                                       IDB[:, :])
                w = [f"attev_{T}_{sub}", f"attev_{T}_3"]
                if k >= 6:
                    w.append(f"a2_{T}_{k-6}")
                sc.op("tensor", tr, waits=w, sets=f"a2tr_{T}_{k}")
                wf = [f"a2tr_{T}_{k}", "conv_done", "ld_cb"]
                if T > 0 and k == 0:
                    wf.append(f"pmm_{T-1}_3")

                def fuse(e, ec=ec, sub=sub, r=r, T=T):
                    return e.tensor_tensor(
                        B2[:, 8192 + ec * 512 + sub * 128:
                           8192 + ec * 512 + sub * 128 + 128],
                        B1[:, ec * S + T * 512 + sub * 128:
                           ec * S + T * 512 + sub * 128 + 128],
                        TRP6[r], ALU.add)
                sc.op("vector", fuse, waits=wf, sets=f"a2_{T}_{k}")
            sc.events[f"a2_{T}_done"] = sc.events[f"a2_{T}_{DC*4-1}"]
            for sub in range(4):
                t128 = T * 4 + sub
                xsl = t128 % 2
                wld = []
                if t128 >= 2:
                    wld.append(f"r1s_{t128-2}")
                sc.dma("sync",
                       lambda e, t128=t128, xsl=xsl: e.dma_start(
                           out=F(8192 + xsl * 1024, 9216 + xsl * 1024),
                           in_=x_d[t128 * 128:(t128 + 1) * 128, :]),
                       waits=wld, sets=f"xld_{t128}", sem=f"d_x{xsl}")
                r = sub % 2
                ps = PS_A[:, 1024:2048] if r == 0 else PS_B[:, 0:1024]
                w = [f"a2_{T}_done", "awrdy_3", "bpl", "one"]
                if sub >= 2:
                    w.append(f"r1s_{t128-2}")
                if T > 0 and sub < 2:
                    w.append(f"r1s_{(T-1)*4 + sub + 2}")
                if T == 0 and sub == 0:
                    w.append(f"kv_ev_{NP-1}")

                def pmm(e, sub=sub, ps=ps):
                    for ec in range(DC):
                        e.matmul(ps[:, 0:512],
                                 B2[:, 8192 + ec * 512 + sub * 128:
                                    8192 + ec * 512 + sub * 128 + 128],
                                 B4[:, 8192 + ec * 1024:8192 + ec * 1024 + 512],
                                 start=(ec == 0), stop=False)
                        e.matmul(ps[:, 512:1024],
                                 B2[:, 8192 + ec * 512 + sub * 128:
                                    8192 + ec * 512 + sub * 128 + 128],
                                 B4[:, 8192 + ec * 1024 + 512:8192 + ec * 1024 + 1024],
                                 start=(ec == 0), stop=False)
                    e.matmul(ps[:, 0:512], ONE[:, :], BPH[0:1, 0:512],
                             start=False, stop=False)
                    e.matmul(ps[:, 512:1024], ONE[:, :], BPH[0:1, 512:1024],
                             start=False, stop=False)
                    e.matmul(ps[:, 0:512], ONE[:, :], BPL[0:1, 0:512],
                             start=False, stop=True)
                    return e.matmul(ps[:, 512:1024], ONE[:, :], BPL[0:1, 512:1024],
                                    start=False, stop=True)
                sc.op("tensor", pmm, waits=w, sets=f"pmm_{T}_{sub}")
                if sub == 3:
                    sc.events[f"pmm_{T}_3"] = sc.events[f"pmm_{T}_{sub}"]

                def r1f(e, ps=ps, t128=t128):
                    xsl = t128 % 2
                    return e.scalar_tensor_tensor(F(7168, 8192), ps, 1.0,
                                                  F(8192 + xsl * 1024,
                                                    9216 + xsl * 1024),
                                                  ALU.mult, ALU.add,
                                                  accum_out=col(f"sr_{t128 % 4}"))
                sc.op("vector", r1f, waits=[f"pmm_{T}_{sub}", f"xld_{t128}"],
                      sets=f"r1s_{t128}")
                sc.op("scalar",
                      lambda e, t128=t128: e.activation(F(4096, 5120), F(7168, 8192),
                                                        AF.Square,
                                                        accum_out=col(f"sq_{t128 % 4}")),
                      waits=[f"r1s_{t128}"], sets=f"r1q_{t128}")

                def stats(e, t128=t128):
                    j = t128 % 4
                    sc.edge("vector", e, e.tensor_scalar(col(f"m_{j}"), col(f"sr_{j}"), 1.0 / D, None, ALU.mult))
                    sc.edge("vector", e, e.tensor_scalar(col(f"e2_{j}"), col(f"sq_{j}"), 1.0 / D, None, ALU.mult))
                    sc.edge("vector", e, e.tensor_tensor(col(f"mm_{j}"), col(f"m_{j}"), col(f"m_{j}"), ALU.mult))
                    sc.edge("vector", e, e.tensor_tensor(col(f"v_{j}"), col(f"e2_{j}"), col(f"mm_{j}"), ALU.subtract))
                    return e.tensor_scalar(col(f"v_{j}"), col(f"v_{j}"), EPS, None, ALU.add)
                sc.op("vector", stats, waits=[f"r1q_{t128}"], sets=f"st_{t128}")

                def rstd1(e, t128=t128):
                    j = t128 % 4
                    sc.edge("scalar", e, e.activation(col(f"ln_{j}"), col(f"v_{j}"), AF.Ln))
                    return e.activation(col(f"r0_{j}"), col(f"ln_{j}"), AF.Exp, scale=-0.5)
                sc.op("scalar", rstd1, waits=[f"st_{t128}"], sets=f"rs0_{t128}")

                def rstd2(e, t128=t128):
                    j = t128 % 4
                    sc.edge("vector", e, e.tensor_tensor(col(f"t1_{j}"), col(f"r0_{j}"), col(f"r0_{j}"), ALU.mult))
                    sc.edge("vector", e, e.tensor_tensor(col(f"t1_{j}"), col(f"t1_{j}"), col(f"v_{j}"), ALU.mult))
                    sc.edge("vector", e, e.tensor_scalar(col(f"t1_{j}"), col(f"t1_{j}"), -0.5, 1.5, ALU.mult, ALU.add))
                    sc.edge("vector", e, e.tensor_tensor(col(f"rstd_{j}"), col(f"r0_{j}"), col(f"t1_{j}"), ALU.mult))
                    sc.edge("vector", e, e.tensor_tensor(col(f"nm_{j}"), col(f"m_{j}"), col(f"rstd_{j}"), ALU.mult))
                    return e.tensor_scalar(col(f"nm_{j}"), col(f"nm_{j}"), -1.0, None,
                                           ALU.mult)
                sc.op("vector", rstd2, waits=[f"rs0_{t128}"], sets=f"rs_{t128}")

                def yslot(t):
                    return F((t % 2) * 1024, (t % 2) * 1024 + 1024)

                def yap(e, t128=t128):
                    j = t128 % 4
                    return e.activation(yslot(t128), F(7168, 8192), AF.Identity,
                                        bias=col(f"nm_{j}"), scale=col(f"rstd_{j}"))
                w2 = [f"rs_{t128}"]
                if t128 >= 2:
                    w2.append(f"ymx_{t128-2}")
                else:
                    w2.append("conv_done")
                sc.op("scalar", yap, waits=w2, sets=f"y_{t128}")
                sc.dma("gpsimd",
                       lambda e, t128=t128: e.dma_start(
                           out=y_sp[t128 * 128:(t128 + 1) * 128, :], in_=yslot(t128)),
                       waits=[f"y_{t128}"], sets=f"yst_{t128}", sem=f"d_y{t128 % 2}")
                sc.op("vector",
                      lambda e, t128=t128: e.tensor_reduce(
                          COLS[:, CL["ymxs"][0] + t128:CL["ymxs"][0] + t128 + 1],
                          yslot(t128), AX.X, ALU.max, apply_absolute_value=True),
                      waits=[f"y_{t128}", f"yst_{t128}"], sets=f"ymx_{t128}")
            if mid is not None:
                mid(T)

    def allreduce_cols(tag, srcs, waits, ncols):
        wz = list(waits)
        if last_ccr[0] is not None:
            wz.append(last_ccr[0])
        sc.op("gpsimd", lambda e: e.memset(CCR[:, :], 0.0), waits=wz, sets=f"ccz_{tag}")
        for i, cn in enumerate(srcs):
            wl = [f"ccz_{tag}"]
            if i > 0:
                wl.append(f"ccr_{tag}_{i-1}")
            sc.dma("gpsimd",
                   lambda e, cn=cn: e.dma_start(out=mx_d[:],
                                                in_=COLS[:, CL[cn][0]:CL[cn][0] + 1]),
                   waits=wl, sets=f"cca_{tag}_{i}", sem="d_cc")
            mx_src = bass.AP(mx_d, 0, [[0, 1], [1, 128]])
            sc.dma("gpsimd", lambda e: e.dma_start(out=ROWB[:, :], in_=mx_src),
                   waits=[f"cca_{tag}_{i}"], sets=f"ccb_{tag}_{i}", sem="d_cc")
            sc.op("vector",
                  lambda e, i=i: e.tensor_reduce(CCR[0:1, i:i + 1], ROWB[:, :], AX.X,
                                                 ALU.max),
                  waits=[f"ccb_{tag}_{i}"], sets=f"ccr_{tag}_{i}")
        last_ccr[0] = f"ccr_{tag}_{len(srcs)-1}"
        sc.dma("gpsimd", lambda e: e.dma_start(out=cc_in[:, :], in_=CCR[:, :]),
               waits=[f"ccr_{tag}_{len(srcs)-1}"], sets=f"cci_{tag}", sem="d_cc")
        sc.op("gpsimd",
              lambda e: e.collective_compute("AllReduce", ALU.max,
                                             replica_groups=[list(range(8))],
                                             ins=[cc_in.ap()], outs=[cc_out.ap()]),
              waits=[f"cci_{tag}"], sets=f"cc_{tag}", sem="ccs", amt=1)
        for i in range(ncols):
            src = bass.AP(cc_out, i, [[0, 128], [1, 1]])
            sc.dma("gpsimd",
                   lambda e, i=i, src=src, tag=tag: e.dma_start(
                       out=COLS[:, CL[f"g{tag}_{i}"][0]:CL[f"g{tag}_{i}"][0] + 1],
                       in_=src),
                   waits=[f"cc_{tag}"], sets=f"gbc_{tag}_{i}", sem="d_cc")

    def p2():
        sc.op("vector",
              lambda e: e.tensor_reduce(col("ymxa"),
                                        COLS[:, CL["ymxs"][0]:CL["ymxs"][0] + NT1],
                                        AX.X, ALU.max),
              waits=[f"ymx_{NT1-1}"], sets="ymxa")
        allreduce_cols("0", ["ymxa"], ["ymxa"], 1)

        def sxf(e):
            sc.edge("vector", e, e.tensor_scalar(col("sx"), col("g0_0"), 1.0 / 32767.0, 1e-12, ALU.mult, ALU.max))
            return e.reciprocal(col("rsx"), col("sx"))
        sc.op("vector", sxf, waits=["gbc_0_0"], sets="sx")
        for t in range(NT1):
            sl = t % 2
            wl = []
            if t >= 2:
                wl.append(f"ixr_{t-2}")
            else:
                wl.append("ymxa")
                wl.append(f"yst_{NT1-1}")
                wl.append(f"yst_{NT1-2}")
            sc.dma("sync",
                   lambda e, t=t, sl=sl: e.dma_start(
                       out=F(sl * 1024, (sl + 1) * 1024),
                       in_=y_sp[t * 128:(t + 1) * 128, :]),
                   waits=wl, sets=f"yld_{t}", sem=f"d_yl{sl}")

            def rnd(e, sl=sl):
                sc.edge("vector", e, e.tensor_scalar(F(2048, 3072), F(sl * 1024, (sl + 1) * 1024),
                                col("rsx"), MAGIC, ALU.mult, ALU.add))
                return e.tensor_scalar(F(3072, 4096), F(2048, 3072), -MAGIC, None, ALU.add)
            wr = [f"yld_{t}", "sx"]
            if t >= 1:
                wr.append(f"ixb_{t-1}")
            sc.op("vector", rnd, waits=wr, sets=f"ixr_{t}")
            wb2 = [f"ixr_{t}"]
            if t >= 1:
                wb2.append(f"ixtr_{t-1}_{DC-1}")
            else:
                wb2.append(f"Wtr_{NUC-1}_{DC-1}")
            sc.op("vector", lambda e: e.tensor_copy(WQB[:, 0:1024], F(3072, 4096)),
                  waits=wb2, sets=f"ixb_{t}")
            for dc in range(DC):
                k = t * DC + dc
                r = k % 8

                def tr(e, dc=dc, r=r):
                    return e.transpose(TRP8[r], WQB[:, dc * 128:(dc + 1) * 128], IDB[:, :])
                w = [f"ixb_{t}"]
                if k >= 8:
                    w.append(f"ixev_{k-8}")
                sc.op("tensor", tr, waits=w, sets=f"ixtr_{t}_{dc}")

                def ev(e, t=t, dc=dc, r=r):
                    return e.tensor_copy(B2[:, dc * S + t * 128:dc * S + t * 128 + 128],
                                         TRP8[r])
                sc.op("vector", ev, waits=[f"ixtr_{t}_{dc}"], sets=f"ixev_{k}")
        sc.events["ix_done"] = sc.events[f"ixev_{NT1*DC-1}"]

    def p3_one(mat, sp, tag, mcols):
        for ic in range(IC):
            u = mat * IC + ic
            row = ((u % 8) * NUC + u // 8) * 128
            rsl = ic % 4
            wl = ["ag2"]
            if ic >= 4:
                wl.append(f"{tag}mm_{ic-4}")
            if mat == 0 and ic < 4:
                wl.append(f"q_mm_{T512-1}_{DC-1}")
            if mat == 1 and ic < 4:
                wl.append(f"u_mm_{IC-4+ic}")
            sc.dma("sync",
                   lambda e, row=row, rsl=rsl: e.dma_start(
                       out=B4[:, rsl * 1024:(rsl + 1) * 1024],
                       in_=agout[row:row + 128, :]),
                   waits=wl, sets=f"{tag}wt_{ic}", sem=f"d_wn{rsl}")
            pr = ic % 2
            PS = PS_A if pr == 0 else PS_B
            w5 = [f"{tag}wt_{ic}", "ix_done"]
            if ic >= 2:
                w5.append(f"{tag}ev_{ic-2}")
                w5.append(f"{tag}mx_{ic-2}")
            if mat == 0 and ic < 2:
                w5.append(f"r1s_{S//128-1}")
            if mat == 1 and ic < 2:
                w5.append(f"u_ev_{IC-2+ic}")
                w5.append(f"u_mx_{IC-2+ic}")

            def mms(e, rsl=rsl, PS=PS):
                inst = None
                for dc in range(DC):
                    for tt in range(S // 512):
                        inst = e.matmul(
                            PS[:, tt * 512:(tt + 1) * 512],
                            B4[:, rsl * 1024 + dc * 128:rsl * 1024 + dc * 128 + 128],
                            B2[:, dc * S + tt * 512:dc * S + (tt + 1) * 512],
                            start=(dc == 0), stop=(dc == DC - 1))
                return inst
            sc.op("tensor", mms, waits=w5, sets=f"{tag}mm_{ic}")
            esl = ic % 2
            w6 = [f"{tag}mm_{ic}"]
            if ic >= 2:
                w6.append(f"{tag}sp_{ic-2}")
            if mat == 1 and ic < 2:
                w6.append(f"u_sp_{IC-2+ic}")
            sc.op("scalar",
                  lambda e, esl=esl, PS=PS: e.copy(
                      F(6144 + esl * 2048, 6144 + esl * 2048 + S), PS[:, 0:S]),
                  waits=w6, sets=f"{tag}ev_{ic}")
            sc.op("vector",
                  lambda e, ic=ic, esl=esl: e.tensor_reduce(
                      COLS[:, CL[mcols][0] + ic:CL[mcols][0] + ic + 1],
                      F(6144 + esl * 2048, 6144 + esl * 2048 + S), AX.X, ALU.max,
                      apply_absolute_value=True),
                  waits=[f"{tag}ev_{ic}"], sets=f"{tag}mx_{ic}")
            sc.dma("gpsimd",
                   lambda e, ic=ic, esl=esl, sp=sp: e.dma_start(
                       out=sp[ic * 128:(ic + 1) * 128, :],
                       in_=F(6144 + esl * 2048, 6144 + esl * 2048 + S)),
                   waits=[f"{tag}ev_{ic}"], sets=f"{tag}sp_{ic}", sem=f"d_{tag}sp{esl}")

    def p3():
        p3_one(0, u_sp, "u_", "umx")
        p3_one(1, g_sp, "g_", "gmx")
        sc.op("vector",
              lambda e: e.tensor_reduce(col("umxa"),
                                        COLS[:, CL["umx"][0]:CL["umx"][0] + IC],
                                        AX.X, ALU.max),
              waits=[f"u_mx_{IC-1}"], sets="umxa")
        sc.op("vector",
              lambda e: e.tensor_reduce(col("gmxa"),
                                        COLS[:, CL["gmx"][0]:CL["gmx"][0] + IC],
                                        AX.X, ALU.max),
              waits=[f"g_mx_{IC-1}"], sets="gmxa")

    def barrier_a():
        allreduce_cols("A", ["umxa", "gmxa"], ["umxa", "gmxa"], 2)

        def alg(e):
            sc.edge("vector", e, e.tensor_tensor(col("tmpc"), col("gA_0"), col("sx"), ALU.mult))
            sc.edge("vector", e, e.tensor_scalar(col("su"), col("tmpc"), 1.0 / 32767.0, 1e-12, ALU.mult, ALU.max))
            sc.edge("vector", e, e.reciprocal(col("fu"), col("su")))
            sc.edge("vector", e, e.tensor_tensor(col("fu"), col("fu"), col("sx"), ALU.mult))
            sc.edge("vector", e, e.tensor_tensor(col("tmpc"), col("gA_1"), col("sx"), ALU.mult))
            sc.edge("vector", e, e.tensor_scalar(col("sg"), col("tmpc"), 1.0 / 32767.0, 1e-12, ALU.mult, ALU.max))
            sc.edge("vector", e, e.reciprocal(col("fg"), col("sg")))
            return e.tensor_tensor(col("fg"), col("fg"), col("sx"), ALU.mult)
        sc.op("vector", alg, waits=["gbc_A_1", "sx"], sets="algA")

    def p2a():
        B3i = B3.bitcast(I16)
        B1f = B1.bitcast(F32)
        B4i2 = B4.bitcast(I16)

        def UR(q):
            return F(q * S, (q + 1) * S)

        def GR(q):
            return B1f[:, q * S:(q + 1) * S]

        def GI(q):
            return B3i[:, q * S:(q + 1) * S]

        def SQ(st):
            return B4i2[:, st * S:(st + 1) * S]

        def IUS(q):
            return I16R[:, q * S:(q + 1) * S]

        def loads(ic):
            q = ic % 4
            wl = []
            wlg = []
            if ic >= 4:
                wl.append(f"iu_{ic-4}")
                wlg.append(f"t2sp_{ic-4}")
            else:
                if ic < 2:
                    wl.append(f"ixb_{NT1-1}")
                else:
                    wl.append("umxa")
                    wl.append("gmxa")
                for q_ in (f"u_sp_{IC-1}", f"u_sp_{IC-2}",
                           f"g_sp_{IC-1}", f"g_sp_{IC-2}"):
                    wl.append(q_)
                    wlg.append(q_)
            sc.dma("sync",
                   lambda e, ic=ic, q=q: e.dma_start(
                       out=UR(q), in_=u_sp[ic * 128:(ic + 1) * 128, :]),
                   waits=wl, sets=f"uld_{ic}", sem=f"d_ul{ic % 4}")
            sc.dma("sync",
                   lambda e, ic=ic, q=q: e.dma_start(
                       out=GR(q), in_=g_sp[ic * 128:(ic + 1) * 128, :]),
                   waits=wlg, sets=f"gld_{ic}", sem=f"d_gl{ic % 4}")
            wdl_ops(ic)

        def casts(ic):
            q = ic % 4
            w0 = [f"uld_{ic}", "algA"]
            if ic >= 4:
                w0.append(f"iusp_{ic-4}")
            sc.op("vector",
                  lambda e, ic=ic, q=q: e.tensor_scalar(IUS(q), UR(q), col("fu"),
                                                        None, ALU.mult),
                  waits=w0, sets=f"iu_{ic}")
            w1 = [f"gld_{ic}", "algA"]
            if ic >= 8:
                w1.append(f"t2o_{ic-8}")
            sc.op("vector",
                  lambda e, ic=ic, q=q: e.tensor_scalar(GI(ic % 8), GR(q), col("fg"),
                                                        None, ALU.mult),
                  waits=w1, sets=f"ig_{ic}")
            sc.op("scalar",
                  lambda e, ic=ic, q=q: e.activation(GR(q), GI(ic % 8), AF.Sigmoid,
                                                     scale=col("sg")),
                  waits=[f"ig_{ic}"], sets=f"sg_{ic}")

        for ic0 in range(4):
            loads(ic0)
        casts(0)
        casts(1)
        for ic in range(IC):
            q = ic % 4
            st = ic % 2
            w2 = [f"sg_{ic}"]
            if ic >= 2:
                w2.append(f"t2g_{ic-2}")
            sc.op("vector",
                  lambda e, ic=ic, q=q, st=st: e.tensor_scalar(SQ(st), GR(q), 65536.0,
                                                               -32768.0, ALU.mult,
                                                               ALU.add),
                  waits=w2, sets=f"sq_{ic}")
            sc.op("gpsimd",
                  lambda e, ic=ic, q=q, st=st: e.tensor_tensor(GR(q), GI(ic % 8),
                                                               SQ(st), ALU.mult),
                  waits=[f"sq_{ic}"], sets=f"t2g_{ic}")
            sc.dma("gpsimd",
                   lambda e, ic=ic, q=q: e.dma_start(
                       out=iup_sp[ic * 128:(ic + 1) * 128, :], in_=IUS(q)),
                   waits=[f"iu_{ic}"], sets=f"iusp_{ic}", sem=f"d_iu{ic % 4}")
            if ic + 4 < IC:
                loads(ic + 4)
            if ic + 2 < IC:
                casts(ic + 2)
            sc.op("vector",
                  lambda e, ic=ic, q=q: e.scalar_tensor_tensor(
                      GR(q), GI(ic % 8), 32768.0, GR(q), ALU.mult, ALU.add),
                  waits=[f"t2g_{ic}"], sets=f"t2o_{ic}")
            sc.op("vector",
                  lambda e, ic=ic, q=q: e.tensor_reduce(
                      COLS[:, CL["tmx"][0] + ic:CL["tmx"][0] + ic + 1],
                      GR(q), AX.X, ALU.max, apply_absolute_value=True),
                  waits=[f"t2o_{ic}"], sets=f"t2_{ic}")
            sc.dma("gpsimd",
                   lambda e, ic=ic, q=q: e.dma_start(
                       out=g_sp[ic * 128:(ic + 1) * 128, :], in_=GR(q)),
                   waits=[f"t2_{ic}"], sets=f"t2sp_{ic}", sem=f"d_t2{ic % 4}")
        sc.op("vector",
              lambda e: e.tensor_reduce(col("tmxa"),
                                        COLS[:, CL["tmx"][0]:CL["tmx"][0] + IC],
                                        AX.X, ALU.max),
              waits=[f"t2_{IC-1}"], sets="tmxa")

    def wd_slot(ic):
        if ic < 12:
            return B4[:, 4096 + ic * 1024:4096 + (ic + 1) * 1024]
        return B2[:, (ic - 12) * 1024:(ic - 11) * 1024]

    def p_wdeq():
        pass

    def wdl_ops(j):
        oc, s3 = divmod(j, 3)
        u = 48 + j
        row = ((u % 8) * NUC + u // 8) * 128
        for dc in range(DC):
            ic = s3 * 8 + dc
            dst = wd_slot(ic)
            wl = []
            if j == 0 and dc == 0:
                wl = ["algA"]
            if j == 1 and dc == 4:
                wl = [f"g_mm_{IC-1}"]
            sc.dma("scalar",
                   lambda e, row=row, dc=dc, dst=dst, oc=oc: e.dma_start(
                       out=dst[:, oc * 128:(oc + 1) * 128],
                       in_=agout[row:row + 128, dc * 128:(dc + 1) * 128]),
                   waits=wl, sets=f"wdl_{j}_{dc}", sem=f"d_wdl{j % 4}")

    def wdl_fin():
        sc.op("scalar", lambda e: e.copy(col("cjunk"), CWT[:, 0:1]),
              waits=[f"wdl_{j}_{DC-1}" for j in range(20, 24)] + ["ld_cw"],
              sets="wd_ready")

    def barrier_b():
        allreduce_cols("B", ["tmxa"], ["tmxa"], 1)

        def alg(e):
            sc.edge("vector", e, e.tensor_scalar(col("sg16"), col("sg"), 1.0 / 65536.0, None, ALU.mult))
            sc.edge("vector", e, e.tensor_tensor(col("tmpc"), col("gB_0"), col("sg16"), ALU.mult))
            sc.edge("vector", e, e.tensor_scalar(col("st2"), col("tmpc"), 1.0 / 32767.0, 1e-12, ALU.mult, ALU.max))
            sc.edge("vector", e, e.reciprocal(col("ft"), col("st2")))
            return e.tensor_tensor(col("ft"), col("ft"), col("sg16"), ALU.mult)
        sc.op("vector", alg, waits=["gbc_B_0", "algA"], sets="algB")

    def p2c():
        B3i = B3.bitcast(I16)
        B1f = B1.bitcast(F32)

        def FR(q):
            return F(q * S, (q + 1) * S)

        def IUP(q):
            return B3i[:, q * S:(q + 1) * S]

        def PRD(q):
            return B1f[:, q * S:(q + 1) * S]

        def IG2(q):
            return I16R[:, q * S:(q + 1) * S]

        def loads2(ic):
            q = ic % 4
            wl = [f"t2sp_{ic}"]
            if ic >= 4:
                wl.append(f"c_cast_{ic-4}")
            else:
                wl.append(f"iu_{IC-4+ic}")
            sc.dma("sync",
                   lambda e, ic=ic, q=q: e.dma_start(
                       out=FR(q), in_=g_sp[ic * 128:(ic + 1) * 128, :]),
                   waits=wl, sets=f"t2ld_{ic}", sem=f"d_cig{ic % 4}")
            wu_ = [f"iusp_{ic}"]
            if ic >= 4:
                wu_.append(f"otg_{ic-4}")
            else:
                wu_.append(f"t2o_{IC-8+ic}")
                wu_.append(f"t2o_{IC-4+ic}")
            sc.dma("sync",
                   lambda e, ic=ic, q=q: e.dma_start(out=IUP(q),
                                                     in_=iup_sp[ic * 128:(ic + 1) * 128, :]),
                   waits=wu_, sets=f"iuld_{ic}", sem=f"d_ciu{ic % 4}")

        def ccast(ic):
            q = ic % 4
            w0 = [f"t2ld_{ic}", "algB"]
            if ic >= 4:
                w0.append(f"ig2s_{ic-4}")
                w0.append(f"otg_{ic-4}")
            else:
                w0.append(f"iusp_{IC-4+ic}")
            sc.op("vector",
                  lambda e, ic=ic, q=q: e.tensor_scalar(IG2(q), FR(q), col("ft"),
                                                        None, ALU.mult),
                  waits=w0, sets=f"c_cast_{ic}")

        for ic0 in range(4):
            loads2(ic0)
        ccast(0)
        for ic in range(IC):
            q = ic % 4
            sc.dma("gpsimd",
                   lambda e, ic=ic, q=q: e.dma_start(
                       out=ig2_sp[ic * 128:(ic + 1) * 128, :], in_=IG2(q)),
                   waits=[f"c_cast_{ic}"], sets=f"ig2s_{ic}", sem=f"d_g2{ic % 4}")
            w1 = [f"c_cast_{ic}", f"iuld_{ic}"]
            if ic >= 4:
                w1.append(f"ot_{ic-4}")
            else:
                w1.append(f"t2sp_{IC-4+ic}")
            sc.op("gpsimd",
                  lambda e, ic=ic, q=q: e.tensor_tensor(PRD(q), IG2(q), IUP(q),
                                                        ALU.mult),
                  waits=w1, sets=f"otg_{ic}")
            if ic + 4 < IC:
                loads2(ic + 4)
            if ic + 1 < IC:
                ccast(ic + 1)
            sc.op("vector",
                  lambda e, ic=ic, q=q: e.tensor_reduce(
                      COLS[:, CL["omx"][0] + ic:CL["omx"][0] + ic + 1],
                      PRD(q), AX.X, ALU.max, apply_absolute_value=True),
                  waits=[f"otg_{ic}"], sets=f"ot_{ic}")
        sc.op("vector",
              lambda e: e.tensor_reduce(col("omxa"),
                                        COLS[:, CL["omx"][0]:CL["omx"][0] + IC],
                                        AX.X, ALU.max),
              waits=[f"ot_{IC-1}"], sets="omxa")

    def barrier_c():
        allreduce_cols("C", ["omxa"], ["omxa"], 1)

        def alg(e):
            sc.edge("vector", e, e.tensor_tensor(col("pso"), col("st2"), col("su"), ALU.mult))
            sc.edge("vector", e, e.tensor_tensor(col("tmpc"), col("gC_0"), col("pso"), ALU.mult))
            sc.edge("vector", e, e.tensor_scalar(col("so"), col("tmpc"), 1.0 / 32767.0, 1e-12, ALU.mult, ALU.max))
            sc.edge("vector", e, e.reciprocal(col("fo"), col("so")))
            return e.tensor_tensor(col("fo"), col("fo"), col("pso"), ALU.mult)
        sc.op("vector", alg, waits=["gbc_C_0", "algB"], sets="algC")

    def p2d():
        B4f = B4.bitcast(F32)

        def io_chain(sb):
            IOB = B3 if sb % 2 == 0 else B1
            for b in range(IC // 4):
                kb = sb * (IC // 4) + b
                bsl = b % 2

                def b_src(sp, sb=sb, b=b):
                    return bass.AP(sp, (b * 4) * 128 * S + sb * 512,
                                   [[S, 128], [128 * S, 4], [1, 512]])
                wl = []
                if kb >= 2:
                    wl.append(f"dot_{kb-2}")
                else:
                    for nm_ in (f"ig2s_{IC-4}", f"ig2s_{IC-3}", f"ig2s_{IC-2}",
                                f"ig2s_{IC-1}", f"otg_{IC-1}"):
                        wl.append(nm_)
                sc.dma("sync",
                       lambda e, sb=sb, b=b, bsl=bsl: e.dma_start(
                           out=I16R[:, bsl * 2048:bsl * 2048 + 2048],
                           in_=b_src(ig2_sp, sb, b)),
                       waits=wl, sets=f"g2ld_{kb}", sem=f"d_dg{kb % 4}")
                sc.dma("sync",
                       lambda e, sb=sb, b=b, bsl=bsl: e.dma_start(
                           out=I16R[:, 4096 + bsl * 2048:4096 + bsl * 2048 + 2048],
                           in_=b_src(iup_sp, sb, b)),
                       waits=wl, sets=f"duld_{kb}", sem=f"d_du{kb % 4}")
                w1 = [f"g2ld_{kb}", f"duld_{kb}"]
                if kb >= 2:
                    w1.append(f"iot_{kb-2}")
                else:
                    w1.append(f"c_cast_{IC-3+kb}")
                sc.op("gpsimd",
                      lambda e, bsl=bsl: e.tensor_tensor(
                          F(2048 + bsl * 2048, 4096 + bsl * 2048),
                          I16R[:, bsl * 2048:bsl * 2048 + 2048],
                          I16R[:, 4096 + bsl * 2048:4096 + bsl * 2048 + 2048],
                          ALU.mult),
                      waits=w1, sets=f"dot_{kb}")
                w2 = [f"dot_{kb}", "algC"]
                if kb >= 1:
                    w2.append(f"io_{kb-1}")
                sc.op("scalar",
                      lambda e, bsl=bsl: e.activation(
                          B4f[:, 0:2048], F(2048 + bsl * 2048, 4096 + bsl * 2048),
                          AF.Copy, bias=MAGIC, scale=col("fo")),
                      waits=w2, sets=f"iot_{kb}")
                w3 = [f"iot_{kb}"]
                if sb >= 2 and b == 0:
                    w3.append(f"dmm_{sb-2}_3")
                sc.op("vector",
                      lambda e, IOB=IOB, b=b: e.tensor_scalar(
                          IOB[:, b * 2048:(b + 1) * 2048], B4f[:, 0:2048], -MAGIC,
                          None, ALU.add),
                      waits=w3, sets=f"io_{kb}")

        def mm_ln(sb):
            IOB = B3 if sb % 2 == 0 else B1
            for sub in range(4):
                t128 = sb * 4 + sub
                r = sub % 2
                ps = PS_A[:, 0:1024] if r == 0 else PS_A[:, 1024:2048]
                w = [f"io_{sb*(IC//4)+IC//4-1}", "wd_ready"]
                if sub >= 2:
                    w.append(f"r2s_{t128-2}")
                if sb > 0 and sub < 2:
                    w.append(f"r2s_{(sb-1)*4 + sub + 2}")

                def dmm(e, IOB=IOB, sub=sub, ps=ps):
                    inst = None
                    for ic in range(IC):
                        wt = wd_slot(ic)
                        e.matmul(ps[:, 0:512],
                                 IOB[:, ic * 512 + sub * 128:ic * 512 + sub * 128 + 128],
                                 wt[:, 0:512], start=(ic == 0), stop=(ic == IC - 1))
                        inst = e.matmul(ps[:, 512:1024],
                                        IOB[:, ic * 512 + sub * 128:
                                            ic * 512 + sub * 128 + 128],
                                        wt[:, 512:1024], start=(ic == 0),
                                        stop=(ic == IC - 1))
                    return inst
                sc.op("tensor", dmm, waits=w, sets=f"dmm_{sb}_{sub}")
                if sub == 3:
                    sc.events[f"dmm_{sb}_3"] = sc.events[f"dmm_{sb}_{sub}"]
                ysl = t128 % 2
                wld = []
                if t128 >= 2:
                    wld.append(f"r2s_{t128-2}")
                else:
                    wld.append(f"c_cast_{IC-4}")
                sc.dma("sync",
                       lambda e, t128=t128, ysl=ysl: e.dma_start(
                           out=F(ysl * 1024, (ysl + 1) * 1024),
                           in_=y_sp[t128 * 128:(t128 + 1) * 128, :]),
                       waits=wld, sets=f"yld2_{t128}", sem=f"d_y2{ysl}")
                wr2 = [f"dmm_{sb}_{sub}", f"yld2_{t128}", "algC"]
                if t128 >= 1:
                    wr2.append(f"r2_{t128-1}")

                def r2f(e, ps=ps, ysl=ysl, t128=t128):
                    return e.scalar_tensor_tensor(F(6144, 7168), ps, col("so"),
                                                  F(ysl * 1024, (ysl + 1) * 1024),
                                                  ALU.mult, ALU.add,
                                                  accum_out=col(f"sr_{t128 % 4}"))
                sc.op("vector", r2f, waits=wr2, sets=f"r2s_{t128}")
                sc.op("scalar",
                      lambda e, t128=t128: e.activation(F(7168, 8192), F(6144, 7168),
                                                        AF.Square,
                                                        accum_out=col(f"sq_{t128 % 4}")),
                      waits=[f"r2s_{t128}"], sets=f"r2q_{t128}")

                def stats(e, t128=t128):
                    j = t128 % 4
                    sc.edge("vector", e, e.tensor_scalar(col(f"m_{j}"), col(f"sr_{j}"), 1.0 / D, None, ALU.mult))
                    sc.edge("vector", e, e.tensor_scalar(col(f"e2_{j}"), col(f"sq_{j}"), 1.0 / D, None, ALU.mult))
                    sc.edge("vector", e, e.tensor_tensor(col(f"mm_{j}"), col(f"m_{j}"), col(f"m_{j}"), ALU.mult))
                    sc.edge("vector", e, e.tensor_tensor(col(f"v_{j}"), col(f"e2_{j}"), col(f"mm_{j}"), ALU.subtract))
                    return e.tensor_scalar(col(f"v_{j}"), col(f"v_{j}"), EPS, None, ALU.add)
                sc.op("vector", stats, waits=[f"r2q_{t128}"], sets=f"st2_{t128}")

                def rstd1(e, t128=t128):
                    j = t128 % 4
                    sc.edge("scalar", e, e.activation(col(f"ln_{j}"), col(f"v_{j}"), AF.Ln))
                    return e.activation(col(f"r0_{j}"), col(f"ln_{j}"), AF.Exp, scale=-0.5)
                sc.op("scalar", rstd1, waits=[f"st2_{t128}"], sets=f"rs20_{t128}")

                def rstd2(e, t128=t128):
                    j = t128 % 4
                    sc.edge("vector", e, e.tensor_tensor(col(f"t1_{j}"), col(f"r0_{j}"), col(f"r0_{j}"), ALU.mult))
                    sc.edge("vector", e, e.tensor_tensor(col(f"t1_{j}"), col(f"t1_{j}"), col(f"v_{j}"), ALU.mult))
                    sc.edge("vector", e, e.tensor_scalar(col(f"t1_{j}"), col(f"t1_{j}"), -0.5, 1.5, ALU.mult, ALU.add))
                    sc.edge("vector", e, e.tensor_tensor(col(f"rstd_{j}"), col(f"r0_{j}"), col(f"t1_{j}"), ALU.mult))
                    sc.edge("vector", e, e.tensor_tensor(col(f"nm_{j}"), col(f"m_{j}"), col(f"rstd_{j}"), ALU.mult))
                    return e.tensor_scalar(col(f"nm_{j}"), col(f"nm_{j}"), -1.0, None,
                                           ALU.mult)
                sc.op("vector", rstd2, waits=[f"rs20_{t128}"], sets=f"rs2_{t128}")
                osl = t128 % 2

                def oap(e, t128=t128, osl=osl):
                    j = t128 % 4
                    return e.activation(F(8192 + osl * 1024, 9216 + osl * 1024),
                                        F(6144, 7168), AF.Identity,
                                        bias=col(f"nm_{j}"), scale=col(f"rstd_{j}"))
                w3 = [f"rs2_{t128}"]
                if t128 >= 2:
                    w3.append(f"ost_{t128-2}")
                sc.op("scalar", oap, waits=w3, sets=f"r2_{t128}")
                sc.dma("sync",
                       lambda e, t128=t128, osl=osl: e.dma_start(
                           out=out_d[t128 * 128:(t128 + 1) * 128, :],
                           in_=F(8192 + osl * 1024, 9216 + osl * 1024)),
                       waits=[f"r2_{t128}"], sets=f"ost_{t128}", sem=f"d_out{t128 % 2}")
        for sb in range(T512):
            io_chain(sb)
            if sb >= 1:
                mm_ln(sb - 1)
        mm_ln(T512 - 1)
        nst = S // 128

        def finw(e):
            e.wait_ge(sc.sems["d_out0"], 16 * ((nst + 1) // 2))
            return e.wait_ge(sc.sems["d_out1"], 16 * (nst // 2))
        sc.ops["sync"].append(([], finw, None))
        sc.counts.setdefault("d_out0", 0)
        sc.counts.setdefault("d_out1", 0)

    p0()
    p0_weights([0, 1])
    p1_xbfm()
    proj_tm(B1, 0, True, "kb")
    proj_tm(B2, 1, False, "v")
    p1_kstate_finish()
    p1_kv()
    p0_weights([2, 3])
    def _pw_mid(T):
        if T512 == 1:
            phase_w_chunk(0, NUC)
            phase_w_tail()
        elif T == 0:
            phase_w_chunk(0, 3)
        elif T == 1:
            phase_w_chunk(3, 6)
        elif T == 2:
            phase_w_chunk(6, NUC)
            phase_w_tail()

    p1_conv()
    p1_qattn(mid=_pw_mid)
    p2()
    p3()
    barrier_a()
    p2a()
    wdl_fin()
    barrier_b()
    p2c()
    barrier_c()
    p2d()

    assert _next[0] <= COLW, f"COLS overflow: {_next[0]}"
    sc.emit(stack)
    stack.close()
    return nc


_NC_CACHE = {}


def get_nc(S):
    if S not in _NC_CACHE:
        _NC_CACHE[S] = build_nc(S)
    return _NC_CACHE[S]


def kernel(L0_attn_x, L0_attn_Wq, L0_attn_Wk, L0_attn_Wv, L0_attn_Wp, L0_attn_bp,
           L0_attn_conv_w, L0_attn_conv_b, L0_ffn_w_gate, L0_ffn_w_up, L0_ffn_w_down):
    x = np.ascontiguousarray(np.asarray(L0_attn_x, dtype=np.float32))
    B, S, _ = x.shape
    nc = get_nc(S)
    w = {
        "wq": np.ascontiguousarray(L0_attn_Wq, dtype=np.float32),
        "wk": np.ascontiguousarray(L0_attn_Wk, dtype=np.float32),
        "wv": np.ascontiguousarray(L0_attn_Wv, dtype=np.float32),
        "wp": np.ascontiguousarray(L0_attn_Wp, dtype=np.float32),
        "bp": np.ascontiguousarray(L0_attn_bp, dtype=np.float32),
        "cw": np.ascontiguousarray(L0_attn_conv_w, dtype=np.float32),
        "cb": np.ascontiguousarray(L0_attn_conv_b, dtype=np.float32),
    }
    wu = np.asarray(L0_ffn_w_up, dtype=np.float32)
    wg = np.asarray(L0_ffn_w_gate, dtype=np.float32)
    wdn = np.asarray(L0_ffn_w_down, dtype=np.float32)

    def unit(u):
        if u < 24:
            return wu[u * 128:(u + 1) * 128, :]
        if u < 48:
            return wg[(u - 24) * 128:(u - 23) * 128, :]
        j = u - 48
        oc, s3 = divmod(j, 3)
        return wdn[oc * 128:(oc + 1) * 128, s3 * 1024:(s3 + 1) * 1024]

    in_maps = []
    for b in range(B):
        ws = np.ascontiguousarray(
            np.concatenate([unit(b + 8 * p) for p in range(9)], axis=0))
        in_maps.append(dict(w, x=np.ascontiguousarray(x[b]), ws=ws))
    res = run_bass_kernel_spmd(nc, in_maps, core_ids=list(range(B)))
    return np.stack([np.asarray(r["out"]) for r in res.results], axis=0)


# version 34 (final)
# speedup vs baseline: 1.0298x; 1.0078x over previous
from contextlib import ExitStack
import numpy as np
import concourse.bass as bass
import concourse.mybir as mybir
from concourse.bass_utils import run_bass_kernel_spmd

F32 = mybir.dt.float32
BF16 = mybir.dt.bfloat16
I16 = mybir.dt.int16
U16 = mybir.dt.uint16
ALU = mybir.AluOpType
AF = mybir.ActivationFunctionType
AX = mybir.AxisListType

MAGIC = float(np.float32(1.5 * 2.0 ** 23))
D = 1024
DC = 8
NP = 8
I3 = 3072
IC = 24
PACK = 176
EPS = 1e-5
CPAD = 2
CW = 2048 + 2 * CPAD
COLW = 336

ENGINES = ("sync", "gpsimd", "vector", "scalar", "tensor")


class Sched:
    def __init__(self, nc):
        self.nc = nc
        self.ops = {e: [] for e in ENGINES}
        self.counts = {}
        self.events = {}
        self.sems = {}
        self.auto_counts = {e: 0 for e in ENGINES}

    def edge(self, engname, e, inst):
        self.auto_counts[engname] += 1
        n = self.auto_counts[engname]
        sem = self.sems["a_" + engname]
        inst.then_inc(sem, 1)
        e.wait_ge(sem, n)

    def op(self, eng, fn, waits=(), sets=None, sem=None, amt=1):
        wl = [w for w in waits if w is not None]
        inc = None
        if sets is not None:
            assert sets not in self.events, f"duplicate event {sets}"
            if sem is None:
                sem = "c_" + eng
                amt = 1
            v = self.counts.get(sem, 0) + amt
            self.counts[sem] = v
            self.events[sets] = (sem, v)
            inc = (sem, amt)
        self.ops[eng].append((wl, fn, inc))
        return sets

    def dma(self, eng, fn, waits=(), sets=None, sem=None):
        assert sem is not None
        return self.op(eng, fn, waits, sets, sem=sem, amt=16)

    def emit(self, stack):
        names = set(self.counts.keys())
        for e in ENGINES:
            names.add("a_" + e)
        for (sn, _v) in self.events.values():
            names.add(sn)
        for n in sorted(names):
            self.sems[n] = stack.enter_context(self.nc.semaphore(n))
        block = stack.enter_context(self.nc.Block())
        sched = self

        def mk(engname):
            own = "c_" + engname
            engops = sched.ops[engname]

            def body(eng):
                for wl, fn, inc in engops:
                    for w in wl:
                        s, v = sched.events[w]
                        eng.wait_ge(sched.sems[s], v)
                    inst = fn(eng)
                    if inc is not None:
                        if inst is None:
                            inst = eng.nop()
                        inst.then_inc(sched.sems[inc[0]], inc[1])
            return body

        block.sync(mk("sync"))
        block.gpsimd(mk("gpsimd"))
        block.vector(mk("vector"))
        block.scalar(mk("scalar"))
        block.tensor(mk("tensor"))


def build_nc(S):
    NT1 = S // 128
    T512 = S // 512
    CW = S + 2 * CPAD
    nc = bass.Bass("TRN2", target_bir_lowering=False)

    x_d = nc.dram_tensor("x", [S, D], F32, kind="ExternalInput")
    wq_d = nc.dram_tensor("wq", [D, D], F32, kind="ExternalInput")
    wk_d = nc.dram_tensor("wk", [D, D], F32, kind="ExternalInput")
    wv_d = nc.dram_tensor("wv", [D, D], F32, kind="ExternalInput")
    wp_d = nc.dram_tensor("wp", [D, D], F32, kind="ExternalInput")
    bp_d = nc.dram_tensor("bp", [D], F32, kind="ExternalInput")
    cw_d = nc.dram_tensor("cw", [D, 3], F32, kind="ExternalInput")
    cb_d = nc.dram_tensor("cb", [D], F32, kind="ExternalInput")
    ws_d = nc.dram_tensor("ws", [9 * 128, D], F32, kind="ExternalInput")
    out_d = nc.dram_tensor("out", [S, D], F32, kind="ExternalOutput")

    u_sp = nc.dram_tensor("u_sp", [IC * 128, S], F32)
    g_sp = nc.dram_tensor("g_sp", [IC * 128, S], F32)
    y_sp = nc.dram_tensor("y_sp", [S, D], F32)
    iup_sp = nc.dram_tensor("iup_sp", [IC * 128, S], I16)
    ig2_sp = nc.dram_tensor("ig2_sp", [IC * 128, S], I16)
    sq_sp = nc.dram_tensor("sq_sp", [IC * 128, S], I16)
    ks_d = nc.dram_tensor("ks_d", [D], F32)
    mx_d = nc.dram_tensor("mx_d", [128], F32)
    agin = nc.dram_tensor("agin", [9 * 128, D], BF16)
    agout = nc.dram_tensor("agout", [72 * 128, D], BF16)
    cc_in = nc.dram_tensor("cc_in", [1, 8], F32)
    cc_out = nc.dram_tensor("cc_out", [1, 8], F32)

    stack = ExitStack()
    E = stack.enter_context

    BW = max(NT1 * 1024, IC * 512)
    B1 = E(nc.sbuf_tensor("B1", [128, BW], BF16))
    B2 = E(nc.sbuf_tensor("B2", [128, BW], BF16))
    B3 = E(nc.sbuf_tensor("B3", [128, max(DC * CW, IC * 512)], BF16))
    B4 = E(nc.sbuf_tensor("B4", [128, 16384], BF16))
    WN = E(nc.sbuf_tensor("WN", [128, 2048], BF16))
    WQB = E(nc.sbuf_tensor("WQB", [128, 2048], BF16))
    F1 = E(nc.sbuf_tensor("F1", [128, 10240], F32))
    I16R = E(nc.sbuf_tensor("I16R", [128, 8192], I16))
    WSTG = I16R.bitcast(BF16)[:, 4096:6144]
    COLS = E(nc.sbuf_tensor("COLS", [128, COLW], F32))
    ROWB = E(nc.sbuf_tensor("ROWB", [1, 128], F32))
    CCR = E(nc.sbuf_tensor("CCR", [1, 8], F32))
    BPR = E(nc.sbuf_tensor("BPR", [1, 1024], F32))
    BPH = E(nc.sbuf_tensor("BPH", [1, 1024], BF16))
    BPL = E(nc.sbuf_tensor("BPL", [1, 1024], BF16))
    ONE = E(nc.sbuf_tensor("ONE", [1, 128], BF16))
    ONC = E(nc.sbuf_tensor("ONC", [128, 1], BF16))
    IDB = E(nc.sbuf_tensor("IDB", [128, 128], BF16))
    CWT = E(nc.sbuf_tensor("CWT", [128, 24], F32))
    CBT = E(nc.sbuf_tensor("CBT", [128, 8], F32))
    KSF = E(nc.sbuf_tensor("KSF", [128, 8], F32))

    KVM = E(nc.sbuf_tensor("KVM", [128, NP * 130], BF16))
    PS_A = E(nc.psum_tensor("PS_A", [128, 2048], F32))
    PS_B = E(nc.psum_tensor("PS_B", [128, 2048], F32))

    sc = Sched(nc)

    CL = {}
    _next = [0]

    def col(name, n=1):
        if name not in CL:
            CL[name] = (_next[0], n)
            _next[0] += n
        o, n2 = CL[name]
        assert n2 >= n
        return COLS[:, o:o + n]

    def F(a, b):
        return F1[:, a:b]

    for j in range(4):
        for nm in ("sr", "sq", "m", "e2", "mm", "v", "ln", "r0", "t1", "rstd", "nm"):
            col(f"{nm}_{j}")
    col("rd", 16)
    col("blk", 32)
    col("sblk", 32)
    col("rblk", 32)
    col("ymxs", NT1)
    col("umx", IC)
    col("gmx", IC)
    col("tmx", IC)
    col("omx", IC)
    for nm in ("ymxa", "umxa", "gmxa", "tmxa", "omxa", "g0_0", "gA_0", "gA_1",
               "gB_0", "gC_0", "sx", "rsx", "su", "fu", "sg", "fg", "sg16",
               "st2", "ft", "so", "fo", "pso", "tmpc", "cjunk"):
        col(nm)

    def _bv(ps, c):
        return ps[:, c:c + 128].bitcast(BF16)[:, 0:128]
    TRP8 = [_bv(PS_A, 0), _bv(PS_A, 512), _bv(PS_A, 1024), _bv(PS_A, 1536),
            _bv(PS_B, 0), _bv(PS_B, 512), _bv(PS_B, 1024), _bv(PS_B, 1536)]
    TRP6 = [_bv(PS_B, 1536), _bv(PS_A, 0), _bv(PS_A, 512),
            _bv(PS_B, 0), _bv(PS_B, 512), _bv(PS_B, 1024)]

    last_ccr = [None]

    def p0():
        sc.dma("sync", lambda e: e.dma_start(out=BPR[:, :], in_=bp_d[None, :]),
               sets="ld_bp", sem="d_cst0")
        cw_src = bass.AP(cw_d, 0, [[3, 128], [384, 8], [1, 3]])
        sc.dma("sync", lambda e: e.dma_start(out=CWT[:, :], in_=cw_src),
               sets="ld_cw", sem="d_cst1")
        cb_src = bass.AP(cb_d, 0, [[1, 128], [128, 8]])
        sc.dma("sync", lambda e: e.dma_start(out=CBT[:, :], in_=cb_src, allow_slow_non_contiguous=True),
               sets="ld_cb", sem="d_cst2")
        for t in range(NT1):
            sc.dma("gpsimd",
                   lambda e, t=t: e.dma_start(out=B1[:, t * 1024:(t + 1) * 1024],
                                              in_=x_d[t * 128:(t + 1) * 128, :]),
                   sets=f"xbtm_{t}", sem="d_gx")

        sc.op("vector", lambda e: e.memset(IDB[:, :], 1.0), sets="idb0")
        sc.op("gpsimd",
              lambda e: e.affine_select(IDB[:, :], IDB[:, :], [[1, 128]], ALU.is_equal,
                                        0.0, base=0, channel_multiplier=-1),
              waits=["idb0"], sets="idb")
        sc.op("gpsimd", lambda e: e.memset(ONE[:, :], 1.0), sets="one")
        sc.op("gpsimd", lambda e: e.memset(ONC[:, :], 1.0), sets="onc")
        sc.op("vector", lambda e: e.tensor_copy(BPH[:, :], BPR[:, :]),
              waits=["ld_bp"], sets="bph")
        sc.op("vector", lambda e: e.tensor_copy(F1[0:1, 0:1024], BPH[:, :]),
              waits=["bph"], sets="bps")
        sc.op("vector", lambda e: e.tensor_tensor(BPL[:, :], BPR[:, :], F1[0:1, 0:1024],
                                                  ALU.subtract),
              waits=["bps"], sets="bpl")

        def padz(e):
            last = None
            for c in range(DC):
                e.memset(B3[:, c * CW:c * CW + CPAD], 0.0)
                last = e.memset(B3[:, c * CW + CPAD + S:(c + 1) * CW], 0.0)
            return last
        sc.op("gpsimd", padz, sets="padz")

    def p0_weights(ms):
        AW = [wk_d, wv_d, wq_d, wp_d]
        for m in ms:
            wd = AW[m]
            slot = m % 2
            teng = "scalar" if m == 2 else "sync"
            tsem = "d_awts" if m == 2 else "d_awtp"
            for oc in range(DC):
                g = m * DC + oc
                wslot = g % 2
                wl = []
                if g >= 2:
                    wl.append(f"awt_{g-2}_{DC-1}")
                sc.dma("gpsimd",
                       lambda e, wd=wd, oc=oc, wslot=wslot:
                       e.dma_start(out=WN[:, wslot * 1024:(wslot + 1) * 1024],
                                   in_=wd[oc * 128:(oc + 1) * 128, :]),
                       waits=wl, sets=f"awc_{g}", sem=f"d_awc{wslot}")
                for dc in range(DC):
                    w2 = [f"awc_{g}"]
                    if oc == 0 and dc == 0 and m >= 2:
                        w2.append(f"aw_used_{m-2}")
                    sc.dma(teng,
                           lambda e, slot=slot, dc=dc, oc=oc, wslot=wslot:
                           e.dma_start_transpose(
                               B4[:, slot * 8192 + dc * 1024 + oc * 128:
                                  slot * 8192 + dc * 1024 + oc * 128 + 128],
                               WN[:, wslot * 1024 + dc * 128:
                                  wslot * 1024 + dc * 128 + 128]),
                           waits=w2, sets=f"awt_{g}_{dc}", sem=f"{tsem}{wslot}")
            sc.op(teng, lambda e: e.nop(),
                  waits=[f"awt_{m*DC+oc}_{DC-1}" for oc in range(6, 8)],
                  sets=f"awrdy_{m}")


    NU = 72
    NUC = NU // 8

    natv_w = I16R.bitcast(F32)

    def phase_w_chunk(lo, hi):
        natv = natv_w
        NSCR = natv[:, 3072:4096]
        for pos in range(lo, hi):
            sl = pos % 2
            wl = []
            if pos >= 2:
                wl.append(f"Wdq1_{pos-2}")
            sc.dma("scalar",
                   lambda e, pos=pos, sl=sl: e.dma_start(
                       out=natv[:, sl * 1024:(sl + 1) * 1024],
                       in_=ws_d[pos * 128:(pos + 1) * 128, :]),
                   waits=wl, sets=f"Wld_{pos}", sem=f"d_ws{sl}")

            def dq0(e, sl=sl):
                w3 = bass.AP(natv, sl * 1024, [[4096, 128], [32, 32], [1, 32]])
                sc.edge("vector", e, e.tensor_reduce(col("blk", 32), w3, AX.X, ALU.max,
                                                     apply_absolute_value=True))
                sc.edge("vector", e, e.tensor_scalar(col("sblk", 32), col("blk", 32),
                                                     1.0 / 7.0, 1e-12, ALU.mult, ALU.max))
                return e.reciprocal(col("rblk", 32), col("sblk", 32))
            sc.op("vector", dq0, waits=[f"Wld_{pos}"], sets=f"Wdq0_{pos}")

            def dq1(e, sl=sl):
                co, _ = CL["rblk"]
                rs3 = bass.AP(COLS.ap().tensor, co, [[COLW, 128], [1, 32], [0, 32]])
                return e.tensor_tensor(NSCR, natv[:, sl * 1024:(sl + 1) * 1024],
                                       rs3, ALU.mult)
            sc.op("vector", dq1, waits=[f"Wdq0_{pos}"], sets=f"Wdq1_{pos}")
            sc.op("vector", lambda e: e.tensor_scalar(NSCR, NSCR,
                                                      MAGIC, -MAGIC, ALU.add, ALU.add),
                  waits=[f"Wdq1_{pos}"], sets=f"Wdq2_{pos}")

            def dq3(e, sl=sl):
                co, _ = CL["sblk"]
                s3 = bass.AP(COLS.ap().tensor, co, [[COLW, 128], [1, 32], [0, 32]])
                return e.tensor_tensor(WQB[:, sl * 1024:(sl + 1) * 1024],
                                       NSCR, s3, ALU.mult)
            w3l = [f"Wdq2_{pos}"]
            if pos >= 2:
                w3l.append(f"Wtr_{pos-2}_{DC-1}")
            sc.op("vector", dq3, waits=w3l, sets=f"Wdq3_{pos}")
            for dc in range(DC):
                w4 = [f"Wdq3_{pos}"]
                if dc == 0 and pos >= 2:
                    w4.append(f"Wsp_{pos-2}")
                sc.dma("sync",
                       lambda e, sl=sl, dc=dc: e.dma_start_transpose(
                           WSTG[:, sl * 1024 + dc * 128:sl * 1024 + dc * 128 + 128],
                           WQB[:, sl * 1024 + dc * 128:sl * 1024 + dc * 128 + 128]),
                       waits=w4, sets=f"Wtr_{pos}_{dc}", sem=f"d_wtr{sl}")
            sc.dma("gpsimd",
                   lambda e, pos=pos, sl=sl: e.dma_start(
                       out=agin[pos * 128:(pos + 1) * 128, :],
                       in_=WSTG[:, sl * 1024:(sl + 1) * 1024]),
                   waits=[f"Wtr_{pos}_{DC-1}"], sets=f"Wsp_{pos}", sem=f"d_agin{sl}")
    def phase_w_tail():
        sc.op("gpsimd",
              lambda e: e.collective_compute("AllGather", ALU.bypass,
                                             replica_groups=[list(range(8))],
                                             ins=[agin.ap()], outs=[agout.ap()]),
              waits=[f"Wsp_{NUC-1}", f"Wsp_{NUC-2}"], sets="ag2", sem="ccs2",
              amt=1)

    def p1_xbfm():
        for t in range(NT1):
            for dc in range(DC):
                k = t * DC + dc
                r = k % 8

                def tr(e, t=t, dc=dc, r=r):
                    return e.transpose(TRP8[r], B1[:, t * 1024 + dc * 128:
                                                   t * 1024 + dc * 128 + 128], IDB[:, :])
                w = [f"xbtm_{NT1-1}", "idb"]
                if k >= 8:
                    w.append(f"xbfm_{k-8}")
                sc.op("tensor", tr, waits=w, sets=f"xbtr_{k}")

                def ev(e, t=t, dc=dc, r=r):
                    return e.tensor_copy(
                        B3[:, dc * CW + CPAD + t * 128:dc * CW + CPAD + t * 128 + 128],
                        TRP8[r])
                sc.op("vector", ev, waits=[f"xbtr_{k}", "padz"], sets=f"xbfm_{k}")
        sc.events["xbfm_done"] = sc.events[f"xbfm_{NT1*DC-1}"]

    def proj_tm(dst, m, elu, tag):
        for nt in range(NT1):
            r = nt % 2
            w = ["xbfm_done", f"awrdy_{m}"]
            if nt >= 2:
                w.append(f"{tag}_ev_{nt-2}")

            def mms(e, nt=nt, r=r, m=m):
                slot = m % 2
                inst = None
                for dc in range(DC):
                    lhsT = B3[:, dc * CW + CPAD + nt * 128:dc * CW + CPAD + nt * 128 + 128]
                    for h in range(2):
                        inst = e.matmul(
                            PS_B[:, r * 1024 + h * 512:r * 1024 + h * 512 + 512],
                            lhsT,
                            B4[:, slot * 8192 + dc * 1024 + h * 512:
                               slot * 8192 + dc * 1024 + h * 512 + 512],
                            start=(dc == 0), stop=(dc == DC - 1))
                return inst
            sc.op("tensor", mms, waits=w, sets=f"{tag}_mm_{nt}")
            if nt == NT1 - 1:
                sc.events[f"aw_used_{m}"] = sc.events[f"{tag}_mm_{nt}"]
            ps = PS_B[:, r * 1024:(r + 1) * 1024]
            if elu:
                w0 = [f"{tag}_mm_{nt}"]
                if nt >= 1:
                    w0.append(f"{tag}_ev_{nt-1}")
                sc.op("scalar", lambda e, ps=ps: e.activation(F(4096, 5120), ps,
                                                              AF.Relu, scale=-1.0),
                      waits=w0, sets=f"{tag}_s0_{nt}")
                sc.op("scalar", lambda e: e.activation(F(5120, 6144), F(4096, 5120),
                                                       AF.Exp, scale=-1.0),
                      waits=[f"{tag}_s0_{nt}"], sets=f"{tag}_s1_{nt}")
                sc.op("vector", lambda e, ps=ps: e.tensor_scalar(F(6144, 7168), ps, 0.0,
                                                                 None, ALU.max),
                      waits=[f"{tag}_s1_{nt}"], sets=f"{tag}_s2_{nt}")

                def fuse(e, nt=nt):
                    return e.tensor_tensor(dst[:, nt * 1024:(nt + 1) * 1024],
                                           F(5120, 6144), F(6144, 7168), ALU.add)
                sc.op("vector", fuse, waits=[f"{tag}_s2_{nt}"], sets=f"{tag}_ev_{nt}")

                def mk_ksmm(nt):
                    def ksmm(e, nt=nt):
                        inst = None
                        for h in range(2):
                            inst = e.matmul(PS_A[0:1, h * 512:(h + 1) * 512],
                                            ONC[:, :],
                                            dst[:, nt * 1024 + h * 512:
                                                nt * 1024 + h * 512 + 512],
                                            start=(nt == 0), stop=(nt == NT1 - 1))
                        return inst
                    return ksmm
                if nt >= 1:
                    sc.op("tensor", mk_ksmm(nt - 1), waits=[f"{tag}_ev_{nt-1}", "onc"],
                          sets=f"ks_mm_{nt-1}")
                if nt == NT1 - 1:
                    sc.op("tensor", mk_ksmm(nt), waits=[f"{tag}_ev_{nt}", "onc"],
                          sets=f"ks_mm_{nt}")
            else:
                def ev(e, nt=nt, ps=ps):
                    return e.copy(dst[:, nt * 1024:(nt + 1) * 1024], ps)
                sc.op("scalar", ev, waits=[f"{tag}_mm_{nt}"], sets=f"{tag}_ev_{nt}")
        sc.events[f"{tag}_done"] = sc.events[f"{tag}_ev_{NT1-1}"]

    def p1_kstate_finish():
        sc.op("vector", lambda e: e.tensor_copy(BPR[0:1, :], PS_A[0:1, 0:1024]),
              waits=[f"ks_mm_{NT1-1}", "bpl"], sets="ks_row")
        sc.dma("gpsimd", lambda e: e.dma_start(out=ks_d[None, :], in_=BPR[0:1, :]),
               waits=["ks_row"], sets="ks_st", sem="d_ks")
        ks_src = bass.AP(ks_d, 0, [[1, 128], [128, 8]])
        sc.dma("gpsimd", lambda e: e.dma_start(out=KSF[:, :], in_=ks_src, allow_slow_non_contiguous=True),
               waits=["ks_st"], sets="ks_ld", sem="d_ks")

    def p1_kv():
        sc.op("gpsimd", lambda e: e.memset(KVM[:, :], 0.0), sets="kvm0")
        for p in range(NP):
            def mm(e, p=p):
                inst = None
                for nt in range(NT1):
                    inst = e.matmul(PS_A[:, 1024 + p * 128:1152 + p * 128],
                                    B1[:, nt * 1024 + p * 128:nt * 1024 + p * 128 + 128],
                                    B2[:, nt * 1024 + p * 128:nt * 1024 + p * 128 + 128],
                                    start=(nt == 0), stop=(nt == NT1 - 1))
                return inst
            sc.op("tensor", mm, waits=["kb_done", "v_done"], sets=f"kv_mm_{p}")
        for p in range(NP):
            def ev(e, p=p):
                e.tensor_copy(KVM[0:64, p * 130:p * 130 + 64],
                              PS_A[0:64, 1024 + p * 128:1024 + p * 128 + 64])
                e.tensor_copy(KVM[64:128, p * 130 + 64:p * 130 + 128],
                              PS_A[64:128, 1024 + p * 128 + 64:1024 + p * 128 + 128])
                e.tensor_copy(KVM[0:64, p * 130 + 128:p * 130 + 129], KSF[0:64, p:p + 1])
                return e.tensor_copy(KVM[64:128, p * 130 + 129:p * 130 + 130],
                                     KSF[64:128, p:p + 1])
            sc.op("vector", ev, waits=[f"kv_mm_{NP-1}", "ks_ld", "kvm0"], sets=f"kv_ev_{p}")
        sc.events["kv_done"] = sc.events[f"kv_ev_{NP-1}"]

    def p1_conv():
        for c in range(DC):
            base = c * CW + CPAD
            w = ["kv_done", "ld_cw", "ld_cb", "xbfm_done"]
            if c >= 1:
                w = [f"cv_{c-1}", "ld_cw", "ld_cb"]
            sc.op("vector",
                  lambda e, c=c, base=base: e.tensor_scalar(
                      F(0, 0 + S), B3[:, base:base + S],
                      CWT[:, 3 * c + 1:3 * c + 2], CBT[:, c:c + 1],
                      ALU.mult, ALU.add),
                  waits=w, sets=f"cv1_{c}")
            sc.op("vector",
                  lambda e, c=c, base=base: e.scalar_tensor_tensor(
                      F(2048, 2048 + S), B3[:, base - 1:base + S - 1],
                      CWT[:, 3 * c:3 * c + 1], F(0, 0 + S), ALU.mult, ALU.add),
                  waits=[f"cv1_{c}"], sets=f"cv2_{c}")
            sc.op("vector",
                  lambda e, c=c, base=base: e.scalar_tensor_tensor(
                      B1[:, c * S:(c + 1) * S], B3[:, base + 1:base + S + 1],
                      CWT[:, 3 * c + 2:3 * c + 3], F(2048, 2048 + S), ALU.mult, ALU.add),
                  waits=[f"cv2_{c}"], sets=f"cv_{c}")
        sc.events["conv_done"] = sc.events[f"cv_{DC-1}"]

    def p1_qattn(mid=None):
        for T in range(T512):
            for oc in range(DC):
                r = oc % 2
                w = ["xbfm_done", "awrdy_2"]
                if T == 0 and oc == 0:
                    w.append("ks_row")
                if oc >= 2:
                    w.append(f"q_ev_{T}_{oc-2}")
                if T > 0 and oc == 0:
                    w.append(f"a2_{T-1}_done")

                def mms(e, T=T, oc=oc, r=r):
                    inst = None
                    for dc in range(DC):
                        inst = e.matmul(
                            PS_A[:, r * 512:(r + 1) * 512],
                            B4[:, dc * 1024 + oc * 128:dc * 1024 + oc * 128 + 128],
                            B3[:, dc * CW + CPAD + T * 512:dc * CW + CPAD + (T + 1) * 512],
                            start=(dc == 0), stop=(dc == DC - 1))
                    return inst
                sc.op("tensor", mms, waits=w, sets=f"q_mm_{T}_{oc}")
                ps = PS_A[:, r * 512:(r + 1) * 512]
                w0 = [f"q_mm_{T}_{oc}"]
                if oc >= 1:
                    w0.append(f"q_ev_{T}_{oc-1}")
                elif T > 0:
                    w0.append(f"q_ev_{T-1}_{DC-1}")
                sc.op("scalar", lambda e, ps=ps: e.activation(F(4096, 4608), ps,
                                                              AF.Relu, scale=-1.0),
                      waits=w0, sets=f"q_s0_{T}_{oc}")
                sc.op("scalar", lambda e: e.activation(F(5120, 5632), F(4096, 4608),
                                                       AF.Exp, scale=-1.0),
                      waits=[f"q_s0_{T}_{oc}"], sets=f"q_s1_{T}_{oc}")
                sc.op("scalar", lambda e, ps=ps: e.activation(F(6144, 6656), ps,
                                                              AF.Relu),
                      waits=[f"q_s1_{T}_{oc}"], sets=f"q_s2_{T}_{oc}")
                wf = [f"q_s2_{T}_{oc}"]
                if T == 0 and oc == 0:
                    wf.append("kv_done")
                if T > 0 and oc == 0:
                    wf.append(f"amm_{T-1}_3")

                def qev(e, oc=oc):
                    return e.tensor_tensor(B2[:, oc * 512:(oc + 1) * 512],
                                           F(5120, 5632), F(6144, 6656), ALU.add)
                sc.op("vector", qev, waits=wf, sets=f"q_ev_{T}_{oc}")
            for sub in range(4):
                w = [f"q_ev_{T}_{DC-1}", "kv_done"]
                if sub > 0:
                    w.append(f"attev_{T}_{sub-1}")
                elif T > 0:
                    w.append(f"r1s_{T*4-1}")

                def amm(e, T=T, sub=sub):
                    inst = None
                    for p in range(NP):
                        inst = e.matmul(
                            PS_B[:, p * PACK:p * PACK + 130],
                            B2[:, p * 512 + sub * 128:p * 512 + sub * 128 + 128],
                            KVM[:, p * 130:(p + 1) * 130],
                            start=True, stop=True)
                    return inst
                sc.op("tensor", amm, waits=w, sets=f"amm_{T}_{sub}")

                def rdg(e):
                    src = bass.AP(PS_B.ap().tensor, 128, [[2048, 128], [PACK, 8], [1, 2]])
                    sc.edge("vector", e, e.tensor_scalar(col("rd", 16), src, 1e-6, None,
                                                         ALU.max))
                    return e.reciprocal(col("rd", 16), col("rd", 16))
                sc.op("vector", rdg, waits=[f"amm_{T}_{sub}"], sets=f"rdr_{T}_{sub}")
                wv = [f"rdr_{T}_{sub}"]
                if T > 0 and sub == 0:
                    wv.append(f"a2_{T-1}_done")

                def adiv(e, sub=sub):
                    src = bass.AP(PS_B.ap().tensor, 0,
                                  [[2048, 128], [PACK, 8], [64, 2], [1, 64]])
                    co, _ = CL["rd"]
                    rdx = bass.AP(COLS.ap().tensor, co,
                                  [[COLW, 128], [2, 8], [1, 2], [0, 64]])
                    return e.tensor_tensor(B2[:, 4096 + sub * 1024:4096 + (sub + 1) * 1024],
                                           src, rdx, ALU.mult)
                sc.op("vector", adiv, waits=wv, sets=f"attev_{T}_{sub}")
            for k in range(DC * 4):
                ec, sub = divmod(k, 4)
                r = k % 6

                def tr(e, ec=ec, sub=sub, r=r):
                    return e.transpose(TRP6[r],
                                       B2[:, 4096 + sub * 1024 + ec * 128:
                                          4096 + sub * 1024 + ec * 128 + 128],
                                       IDB[:, :])
                w = [f"attev_{T}_{sub}", f"attev_{T}_3"]
                if k >= 6:
                    w.append(f"a2_{T}_{k-6}")
                sc.op("tensor", tr, waits=w, sets=f"a2tr_{T}_{k}")
                wf = [f"a2tr_{T}_{k}", "conv_done", "ld_cb"]
                if T > 0 and k == 0:
                    wf.append(f"pmm_{T-1}_3")

                def fuse(e, ec=ec, sub=sub, r=r, T=T):
                    return e.tensor_tensor(
                        B2[:, 8192 + ec * 512 + sub * 128:
                           8192 + ec * 512 + sub * 128 + 128],
                        B1[:, ec * S + T * 512 + sub * 128:
                           ec * S + T * 512 + sub * 128 + 128],
                        TRP6[r], ALU.add)
                sc.op("vector", fuse, waits=wf, sets=f"a2_{T}_{k}")
            sc.events[f"a2_{T}_done"] = sc.events[f"a2_{T}_{DC*4-1}"]
            for sub in range(4):
                t128 = T * 4 + sub
                xsl = t128 % 2
                wld = []
                if t128 >= 2:
                    wld.append(f"r1s_{t128-2}")
                sc.dma("sync",
                       lambda e, t128=t128, xsl=xsl: e.dma_start(
                           out=F(8192 + xsl * 1024, 9216 + xsl * 1024),
                           in_=x_d[t128 * 128:(t128 + 1) * 128, :]),
                       waits=wld, sets=f"xld_{t128}", sem=f"d_x{xsl}")
                r = sub % 2
                ps = PS_A[:, 1024:2048] if r == 0 else PS_B[:, 0:1024]
                w = [f"a2_{T}_done", "awrdy_3", "bpl", "one"]
                if sub >= 2:
                    w.append(f"r1s_{t128-2}")
                if T > 0 and sub < 2:
                    w.append(f"r1s_{(T-1)*4 + sub + 2}")
                if T == 0 and sub == 0:
                    w.append(f"kv_ev_{NP-1}")

                def pmm(e, sub=sub, ps=ps):
                    for ec in range(DC):
                        e.matmul(ps[:, 0:512],
                                 B2[:, 8192 + ec * 512 + sub * 128:
                                    8192 + ec * 512 + sub * 128 + 128],
                                 B4[:, 8192 + ec * 1024:8192 + ec * 1024 + 512],
                                 start=(ec == 0), stop=False)
                        e.matmul(ps[:, 512:1024],
                                 B2[:, 8192 + ec * 512 + sub * 128:
                                    8192 + ec * 512 + sub * 128 + 128],
                                 B4[:, 8192 + ec * 1024 + 512:8192 + ec * 1024 + 1024],
                                 start=(ec == 0), stop=False)
                    e.matmul(ps[:, 0:512], ONE[:, :], BPH[0:1, 0:512],
                             start=False, stop=False)
                    e.matmul(ps[:, 512:1024], ONE[:, :], BPH[0:1, 512:1024],
                             start=False, stop=False)
                    e.matmul(ps[:, 0:512], ONE[:, :], BPL[0:1, 0:512],
                             start=False, stop=True)
                    return e.matmul(ps[:, 512:1024], ONE[:, :], BPL[0:1, 512:1024],
                                    start=False, stop=True)
                sc.op("tensor", pmm, waits=w, sets=f"pmm_{T}_{sub}")
                if sub == 3:
                    sc.events[f"pmm_{T}_3"] = sc.events[f"pmm_{T}_{sub}"]

                def r1f(e, ps=ps, t128=t128):
                    xsl = t128 % 2
                    return e.scalar_tensor_tensor(F(7168, 8192), ps, 1.0,
                                                  F(8192 + xsl * 1024,
                                                    9216 + xsl * 1024),
                                                  ALU.mult, ALU.add,
                                                  accum_out=col(f"sr_{t128 % 4}"))
                sc.op("vector", r1f, waits=[f"pmm_{T}_{sub}", f"xld_{t128}"],
                      sets=f"r1s_{t128}")
                sc.op("scalar",
                      lambda e, t128=t128: e.activation(F(4096, 5120), F(7168, 8192),
                                                        AF.Square,
                                                        accum_out=col(f"sq_{t128 % 4}")),
                      waits=[f"r1s_{t128}"], sets=f"r1q_{t128}")

                def stats(e, t128=t128):
                    j = t128 % 4
                    sc.edge("vector", e, e.tensor_scalar(col(f"m_{j}"), col(f"sr_{j}"), 1.0 / D, None, ALU.mult))
                    sc.edge("vector", e, e.tensor_scalar(col(f"e2_{j}"), col(f"sq_{j}"), 1.0 / D, None, ALU.mult))
                    sc.edge("vector", e, e.tensor_tensor(col(f"mm_{j}"), col(f"m_{j}"), col(f"m_{j}"), ALU.mult))
                    sc.edge("vector", e, e.tensor_tensor(col(f"v_{j}"), col(f"e2_{j}"), col(f"mm_{j}"), ALU.subtract))
                    return e.tensor_scalar(col(f"v_{j}"), col(f"v_{j}"), EPS, None, ALU.add)
                sc.op("vector", stats, waits=[f"r1q_{t128}"], sets=f"st_{t128}")

                def rstd1(e, t128=t128):
                    j = t128 % 4
                    sc.edge("scalar", e, e.activation(col(f"ln_{j}"), col(f"v_{j}"), AF.Ln))
                    return e.activation(col(f"r0_{j}"), col(f"ln_{j}"), AF.Exp, scale=-0.5)
                sc.op("scalar", rstd1, waits=[f"st_{t128}"], sets=f"rs0_{t128}")

                def rstd2(e, t128=t128):
                    j = t128 % 4
                    sc.edge("vector", e, e.tensor_tensor(col(f"t1_{j}"), col(f"r0_{j}"), col(f"r0_{j}"), ALU.mult))
                    sc.edge("vector", e, e.tensor_tensor(col(f"t1_{j}"), col(f"t1_{j}"), col(f"v_{j}"), ALU.mult))
                    sc.edge("vector", e, e.tensor_scalar(col(f"t1_{j}"), col(f"t1_{j}"), -0.5, 1.5, ALU.mult, ALU.add))
                    sc.edge("vector", e, e.tensor_tensor(col(f"rstd_{j}"), col(f"r0_{j}"), col(f"t1_{j}"), ALU.mult))
                    sc.edge("vector", e, e.tensor_tensor(col(f"nm_{j}"), col(f"m_{j}"), col(f"rstd_{j}"), ALU.mult))
                    return e.tensor_scalar(col(f"nm_{j}"), col(f"nm_{j}"), -1.0, None,
                                           ALU.mult)
                sc.op("vector", rstd2, waits=[f"rs0_{t128}"], sets=f"rs_{t128}")

                def yslot(t):
                    return F((t % 2) * 1024, (t % 2) * 1024 + 1024)

                def yap(e, t128=t128):
                    j = t128 % 4
                    return e.activation(yslot(t128), F(7168, 8192), AF.Identity,
                                        bias=col(f"nm_{j}"), scale=col(f"rstd_{j}"))
                w2 = [f"rs_{t128}"]
                if t128 >= 2:
                    w2.append(f"ymx_{t128-2}")
                else:
                    w2.append("conv_done")
                sc.op("scalar", yap, waits=w2, sets=f"y_{t128}")
                sc.dma("gpsimd",
                       lambda e, t128=t128: e.dma_start(
                           out=y_sp[t128 * 128:(t128 + 1) * 128, :], in_=yslot(t128)),
                       waits=[f"y_{t128}"], sets=f"yst_{t128}", sem=f"d_y{t128 % 2}")
                sc.op("vector",
                      lambda e, t128=t128: e.tensor_reduce(
                          COLS[:, CL["ymxs"][0] + t128:CL["ymxs"][0] + t128 + 1],
                          yslot(t128), AX.X, ALU.max, apply_absolute_value=True),
                      waits=[f"y_{t128}", f"yst_{t128}"], sets=f"ymx_{t128}")
            if mid is not None:
                mid(T)

    def allreduce_cols(tag, srcs, waits, ncols):
        wz = list(waits)
        if last_ccr[0] is not None:
            wz.append(last_ccr[0])
        sc.op("gpsimd", lambda e: e.memset(CCR[:, :], 0.0), waits=wz, sets=f"ccz_{tag}")
        for i, cn in enumerate(srcs):
            wl = [f"ccz_{tag}"]
            if i > 0:
                wl.append(f"ccr_{tag}_{i-1}")
            sc.dma("gpsimd",
                   lambda e, cn=cn: e.dma_start(out=mx_d[:],
                                                in_=COLS[:, CL[cn][0]:CL[cn][0] + 1]),
                   waits=wl, sets=f"cca_{tag}_{i}", sem="d_cc")
            mx_src = bass.AP(mx_d, 0, [[0, 1], [1, 128]])
            sc.dma("gpsimd", lambda e: e.dma_start(out=ROWB[:, :], in_=mx_src),
                   waits=[f"cca_{tag}_{i}"], sets=f"ccb_{tag}_{i}", sem="d_cc")
            sc.op("vector",
                  lambda e, i=i: e.tensor_reduce(CCR[0:1, i:i + 1], ROWB[:, :], AX.X,
                                                 ALU.max),
                  waits=[f"ccb_{tag}_{i}"], sets=f"ccr_{tag}_{i}")
        last_ccr[0] = f"ccr_{tag}_{len(srcs)-1}"
        sc.dma("gpsimd", lambda e: e.dma_start(out=cc_in[:, :], in_=CCR[:, :]),
               waits=[f"ccr_{tag}_{len(srcs)-1}"], sets=f"cci_{tag}", sem="d_cc")
        sc.op("gpsimd",
              lambda e: e.collective_compute("AllReduce", ALU.max,
                                             replica_groups=[list(range(8))],
                                             ins=[cc_in.ap()], outs=[cc_out.ap()]),
              waits=[f"cci_{tag}"], sets=f"cc_{tag}", sem="ccs", amt=1)
        for i in range(ncols):
            src = bass.AP(cc_out, i, [[0, 128], [1, 1]])
            sc.dma("gpsimd",
                   lambda e, i=i, src=src, tag=tag: e.dma_start(
                       out=COLS[:, CL[f"g{tag}_{i}"][0]:CL[f"g{tag}_{i}"][0] + 1],
                       in_=src),
                   waits=[f"cc_{tag}"], sets=f"gbc_{tag}_{i}", sem="d_cc")

    def p2():
        sc.op("vector",
              lambda e: e.tensor_reduce(col("ymxa"),
                                        COLS[:, CL["ymxs"][0]:CL["ymxs"][0] + NT1],
                                        AX.X, ALU.max),
              waits=[f"ymx_{NT1-1}"], sets="ymxa")
        allreduce_cols("0", ["ymxa"], ["ymxa"], 1)

        def sxf(e):
            sc.edge("vector", e, e.tensor_scalar(col("sx"), col("g0_0"), 1.0 / 32767.0, 1e-12, ALU.mult, ALU.max))
            return e.reciprocal(col("rsx"), col("sx"))
        sc.op("vector", sxf, waits=["gbc_0_0"], sets="sx")
        for t in range(NT1):
            sl = t % 2
            wl = []
            if t >= 2:
                wl.append(f"ixr_{t-2}")
            else:
                wl.append("ymxa")
                wl.append(f"yst_{NT1-1}")
                wl.append(f"yst_{NT1-2}")
            sc.dma("sync",
                   lambda e, t=t, sl=sl: e.dma_start(
                       out=F(sl * 1024, (sl + 1) * 1024),
                       in_=y_sp[t * 128:(t + 1) * 128, :]),
                   waits=wl, sets=f"yld_{t}", sem=f"d_yl{sl}")

            def rnd(e, sl=sl):
                sc.edge("vector", e, e.tensor_scalar(F(2048, 3072), F(sl * 1024, (sl + 1) * 1024),
                                col("rsx"), MAGIC, ALU.mult, ALU.add))
                return e.tensor_scalar(F(3072, 4096), F(2048, 3072), -MAGIC, None, ALU.add)
            wr = [f"yld_{t}", "sx"]
            if t >= 1:
                wr.append(f"ixb_{t-1}")
            sc.op("vector", rnd, waits=wr, sets=f"ixr_{t}")
            wb2 = [f"ixr_{t}"]
            if t >= 1:
                wb2.append(f"ixtr_{t-1}_{DC-1}")
            else:
                wb2.append(f"Wtr_{NUC-1}_{DC-1}")
            sc.op("vector", lambda e: e.tensor_copy(WQB[:, 0:1024], F(3072, 4096)),
                  waits=wb2, sets=f"ixb_{t}")
            for dc in range(DC):
                k = t * DC + dc
                r = k % 8

                def tr(e, dc=dc, r=r):
                    return e.transpose(TRP8[r], WQB[:, dc * 128:(dc + 1) * 128], IDB[:, :])
                w = [f"ixb_{t}"]
                if k >= 8:
                    w.append(f"ixev_{k-8}")
                sc.op("tensor", tr, waits=w, sets=f"ixtr_{t}_{dc}")

                def ev(e, t=t, dc=dc, r=r):
                    return e.tensor_copy(B2[:, dc * S + t * 128:dc * S + t * 128 + 128],
                                         TRP8[r])
                sc.op("vector", ev, waits=[f"ixtr_{t}_{dc}"], sets=f"ixev_{k}")
        sc.events["ix_done"] = sc.events[f"ixev_{NT1*DC-1}"]

    def p3_one(mat, sp, tag, mcols):
        for ic in range(IC):
            u = mat * IC + ic
            row = ((u % 8) * NUC + u // 8) * 128
            rsl = ic % 4
            wl = ["ag2"]
            if ic >= 4:
                wl.append(f"{tag}mm_{ic-4}")
            if mat == 0 and ic < 4:
                wl.append(f"q_mm_{T512-1}_{DC-1}")
            if mat == 1 and ic < 4:
                wl.append(f"u_mm_{IC-4+ic}")
            sc.dma("sync",
                   lambda e, row=row, rsl=rsl: e.dma_start(
                       out=B4[:, rsl * 1024:(rsl + 1) * 1024],
                       in_=agout[row:row + 128, :]),
                   waits=wl, sets=f"{tag}wt_{ic}", sem=f"d_wn{rsl}")
            pr = ic % 2
            PS = PS_A if pr == 0 else PS_B
            w5 = [f"{tag}wt_{ic}", "ix_done"]
            if ic >= 2:
                w5.append(f"{tag}ev_{ic-2}")
                w5.append(f"{tag}mx_{ic-2}")
            if mat == 0 and ic < 2:
                w5.append(f"r1s_{S//128-1}")
            if mat == 1 and ic < 2:
                w5.append(f"u_ev_{IC-2+ic}")
                w5.append(f"u_mx_{IC-2+ic}")

            def mms(e, rsl=rsl, PS=PS):
                inst = None
                for dc in range(DC):
                    for tt in range(S // 512):
                        inst = e.matmul(
                            PS[:, tt * 512:(tt + 1) * 512],
                            B4[:, rsl * 1024 + dc * 128:rsl * 1024 + dc * 128 + 128],
                            B2[:, dc * S + tt * 512:dc * S + (tt + 1) * 512],
                            start=(dc == 0), stop=(dc == DC - 1))
                return inst
            sc.op("tensor", mms, waits=w5, sets=f"{tag}mm_{ic}")
            esl = ic % 2
            w6 = [f"{tag}mm_{ic}"]
            if ic >= 2:
                w6.append(f"{tag}sp_{ic-2}")
            if mat == 1 and ic < 2:
                w6.append(f"u_sp_{IC-2+ic}")
            sc.op("scalar",
                  lambda e, esl=esl, PS=PS: e.copy(
                      F(6144 + esl * 2048, 6144 + esl * 2048 + S), PS[:, 0:S]),
                  waits=w6, sets=f"{tag}ev_{ic}")
            sc.op("vector",
                  lambda e, ic=ic, esl=esl: e.tensor_reduce(
                      COLS[:, CL[mcols][0] + ic:CL[mcols][0] + ic + 1],
                      F(6144 + esl * 2048, 6144 + esl * 2048 + S), AX.X, ALU.max,
                      apply_absolute_value=True),
                  waits=[f"{tag}ev_{ic}"], sets=f"{tag}mx_{ic}")
            sc.dma("gpsimd",
                   lambda e, ic=ic, esl=esl, sp=sp: e.dma_start(
                       out=sp[ic * 128:(ic + 1) * 128, :],
                       in_=F(6144 + esl * 2048, 6144 + esl * 2048 + S)),
                   waits=[f"{tag}ev_{ic}"], sets=f"{tag}sp_{ic}",
                   sem=f"d_{tag}sp{ic % 4}")

    def p3():
        p3_one(0, u_sp, "u_", "umx")
        p3_one(1, g_sp, "g_", "gmx")
        sc.op("vector",
              lambda e: e.tensor_reduce(col("umxa"),
                                        COLS[:, CL["umx"][0]:CL["umx"][0] + IC],
                                        AX.X, ALU.max),
              waits=[f"u_mx_{IC-1}"], sets="umxa")
        sc.op("vector",
              lambda e: e.tensor_reduce(col("gmxa"),
                                        COLS[:, CL["gmx"][0]:CL["gmx"][0] + IC],
                                        AX.X, ALU.max),
              waits=[f"g_mx_{IC-1}"], sets="gmxa")

    def barrier_a():
        allreduce_cols("A", ["umxa", "gmxa"], ["umxa", "gmxa"], 2)

        def alg(e):
            sc.edge("vector", e, e.tensor_tensor(col("tmpc"), col("gA_0"), col("sx"), ALU.mult))
            sc.edge("vector", e, e.tensor_scalar(col("su"), col("tmpc"), 1.0 / 32767.0, 1e-12, ALU.mult, ALU.max))
            sc.edge("vector", e, e.reciprocal(col("fu"), col("su")))
            sc.edge("vector", e, e.tensor_tensor(col("fu"), col("fu"), col("sx"), ALU.mult))
            sc.edge("vector", e, e.tensor_tensor(col("tmpc"), col("gA_1"), col("sx"), ALU.mult))
            sc.edge("vector", e, e.tensor_scalar(col("sg"), col("tmpc"), 1.0 / 32767.0, 1e-12, ALU.mult, ALU.max))
            sc.edge("vector", e, e.reciprocal(col("fg"), col("sg")))
            return e.tensor_tensor(col("fg"), col("fg"), col("sx"), ALU.mult)
        sc.op("vector", alg, waits=["gbc_A_1", "sx"], sets="algA")

    def p2a():
        B3i = B3.bitcast(I16)
        B1f = B1.bitcast(F32)
        B4i2 = B4.bitcast(I16)

        def UR(q):
            return F(q * S, (q + 1) * S)

        def GR(q):
            return B1f[:, q * S:(q + 1) * S]

        def GI(q):
            return B3i[:, q * S:(q + 1) * S]

        def SQ(st):
            return B4i2[:, st * S:(st + 1) * S]

        def IUS(q):
            return I16R[:, q * S:(q + 1) * S]

        def loads(ic):
            q = ic % 4
            wl = []
            wlg = []
            if ic >= 4:
                wl.append(f"iu_{ic-4}")
                wlg.append(f"t2sp_{ic-4}")
            else:
                if ic < 2:
                    wl.append(f"ixb_{NT1-1}")
                else:
                    wl.append("umxa")
                    wl.append("gmxa")
                for q_ in (f"u_sp_{IC-1}", f"u_sp_{IC-2}",
                           f"g_sp_{IC-1}", f"g_sp_{IC-2}"):
                    wl.append(q_)
                    wlg.append(q_)
            sc.dma("sync",
                   lambda e, ic=ic, q=q: e.dma_start(
                       out=UR(q), in_=u_sp[ic * 128:(ic + 1) * 128, :]),
                   waits=wl, sets=f"uld_{ic}", sem=f"d_ul{ic % 4}")
            sc.dma("sync",
                   lambda e, ic=ic, q=q: e.dma_start(
                       out=GR(q), in_=g_sp[ic * 128:(ic + 1) * 128, :]),
                   waits=wlg, sets=f"gld_{ic}", sem=f"d_gl{ic % 4}")
            wdl_ops(ic)

        def casts(ic):
            q = ic % 4
            w0 = [f"uld_{ic}", "algA"]
            if ic >= 4:
                w0.append(f"iusp_{ic-4}")
            sc.op("vector",
                  lambda e, ic=ic, q=q: e.tensor_scalar(IUS(q), UR(q), col("fu"),
                                                        None, ALU.mult),
                  waits=w0, sets=f"iu_{ic}")
            w1 = [f"gld_{ic}", "algA"]
            if ic >= 8:
                w1.append(f"t2o_{ic-8}")
            sc.op("vector",
                  lambda e, ic=ic, q=q: e.tensor_scalar(GI(ic % 8), GR(q), col("fg"),
                                                        None, ALU.mult),
                  waits=w1, sets=f"ig_{ic}")
            sc.op("scalar",
                  lambda e, ic=ic, q=q: e.activation(GR(q), GI(ic % 8), AF.Sigmoid,
                                                     scale=col("sg")),
                  waits=[f"ig_{ic}"], sets=f"sg_{ic}")

        for ic0 in range(4):
            loads(ic0)
        casts(0)
        casts(1)
        for ic in range(IC):
            q = ic % 4
            st = ic % 2
            w2 = [f"sg_{ic}"]
            if ic >= 2:
                w2.append(f"t2g_{ic-2}")
            sc.op("vector",
                  lambda e, ic=ic, q=q, st=st: e.tensor_scalar(SQ(st), GR(q), 65536.0,
                                                               -32768.0, ALU.mult,
                                                               ALU.add),
                  waits=w2, sets=f"sq_{ic}")
            sc.op("gpsimd",
                  lambda e, ic=ic, q=q, st=st: e.tensor_tensor(GR(q), GI(ic % 8),
                                                               SQ(st), ALU.mult),
                  waits=[f"sq_{ic}"], sets=f"t2g_{ic}")
            sc.dma("gpsimd",
                   lambda e, ic=ic, q=q: e.dma_start(
                       out=iup_sp[ic * 128:(ic + 1) * 128, :], in_=IUS(q)),
                   waits=[f"iu_{ic}"], sets=f"iusp_{ic}", sem=f"d_iu{ic % 4}")
            if ic + 4 < IC:
                loads(ic + 4)
            if ic + 2 < IC:
                casts(ic + 2)
            sc.op("vector",
                  lambda e, ic=ic, q=q: e.scalar_tensor_tensor(
                      GR(q), GI(ic % 8), 32768.0, GR(q), ALU.mult, ALU.add),
                  waits=[f"t2g_{ic}"], sets=f"t2o_{ic}")
            sc.op("vector",
                  lambda e, ic=ic, q=q: e.tensor_reduce(
                      COLS[:, CL["tmx"][0] + ic:CL["tmx"][0] + ic + 1],
                      GR(q), AX.X, ALU.max, apply_absolute_value=True),
                  waits=[f"t2o_{ic}"], sets=f"t2_{ic}")
            sc.dma("gpsimd",
                   lambda e, ic=ic, q=q: e.dma_start(
                       out=g_sp[ic * 128:(ic + 1) * 128, :], in_=GR(q)),
                   waits=[f"t2_{ic}"], sets=f"t2sp_{ic}", sem=f"d_t2{ic % 4}")
        sc.op("vector",
              lambda e: e.tensor_reduce(col("tmxa"),
                                        COLS[:, CL["tmx"][0]:CL["tmx"][0] + IC],
                                        AX.X, ALU.max),
              waits=[f"t2_{IC-1}"], sets="tmxa")

    def wd_slot(ic):
        if ic < 12:
            return B4[:, 4096 + ic * 1024:4096 + (ic + 1) * 1024]
        return B2[:, (ic - 12) * 1024:(ic - 11) * 1024]

    def p_wdeq():
        pass

    def wdl_ops(j):
        oc, s3 = divmod(j, 3)
        u = 48 + j
        row = ((u % 8) * NUC + u // 8) * 128
        for dc in range(DC):
            ic = s3 * 8 + dc
            dst = wd_slot(ic)
            wl = []
            if j == 0 and dc == 0:
                wl = ["algA"]
            if j == 1 and dc == 4:
                wl = [f"g_mm_{IC-1}"]
            sc.dma("scalar",
                   lambda e, row=row, dc=dc, dst=dst, oc=oc: e.dma_start(
                       out=dst[:, oc * 128:(oc + 1) * 128],
                       in_=agout[row:row + 128, dc * 128:(dc + 1) * 128]),
                   waits=wl, sets=f"wdl_{j}_{dc}", sem=f"d_wdl{j % 4}")

    def wdl_fin():
        sc.op("scalar", lambda e: e.copy(col("cjunk"), CWT[:, 0:1]),
              waits=[f"wdl_{j}_{DC-1}" for j in range(20, 24)] + ["ld_cw"],
              sets="wd_ready")

    def barrier_b():
        allreduce_cols("B", ["tmxa"], ["tmxa"], 1)

        def alg(e):
            sc.edge("vector", e, e.tensor_scalar(col("sg16"), col("sg"), 1.0 / 65536.0, None, ALU.mult))
            sc.edge("vector", e, e.tensor_tensor(col("tmpc"), col("gB_0"), col("sg16"), ALU.mult))
            sc.edge("vector", e, e.tensor_scalar(col("st2"), col("tmpc"), 1.0 / 32767.0, 1e-12, ALU.mult, ALU.max))
            sc.edge("vector", e, e.reciprocal(col("ft"), col("st2")))
            return e.tensor_tensor(col("ft"), col("ft"), col("sg16"), ALU.mult)
        sc.op("vector", alg, waits=["gbc_B_0", "algA"], sets="algB")

    def p2c():
        B3i = B3.bitcast(I16)
        B1f = B1.bitcast(F32)

        def FR(q):
            return F(q * S, (q + 1) * S)

        def IUP(q):
            return B3i[:, q * S:(q + 1) * S]

        def PRD(q):
            return B1f[:, q * S:(q + 1) * S]

        def IG2(q):
            return I16R[:, q * S:(q + 1) * S]

        def loads2(ic):
            q = ic % 4
            wl = [f"t2sp_{ic}"]
            if ic >= 4:
                wl.append(f"c_cast_{ic-4}")
            else:
                wl.append(f"iu_{IC-4+ic}")
            sc.dma("sync",
                   lambda e, ic=ic, q=q: e.dma_start(
                       out=FR(q), in_=g_sp[ic * 128:(ic + 1) * 128, :]),
                   waits=wl, sets=f"t2ld_{ic}", sem=f"d_cig{ic % 4}")
            wu_ = [f"iusp_{ic}"]
            if ic >= 4:
                wu_.append(f"otg_{ic-4}")
            else:
                wu_.append(f"t2o_{IC-8+ic}")
                wu_.append(f"t2o_{IC-4+ic}")
            sc.dma("sync",
                   lambda e, ic=ic, q=q: e.dma_start(out=IUP(q),
                                                     in_=iup_sp[ic * 128:(ic + 1) * 128, :]),
                   waits=wu_, sets=f"iuld_{ic}", sem=f"d_ciu{ic % 4}")

        def ccast(ic):
            q = ic % 4
            w0 = [f"t2ld_{ic}", "algB"]
            if ic >= 4:
                w0.append(f"ig2s_{ic-4}")
                w0.append(f"otg_{ic-4}")
            else:
                w0.append(f"iusp_{IC-4+ic}")
            sc.op("vector",
                  lambda e, ic=ic, q=q: e.tensor_scalar(IG2(q), FR(q), col("ft"),
                                                        None, ALU.mult),
                  waits=w0, sets=f"c_cast_{ic}")

        for ic0 in range(4):
            loads2(ic0)
        ccast(0)
        for ic in range(IC):
            q = ic % 4
            sc.dma("gpsimd",
                   lambda e, ic=ic, q=q: e.dma_start(
                       out=ig2_sp[ic * 128:(ic + 1) * 128, :], in_=IG2(q)),
                   waits=[f"c_cast_{ic}"], sets=f"ig2s_{ic}", sem=f"d_g2{ic % 4}")
            w1 = [f"c_cast_{ic}", f"iuld_{ic}"]
            if ic >= 4:
                w1.append(f"ot_{ic-4}")
            else:
                w1.append(f"t2sp_{IC-4+ic}")
            sc.op("gpsimd",
                  lambda e, ic=ic, q=q: e.tensor_tensor(PRD(q), IG2(q), IUP(q),
                                                        ALU.mult),
                  waits=w1, sets=f"otg_{ic}")
            if ic + 4 < IC:
                loads2(ic + 4)
            if ic + 1 < IC:
                ccast(ic + 1)
            sc.op("vector",
                  lambda e, ic=ic, q=q: e.tensor_reduce(
                      COLS[:, CL["omx"][0] + ic:CL["omx"][0] + ic + 1],
                      PRD(q), AX.X, ALU.max, apply_absolute_value=True),
                  waits=[f"otg_{ic}"], sets=f"ot_{ic}")
        sc.op("vector",
              lambda e: e.tensor_reduce(col("omxa"),
                                        COLS[:, CL["omx"][0]:CL["omx"][0] + IC],
                                        AX.X, ALU.max),
              waits=[f"ot_{IC-1}"], sets="omxa")

    def barrier_c():
        allreduce_cols("C", ["omxa"], ["omxa"], 1)

        def alg(e):
            sc.edge("vector", e, e.tensor_tensor(col("pso"), col("st2"), col("su"), ALU.mult))
            sc.edge("vector", e, e.tensor_tensor(col("tmpc"), col("gC_0"), col("pso"), ALU.mult))
            sc.edge("vector", e, e.tensor_scalar(col("so"), col("tmpc"), 1.0 / 32767.0, 1e-12, ALU.mult, ALU.max))
            sc.edge("vector", e, e.reciprocal(col("fo"), col("so")))
            return e.tensor_tensor(col("fo"), col("fo"), col("pso"), ALU.mult)
        sc.op("vector", alg, waits=["gbc_C_0", "algB"], sets="algC")

    def p2d():
        B4f = B4.bitcast(F32)

        def io_chain(sb):
            IOB = B3 if sb % 2 == 0 else B1
            for b in range(IC // 4):
                kb = sb * (IC // 4) + b
                bsl = b % 2

                def b_src(sp, sb=sb, b=b):
                    return bass.AP(sp, (b * 4) * 128 * S + sb * 512,
                                   [[S, 128], [128 * S, 4], [1, 512]])
                wl = []
                if kb >= 2:
                    wl.append(f"dot_{kb-2}")
                else:
                    for nm_ in (f"ig2s_{IC-4}", f"ig2s_{IC-3}", f"ig2s_{IC-2}",
                                f"ig2s_{IC-1}", f"otg_{IC-1}"):
                        wl.append(nm_)
                sc.dma("sync",
                       lambda e, sb=sb, b=b, bsl=bsl: e.dma_start(
                           out=I16R[:, bsl * 2048:bsl * 2048 + 2048],
                           in_=b_src(ig2_sp, sb, b)),
                       waits=wl, sets=f"g2ld_{kb}", sem=f"d_dg{kb % 4}")
                sc.dma("sync",
                       lambda e, sb=sb, b=b, bsl=bsl: e.dma_start(
                           out=I16R[:, 4096 + bsl * 2048:4096 + bsl * 2048 + 2048],
                           in_=b_src(iup_sp, sb, b)),
                       waits=wl, sets=f"duld_{kb}", sem=f"d_du{kb % 4}")
                w1 = [f"g2ld_{kb}", f"duld_{kb}"]
                if kb >= 2:
                    w1.append(f"iot_{kb-2}")
                else:
                    w1.append(f"c_cast_{IC-3+kb}")
                sc.op("gpsimd",
                      lambda e, bsl=bsl: e.tensor_tensor(
                          F(2048 + bsl * 2048, 4096 + bsl * 2048),
                          I16R[:, bsl * 2048:bsl * 2048 + 2048],
                          I16R[:, 4096 + bsl * 2048:4096 + bsl * 2048 + 2048],
                          ALU.mult),
                      waits=w1, sets=f"dot_{kb}")
                w2 = [f"dot_{kb}", "algC"]
                if kb >= 1:
                    w2.append(f"io_{kb-1}")
                sc.op("scalar",
                      lambda e, bsl=bsl: e.activation(
                          B4f[:, 0:2048], F(2048 + bsl * 2048, 4096 + bsl * 2048),
                          AF.Copy, bias=MAGIC, scale=col("fo")),
                      waits=w2, sets=f"iot_{kb}")
                w3 = [f"iot_{kb}"]
                if sb >= 2 and b == 0:
                    w3.append(f"dmm_{sb-2}_3")
                sc.op("vector",
                      lambda e, IOB=IOB, b=b: e.tensor_scalar(
                          IOB[:, b * 2048:(b + 1) * 2048], B4f[:, 0:2048], -MAGIC,
                          None, ALU.add),
                      waits=w3, sets=f"io_{kb}")

        def mm_ln(sb):
            IOB = B3 if sb % 2 == 0 else B1
            for sub in range(4):
                t128 = sb * 4 + sub
                r = sub % 2
                ps = PS_A[:, 0:1024] if r == 0 else PS_A[:, 1024:2048]
                w = [f"io_{sb*(IC//4)+IC//4-1}", "wd_ready"]
                if sub >= 2:
                    w.append(f"r2s_{t128-2}")
                if sb > 0 and sub < 2:
                    w.append(f"r2s_{(sb-1)*4 + sub + 2}")

                def dmm(e, IOB=IOB, sub=sub, ps=ps):
                    inst = None
                    for ic in range(IC):
                        wt = wd_slot(ic)
                        e.matmul(ps[:, 0:512],
                                 IOB[:, ic * 512 + sub * 128:ic * 512 + sub * 128 + 128],
                                 wt[:, 0:512], start=(ic == 0), stop=(ic == IC - 1))
                        inst = e.matmul(ps[:, 512:1024],
                                        IOB[:, ic * 512 + sub * 128:
                                            ic * 512 + sub * 128 + 128],
                                        wt[:, 512:1024], start=(ic == 0),
                                        stop=(ic == IC - 1))
                    return inst
                sc.op("tensor", dmm, waits=w, sets=f"dmm_{sb}_{sub}")
                if sub == 3:
                    sc.events[f"dmm_{sb}_3"] = sc.events[f"dmm_{sb}_{sub}"]
                ysl = t128 % 2
                wld = []
                if t128 >= 2:
                    wld.append(f"r2s_{t128-2}")
                else:
                    wld.append(f"c_cast_{IC-4}")
                sc.dma("sync",
                       lambda e, t128=t128, ysl=ysl: e.dma_start(
                           out=F(ysl * 1024, (ysl + 1) * 1024),
                           in_=y_sp[t128 * 128:(t128 + 1) * 128, :]),
                       waits=wld, sets=f"yld2_{t128}", sem=f"d_y2{ysl}")
                wr2 = [f"dmm_{sb}_{sub}", f"yld2_{t128}", "algC"]
                if t128 >= 1:
                    wr2.append(f"r2_{t128-1}")

                def r2f(e, ps=ps, ysl=ysl, t128=t128):
                    return e.scalar_tensor_tensor(F(6144, 7168), ps, col("so"),
                                                  F(ysl * 1024, (ysl + 1) * 1024),
                                                  ALU.mult, ALU.add,
                                                  accum_out=col(f"sr_{t128 % 4}"))
                sc.op("vector", r2f, waits=wr2, sets=f"r2s_{t128}")
                sc.op("scalar",
                      lambda e, t128=t128: e.activation(F(7168, 8192), F(6144, 7168),
                                                        AF.Square,
                                                        accum_out=col(f"sq_{t128 % 4}")),
                      waits=[f"r2s_{t128}"], sets=f"r2q_{t128}")

                def stats(e, t128=t128):
                    j = t128 % 4
                    sc.edge("vector", e, e.tensor_scalar(col(f"m_{j}"), col(f"sr_{j}"), 1.0 / D, None, ALU.mult))
                    sc.edge("vector", e, e.tensor_scalar(col(f"e2_{j}"), col(f"sq_{j}"), 1.0 / D, None, ALU.mult))
                    sc.edge("vector", e, e.tensor_tensor(col(f"mm_{j}"), col(f"m_{j}"), col(f"m_{j}"), ALU.mult))
                    sc.edge("vector", e, e.tensor_tensor(col(f"v_{j}"), col(f"e2_{j}"), col(f"mm_{j}"), ALU.subtract))
                    return e.tensor_scalar(col(f"v_{j}"), col(f"v_{j}"), EPS, None, ALU.add)
                sc.op("vector", stats, waits=[f"r2q_{t128}"], sets=f"st2_{t128}")

                def rstd1(e, t128=t128):
                    j = t128 % 4
                    sc.edge("scalar", e, e.activation(col(f"ln_{j}"), col(f"v_{j}"), AF.Ln))
                    return e.activation(col(f"r0_{j}"), col(f"ln_{j}"), AF.Exp, scale=-0.5)
                sc.op("scalar", rstd1, waits=[f"st2_{t128}"], sets=f"rs20_{t128}")

                def rstd2(e, t128=t128):
                    j = t128 % 4
                    sc.edge("vector", e, e.tensor_tensor(col(f"t1_{j}"), col(f"r0_{j}"), col(f"r0_{j}"), ALU.mult))
                    sc.edge("vector", e, e.tensor_tensor(col(f"t1_{j}"), col(f"t1_{j}"), col(f"v_{j}"), ALU.mult))
                    sc.edge("vector", e, e.tensor_scalar(col(f"t1_{j}"), col(f"t1_{j}"), -0.5, 1.5, ALU.mult, ALU.add))
                    sc.edge("vector", e, e.tensor_tensor(col(f"rstd_{j}"), col(f"r0_{j}"), col(f"t1_{j}"), ALU.mult))
                    sc.edge("vector", e, e.tensor_tensor(col(f"nm_{j}"), col(f"m_{j}"), col(f"rstd_{j}"), ALU.mult))
                    return e.tensor_scalar(col(f"nm_{j}"), col(f"nm_{j}"), -1.0, None,
                                           ALU.mult)
                sc.op("vector", rstd2, waits=[f"rs20_{t128}"], sets=f"rs2_{t128}")
                osl = t128 % 2

                def oap(e, t128=t128, osl=osl):
                    j = t128 % 4
                    return e.activation(F(8192 + osl * 1024, 9216 + osl * 1024),
                                        F(6144, 7168), AF.Identity,
                                        bias=col(f"nm_{j}"), scale=col(f"rstd_{j}"))
                w3 = [f"rs2_{t128}"]
                if t128 >= 2:
                    w3.append(f"ost_{t128-2}")
                sc.op("scalar", oap, waits=w3, sets=f"r2_{t128}")
                sc.dma("sync",
                       lambda e, t128=t128, osl=osl: e.dma_start(
                           out=out_d[t128 * 128:(t128 + 1) * 128, :],
                           in_=F(8192 + osl * 1024, 9216 + osl * 1024)),
                       waits=[f"r2_{t128}"], sets=f"ost_{t128}", sem=f"d_out{t128 % 2}")
        for sb in range(T512):
            io_chain(sb)
            if sb >= 1:
                mm_ln(sb - 1)
        mm_ln(T512 - 1)
        nst = S // 128

        def finw(e):
            e.wait_ge(sc.sems["d_out0"], 16 * ((nst + 1) // 2))
            return e.wait_ge(sc.sems["d_out1"], 16 * (nst // 2))
        sc.ops["sync"].append(([], finw, None))
        sc.counts.setdefault("d_out0", 0)
        sc.counts.setdefault("d_out1", 0)

    p0()
    p0_weights([0, 1])
    p1_xbfm()
    proj_tm(B1, 0, True, "kb")
    proj_tm(B2, 1, False, "v")
    p1_kstate_finish()
    p1_kv()
    p0_weights([2, 3])
    def _pw_mid(T):
        if T512 == 1:
            phase_w_chunk(0, NUC)
            phase_w_tail()
        elif T == 0:
            phase_w_chunk(0, 3)
        elif T == 1:
            phase_w_chunk(3, 6)
        elif T == 2:
            phase_w_chunk(6, NUC)
            phase_w_tail()

    p1_conv()
    p1_qattn(mid=_pw_mid)
    p2()
    p3()
    barrier_a()
    p2a()
    wdl_fin()
    barrier_b()
    p2c()
    barrier_c()
    p2d()

    assert _next[0] <= COLW, f"COLS overflow: {_next[0]}"
    sc.emit(stack)
    stack.close()
    return nc


_NC_CACHE = {}


def get_nc(S):
    if S not in _NC_CACHE:
        _NC_CACHE[S] = build_nc(S)
    return _NC_CACHE[S]


def kernel(L0_attn_x, L0_attn_Wq, L0_attn_Wk, L0_attn_Wv, L0_attn_Wp, L0_attn_bp,
           L0_attn_conv_w, L0_attn_conv_b, L0_ffn_w_gate, L0_ffn_w_up, L0_ffn_w_down):
    x = np.ascontiguousarray(np.asarray(L0_attn_x, dtype=np.float32))
    B, S, _ = x.shape
    nc = get_nc(S)
    w = {
        "wq": np.ascontiguousarray(L0_attn_Wq, dtype=np.float32),
        "wk": np.ascontiguousarray(L0_attn_Wk, dtype=np.float32),
        "wv": np.ascontiguousarray(L0_attn_Wv, dtype=np.float32),
        "wp": np.ascontiguousarray(L0_attn_Wp, dtype=np.float32),
        "bp": np.ascontiguousarray(L0_attn_bp, dtype=np.float32),
        "cw": np.ascontiguousarray(L0_attn_conv_w, dtype=np.float32),
        "cb": np.ascontiguousarray(L0_attn_conv_b, dtype=np.float32),
    }
    wu = np.asarray(L0_ffn_w_up, dtype=np.float32)
    wg = np.asarray(L0_ffn_w_gate, dtype=np.float32)
    wdn = np.asarray(L0_ffn_w_down, dtype=np.float32)

    def unit(u):
        if u < 24:
            return wu[u * 128:(u + 1) * 128, :]
        if u < 48:
            return wg[(u - 24) * 128:(u - 23) * 128, :]
        j = u - 48
        oc, s3 = divmod(j, 3)
        return wdn[oc * 128:(oc + 1) * 128, s3 * 1024:(s3 + 1) * 1024]

    in_maps = []
    for b in range(B):
        ws = np.ascontiguousarray(
            np.concatenate([unit(b + 8 * p) for p in range(9)], axis=0))
        in_maps.append(dict(w, x=np.ascontiguousarray(x[b]), ws=ws))
    res = run_bass_kernel_spmd(nc, in_maps, core_ids=list(range(B)))
    return np.stack([np.asarray(r["out"]) for r in res.results], axis=0)
